# version 11
# baseline (speedup 1.0000x reference)
import numpy as np
import contextlib
import concourse.bass as bass
import concourse.mybir as mybir
from concourse.bass_utils import run_bass_kernel_spmd

F32 = mybir.dt.float32
BF16 = mybir.dt.bfloat16
AF = mybir.ActivationFunctionType
OP = mybir.AluOpType
AX = mybir.AxisListType

D, L, NN, RD = 2048, 4096, 16, 64
NB = 8
KD = D // 128
T = 512
NM = L // T
NFC = 96
LN_EPS = 1e-5
Q_EPS = 1e-10

ENGS = ("sync", "gpsimd", "tensor", "vector", "scalar")
DBG = {"on": False, "res": None}
NM_RUN = NM
ENG_OF = dict(v="vector", a="scalar", t="tensor", g="gpsimd", s="sync")
DMA_SEMS = ("sw", "ws0", "ws1", "qw0", "qw1", "wm0", "wm1", "md",
            "xd", "wq0", "wq1", "od0", "od1", "dbgd")
STEP = dict(v=1, a=1, t=1, g=1, s=1, **{k: 16 for k in DMA_SEMS})
SEM_OF_CNT = dict(v="v", a="a", t="t", **{k: k for k in DMA_SEMS})


def build_nc():
    nc = bass.Bass("TRN2", target_bir_lowering=False)

    xT = nc.dram_tensor("xT", [D, L], F32, kind="ExternalInput").ap()
    wxT = nc.dram_tensor("wxT", [D, 96], F32, kind="ExternalInput").ap()
    wdtT = nc.dram_tensor("wdtT", [RD, D], F32, kind="ExternalInput").ap()
    bdt = nc.dram_tensor("bdt", [128, KD], F32, kind="ExternalInput").ap()
    alog = nc.dram_tensor("alog", [128, KD * NN], F32, kind="ExternalInput").ap()
    dvec = nc.dram_tensor("dvec", [128, KD], F32, kind="ExternalInput").ap()
    fbias = nc.dram_tensor("fbias", [128, KD], F32, kind="ExternalInput").ap()
    wT = nc.dram_tensor("wT", [D, D], F32, kind="ExternalInput").ap()
    cb_in = nc.dram_tensor("cb_in", [128, KD], F32, kind="ExternalInput").ap()
    wmod = nc.dram_tensor("wmod_r", [NFC * 128, KD * 128], F32,
                          kind="ExternalInput").ap()
    onehot_d = nc.dram_tensor("onehot_bc", [96, 32 * 128], BF16,
                              kind="ExternalInput").ap()
    ident_d = nc.dram_tensor("ident_bc", [128, 128], BF16,
                             kind="ExternalInput").ap()
    yT = nc.dram_tensor("yT", [D, L], F32, kind="ExternalOutput").ap()
    if DBG["on"]:
        dbg_x2 = nc.dram_tensor("dbg_x2", [128, KD * T], F32,
                                kind="ExternalOutput").ap()
        dbg_ym = nc.dram_tensor("dbg_ym", [128, KD * T], BF16,
                                kind="ExternalOutput").ap()
        dbg_dt = nc.dram_tensor("dbg_dt", [128, KD * T], BF16,
                                kind="ExternalOutput").ap()
        dbg_vb = nc.dram_tensor("dbg_vb", [128, KD * T], BF16,
                                kind="ExternalOutput").ap()
        dbg_h2 = nc.dram_tensor("dbg_h2", [128, KD * T], BF16,
                                kind="ExternalOutput").ap()
        dbg_xp = nc.dram_tensor("dbg_xp", [96, T], BF16,
                                kind="ExternalOutput").ap()
        dbg_mm = nc.dram_tensor("dbg_mm", [128, 96], F32,
                                kind="ExternalOutput").ap()
        dbg_g = nc.dram_tensor("dbg_g", [128, 4 * KD], F32,
                               kind="ExternalOutput").ap()
        dbg_cy = nc.dram_tensor("dbg_cy", [128, NN * KD], BF16,
                                kind="ExternalOutput").ap()
        dbg_as = nc.dram_tensor("dbg_as", [128, NN * KD], F32,
                                kind="ExternalOutput").ap()
    wq_h = nc.dram_tensor("wq_scratch", [D, D], BF16)
    wq_d = wq_h.ap()
    modscr_h = nc.dram_tensor("mod_scratch", [6 * D], F32)
    modscr = modscr_h.ap()

    script = {e: [] for e in ENGS}
    cnt = dict(v=0, a=0, t=0, **{k: 0 for k in DMA_SEMS})
    DEFAULT_BUMP = dict(v="v", a="a", t="t", g=None, s=None)

    waited = {k: set() for k in cnt}
    cnt_tok = {}

    def E(eng, fn, waits=(), bump="default", war=False, defer=False):
        if bump == "default":
            bump = DEFAULT_BUMP[eng]
        w = tuple((k, val) for (k, val) in waits if val and val > 0)
        for (k, val) in w:
            waited[k].add(val)
        tok = None
        if bump is not None:
            cnt[bump] += STEP[bump]
            tok = cnt[bump]
        cnt_tok[id(fn)] = tok
        script[ENG_OF[eng]].append((w, fn, bump, war, defer))
        return tok

    def FLUSHV():
        return E("v", lambda h: h.tensor_copy(flsh[:, 512:1024], flsh[:, 0:512]))

    def DRAIN(eng):
        E(eng, lambda h: h.drain(), bump=None)

    def FLUSHA():
        return E("a", lambda h: h.activation(flsh[:, 512:1024], flsh[:, 0:512],
                                             AF.Copy))

    stack = contextlib.ExitStack()
    with stack:
        sems = {}
        for k in ("v", "a", "t") + DMA_SEMS:
            sems[k] = stack.enter_context(nc.semaphore("sem_" + k))

        def ST(name, shape, dt):
            return stack.enter_context(nc.sbuf_tensor(name, shape, dt))

        def PT(name, shape, dt):
            return stack.enter_context(nc.psum_tensor(name, shape, dt))

        xf = ST("xf", [128, KD * T], F32)
        hb = ST("hb", [128, KD * T], BF16)
        dtb = ST("dtb", [128, KD * T], BF16)
        vb = ST("vb", [128, KD * T], BF16)
        yac = ST("yac", [128, KD * T], BF16)
        da = ST("da", [128, 8 * T], BF16)
        ubf = ST("ubf", [128, T], BF16)
        sbf = ST("sbf", [128, T], BF16)
        scb = ST("scb", [128, T], BF16)
        brepall = ST("brepall", [128, NN * T], BF16)
        crepall = ST("crepall", [128, NN * T], BF16)
        scq = ST("scq", [128, 4 * T], BF16)
        identb = ST("identb", [128, 128], BF16)
        carry = ST("carry", [128, NN * KD], BF16)
        xprj = ST("xprj", [96, T], BF16)
        e1b = ST("e1b", [128, 2 * T], BF16)
        outst = ST("outst", [128, 2 * T], F32)
        wqb = ST("wqb", [128, 2 * KD * 128], BF16)
        t1f = ST("t1f", [128, T], F32)
        mrow = ST("mrow", [1, T], F32)
        e2row = ST("e2row", [1, T], F32)
        vrow = ST("vrow", [1, T], F32)
        isrow = ST("isrow", [1, T], F32)
        m_bf = ST("m_bf", [1, T], BF16)
        is_bf = ST("is_bf", [1, T], BF16)
        mrep_sb = ST("mrep_sb", [128, T], BF16)
        isrep_sb = ST("isrep_sb", [128, T], BF16)
        wx_sb = ST("wx_sb", [128, KD * 96], BF16)
        wdt_sb = ST("wdt_sb", [RD, D], BF16)
        a_sb = ST("a_sb", [128, KD * NN], F32)
        bdt_sb = ST("bdt_sb", [128, KD], F32)
        dvec_sb = ST("dvec_sb", [128, KD], F32)
        fbias_sb = ST("fbias_sb", [128, KD], F32)
        cbq = ST("cbq", [128, KD], BF16)
        modmat = ST("modmat", [128, 96], F32)
        onepca = ST("onepca", [128, KD], F32)
        ga_cl = ST("ga_cl", [128, KD], F32)
        onepcb = ST("onepcb", [128, KD], F32)
        gb_cl = ST("gb_cl", [128, KD], F32)
        gq = ST("gq", [128, KD], F32)
        gbb = ST("gbb", [128, KD], F32)
        invcol = ST("invcol", [128, 1], F32)
        scol = ST("scol", [128, 1], F32)
        sc_sc = ST("sc_sc", [1, 2], F32)
        invr = ST("invr", [1, 1], F32)
        abscol = ST("abscol", [128, KD], F32)
        abssum = ST("abssum", [128, 1], F32)
        modrow = ST("modrow", [1, 256], F32)
        ones128f = ST("ones128f", [128, 1], F32)
        ones128b = ST("ones128b", [128, 1], BF16)
        ones1f = ST("ones1f", [1, 128], F32)
        ones1b = ST("ones1b", [1, 128], BF16)
        onehotm = ST("onehotm", [96, 32 * 128], BF16)
        qtmp = ST("qtmp", [128, 4 * KD], F32)
        t2b = ST("t2b", [128, T], BF16)
        vrow2 = ST("vrow2", [1, T], F32)
        isrow2 = ST("isrow2", [1, T], F32)
        flsh = ST("flsh", [128, 1024], F32)

        ps_s1 = PT("ps_s1", [128, T], F32)
        ps_s2 = PT("ps_s2", [128, T], F32)
        ps_m = PT("ps_m", [128, T], F32)
        ps_i = PT("ps_i", [128, T], F32)
        ps_b0 = PT("ps_b0", [128, T], F32)
        ps_b1 = PT("ps_b1", [128, T], F32)
        ps_f0 = PT("ps_f0", [128, T], F32)
        ps_f1 = PT("ps_f1", [128, T], F32)

        E("v", lambda h: h.memset(ones128f[:, :], 1.0))
        E("v", lambda h: h.memset(ones128b[:, :], 1.0))
        E("v", lambda h: h.memset(ones1f[:, :], 1.0))
        E("v", lambda h: h.memset(ones1b[:, :], 1.0))

        E("v", lambda h: h.memset(carry[:, :], 0.0))

        E("g", lambda h: h.dma_start(
            out=xf[:, 0:KD * 96].rearrange("p (k c) -> p k c", k=KD),
            in_=wxT.rearrange("(k p) c -> p k c", p=128)), bump="sw")
        E("g", lambda h: h.dma_start(
            out=xf[0:RD, 2048:2048 + D], in_=wdtT[:, :]), bump="sw")
        E("g", lambda h: h.dma_start(
            out=xf[:, 4096:4096 + KD * NN], in_=alog[:, :]), bump="sw")
        E("g", lambda h: h.dma_start(out=bdt_sb[:, :], in_=bdt[:, :]), bump="sw")
        E("g", lambda h: h.dma_start(out=dvec_sb[:, :], in_=dvec[:, :]), bump="sw")
        E("g", lambda h: h.dma_start(out=fbias_sb[:, :], in_=fbias[:, :]),
          bump="sw")
        E("g", lambda h: h.dma_start(
            out=xf[:, 4624:4624 + KD], in_=cb_in[:, :]), bump="sw")
        E("g", lambda h: h.dma_start(out=onehotm[:, :], in_=onehot_d[:, :]),
          bump="sw")
        E("g", lambda h: h.dma_start(out=identb[:, :], in_=ident_d[:, :]),
          bump="sw")
        tok_sw = cnt["sw"]


        tok_wx = E("v", lambda h: h.tensor_copy(wx_sb[:, :], xf[:, 0:KD * 96]),
                   waits=[("sw", tok_sw)])
        tok_wdt = E("v", lambda h: h.tensor_copy(wdt_sb[:, :],
                                                 xf[0:RD, 2048:2048 + D]),
                    waits=[("sw", tok_sw)])
        tok_ae = E("a", lambda h: h.activation(xf[:, 4352:4352 + KD * NN],
                                               xf[:, 4096:4096 + KD * NN], AF.Exp),
                   waits=[("sw", tok_sw)])
        tok_a = E("v", lambda h: h.tensor_scalar(
            a_sb[:, :], xf[:, 4352:4352 + KD * NN], -1.0, None, OP.mult),
            waits=[("a", tok_ae)])
        tok_af = FLUSHV()
        tok_cs = E("a", lambda h: h.activation(qtmp[:, 0:KD],
                                               xf[:, 4624:4624 + KD], AF.Silu),
                   waits=[("sw", tok_sw)])
        tok_cb = E("v", lambda h: h.tensor_copy(cbq[:, :], qtmp[:, 0:KD]),
                   waits=[("a", tok_cs)])

        wst = [xf[:, 0:D], xf[:, D:2 * D]]
        tok_v_wst = [tok_wx, tok_wdt]
        tok_wst_d = [0, 0]
        for k in range(KD):
            p = k % 2
            tok_wst_d[p] = E("g", lambda h, k=k, p=p: h.dma_start(
                out=wst[p], in_=wT[k * 128:(k + 1) * 128, :]),
                waits=[("v", tok_v_wst[p])], bump="ws%d" % p)
            tok_v_wst[p] = E("v", lambda h, k=k, p=p: h.reduce_sum(
                abscol[:, k:k + 1], wst[p], axis=AX.X, apply_absolute_value=True),
                waits=[("ws%d" % p, tok_wst_d[p])])
        FLUSHV()
        E("v", lambda h: h.reduce_sum(abssum[:, :], abscol[:, :], axis=AX.X))
        tok_absc = FLUSHV()
        tok_sred = E("t", lambda h: h.matmul(ps_s1[0:1, 0:1], ones128f[:, :],
                                             abssum[:, :], start=True, stop=True),
                     waits=[("v", tok_absc)])
        E("v", lambda h: h.tensor_scalar(sc_sc[0:1, 0:1], ps_s1[0:1, 0:1],
                                         1.0 / (D * D), None, OP.mult),
          waits=[("t", tok_sred)])
        FLUSHV()
        E("v", lambda h: h.tensor_scalar(sc_sc[0:1, 1:2], sc_sc[0:1, 0:1],
                                         Q_EPS, None, OP.add))
        FLUSHV()
        E("v", lambda h: h.reciprocal(invr[0:1, :], sc_sc[0:1, 1:2]))
        tok_inv = FLUSHV()
        tok_bc1 = E("t", lambda h: h.matmul(ps_m[:, 0:1], ones1f[:, :],
                                            invr[0:1, :], start=True, stop=True),
                    waits=[("v", tok_inv)])
        tok_bc2 = E("t", lambda h: h.matmul(ps_i[:, 0:1], ones1f[:, :],
                                            sc_sc[0:1, 0:1], start=True, stop=True),
                    waits=[("v", tok_inv)])
        E("v", lambda h: h.tensor_copy(invcol[:, :], ps_m[:, 0:1]),
          waits=[("t", tok_bc1)])
        E("v", lambda h: h.tensor_copy(scol[:, :], ps_i[:, 0:1]),
          waits=[("t", tok_bc2)])
        tok_invc = FLUSHV()
        tok_scol = tok_invc

        qabs = [xf[:, 4096 + 0:4096 + D], xf[:, 6144:6144 + D]]
        qsgn = [hb[:, 0:D], hb[:, D:2 * D]]
        qmsk = [hb[:, 2 * D:3 * D], hb[:, 3 * D:4 * D]]
        wqt = [dtb[:, 0:D], dtb[:, D:2 * D]]
        tok_wq_wr = [0, 0]
        tok_v_wq = [0, 0]
        tok_a_abs = [0, 0]
        tok_msk_v = [0, 0]
        for k in range(KD):
            p = k % 2
            tok_wst_d[p] = E("g", lambda h, k=k, p=p: h.dma_start(
                out=wst[p], in_=wT[k * 128:(k + 1) * 128, :]),
                waits=[("v", tok_v_wst[p]), ("a", tok_a_abs[p])],
                bump="ws%d" % p)
            E("a", lambda h, p=p: h.activation(qsgn[p], wst[p], AF.Sign),
              waits=[("ws%d" % p, tok_wst_d[p]), ("v", tok_v_wq[p])])
            tok_a_abs[p] = E("a", lambda h, p=p: h.activation(
                qabs[p], wst[p], AF.Abs, scale=invcol[:, 0:1]),
                waits=[("v", tok_invc), ("v", tok_msk_v[p])])
            tok_msk_v[p] = E("v", lambda h, p=p: h.tensor_single_scalar(
                qmsk[p], qabs[p], 0.5, OP.is_ge),
                waits=[("a", tok_a_abs[p])], war=True)
            tok_v_wst[p] = tok_msk_v[p]
            tok_v_wq[p] = E("v", lambda h, p=p: h.tensor_mul(
                wqt[p], qsgn[p], qmsk[p]),
                waits=[("qw%d" % p, tok_wq_wr[p])])
            tok_wq_wr[p] = E("g", lambda h, k=k, p=p: h.dma_start(
                out=wq_d[k * 128:(k + 1) * 128, :], in_=wqt[p]),
                waits=[("v", tok_v_wq[p])], bump="qw%d" % p)
        tok_quant_v = cnt["v"]

        wmf = [xf[:, 0:D], xf[:, D:2 * D]]
        wmb = [dtb[:, 0:D], dtb[:, D:2 * D]]
        tok_wm_d = [0, 0]
        tok_wm_v = [tok_quant_v, tok_quant_v]
        tok_wm_t = [0, 0]
        tok_mod_a = 0
        tok_mr_d = [0, 0]
        for fc in range(NFC):
            p = fc % 2
            tok_wm_d[p] = E("g", lambda h, fc=fc, p=p: h.dma_start(
                out=wmf[p], in_=wmod[fc * 128:(fc + 1) * 128, :]),
                waits=[("v", tok_wm_v[p])], bump="wm%d" % p)
            tok_wm_v[p] = E("v", lambda h, p=p: h.tensor_copy(wmb[p], wmf[p]),
                            waits=[("wm%d" % p, tok_wm_d[p]),
                                   ("t", tok_wm_t[p])])
            tmod = 0
            for k in range(KD):
                w = []
                if k == 0:
                    w = [("v", tok_wm_v[p]), ("v", tok_cb), ("a", tok_mod_a)]
                tmod = E("t", lambda h, p=p, k=k: h.matmul(
                    ps_f0[0:1, 0:128], cbq[:, k:k + 1],
                    wmb[p][:, k * 128:(k + 1) * 128],
                    start=(k == 0), stop=(k == KD - 1)), waits=w)
            tok_wm_t[p] = tmod
            DRAIN("t")
            tok_mod_a = E("a", lambda h, fc=fc, p=p: h.activation(
                modrow[0:1, p * 128:(p + 1) * 128], ps_f0[0:1, 0:128], AF.Copy),
                waits=[("t", tmod), ("md", tok_mr_d[p])])
            tok_mr_d[p] = E("s", lambda h, fc=fc, p=p: h.dma_start(
                out=modscr[fc * 128:(fc + 1) * 128].rearrange("(a b) -> a b", a=1),
                in_=modrow[0:1, p * 128:(p + 1) * 128]),
                waits=[("a", tok_mod_a)], bump="md")
        tok_wm_full = [cnt["wm0"], cnt["wm1"]]

        tok_mm_d = E("s", lambda h: h.dma_start(
            out=modmat[:, :].rearrange("p (v k) -> p v k", v=6),
            in_=bass.AP(modscr_h, 0, [[1, 128], [D, 6], [128, KD]]),
            allow_slow_non_contiguous=True),
            waits=[("md", cnt["md"])], bump="md")

        E("v", lambda h: h.tensor_scalar(onepca[:, :], modmat[:, 16:32], 1.0, None,
                                         OP.add), waits=[("md", tok_mm_d)])
        E("v", lambda h: h.tensor_scalar(onepcb[:, :], modmat[:, 64:80], 1.0, None,
                                         OP.add))

        def emit_clamp(src_off, dst, prev_tok):
            E("v", lambda h: h.tensor_single_scalar(
                qtmp[:, 0:KD], modmat[:, src_off:src_off + KD], 0.0, OP.is_ge),
              waits=[("md", tok_mm_d)])
            FLUSHV()
            E("v", lambda h: h.tensor_scalar(qtmp[:, 2 * KD:3 * KD], qtmp[:, 0:KD],
                                             2.0, -1.0, OP.mult, OP.add))
            ta = E("a", lambda h: h.activation(
                qtmp[:, KD:2 * KD], modmat[:, src_off:src_off + KD], AF.Abs),
                waits=[("md", tok_mm_d), ("v", prev_tok)])
            FLUSHA()
            ta = FLUSHA()
            E("v", lambda h: h.tensor_scalar(qtmp[:, 3 * KD:4 * KD],
                                             qtmp[:, KD:2 * KD],
                                             0.25, None, OP.max),
              waits=[("a", ta)])
            FLUSHV()
            return E("v", lambda h, dst=dst: h.tensor_mul(
                dst[:, :], qtmp[:, 2 * KD:3 * KD], qtmp[:, 3 * KD:4 * KD]))

        tok_ga = emit_clamp(32, ga_cl, 0)
        FLUSHV()
        tok_gb = emit_clamp(80, gb_cl, tok_ga)
        FLUSHV()
        E("v", lambda h: h.tensor_scalar(gq[:, :], gb_cl[:, :], scol[:, 0:1], None,
                                         OP.mult), waits=[("v", tok_scol)])
        FLUSHV()
        E("v", lambda h: h.tensor_mul(gbb[:, :], gb_cl[:, :], fbias_sb[:, :]))
        tok_gbb = FLUSHV()
        tok_pro_v = cnt["v"]
        tok_pro_a = cnt["a"]
        tok_pro_t = cnt["t"]

        tok_xf_free_v = tok_pro_v
        tok_hb_free_t = tok_pro_t
        tok_s1_free_a = tok_pro_a
        tok_s1_free_v = tok_pro_v
        tok_s2_free_v = tok_pro_v
        tok_m_free_v = tok_pro_v
        tok_i_free_v = tok_pro_v
        tok_b0_free_v = tok_pro_v
        tok_b1_free_v = tok_pro_v
        tok_f_free_a = [tok_pro_a, 0]
        tok_scan = 0
        tok_scan_par = [tok_af, tok_af]
        tok_imm = [0, 0, 0, 0]
        tok_wqb_rd = [0, 0]
        tok_out_dma = [0, 0]
        tok_outst_v = [0, 0]


        for m in range(NM_RUN):
            c0 = m * T
            for k in range(KD):
                w = []
                if k == 0:
                    w = [("v", tok_xf_free_v)]
                    if m == 0:
                        w += [("wm0", tok_wm_full[0]), ("wm1", tok_wm_full[1])]
                E("s", lambda h, k=k, c0=c0: h.dma_start(
                    out=xf[:, k * T:(k + 1) * T],
                    in_=xT[k * 128:(k + 1) * 128, c0:c0 + T]),
                    waits=w, bump="xd")
            tok_x = cnt["xd"]

            def emit_ln(onepc, svec_off, tok_src_v):
                nonlocal tok_s1_free_a, tok_s1_free_v, tok_s2_free_v
                nonlocal tok_m_free_v, tok_i_free_v, tok_hb_free_t
                ta_last = 0
                for k in range(KD):
                    w = [("xd", tok_x), ("v", tok_src_v)]
                    if k == 0:
                        w.append(("t", tok_hb_free_t))
                    ta_last = E("a", lambda h, k=k: h.activation(
                        hb[:, k * T:(k + 1) * T], xf[:, k * T:(k + 1) * T],
                        AF.Square), waits=w)
                tt1 = 0
                for k in range(KD):
                    w = [("xd", tok_x), ("v", tok_src_v)]
                    if k == 0:
                        w += [("v", tok_s1_free_v), ("a", tok_s1_free_a)]
                    tt1 = E("t", lambda h, k=k: h.matmul(
                        ps_s1[0:1, :], ones128f[:, :], xf[:, k * T:(k + 1) * T],
                        start=(k == 0), stop=(k == KD - 1)), waits=w)
                tt2 = 0
                for k in range(KD):
                    w = [("a", ta_last)] if k == 0 else []
                    if k == 0:
                        w.append(("v", tok_s2_free_v))
                    tt2 = E("t", lambda h, k=k: h.matmul(
                        ps_s2[0:1, :], ones128b[:, :], hb[:, k * T:(k + 1) * T],
                        start=(k == 0), stop=(k == KD - 1)), waits=w)
                return emit_ln_finish(onepc, svec_off, tt1, tt2)

            def emit_ln_finish(onepc, svec_off, tt1, tt2):
                nonlocal tok_s1_free_a, tok_s1_free_v, tok_s2_free_v
                nonlocal tok_m_free_v, tok_i_free_v, tok_hb_free_t
                tok_s1_free_v = E("v", lambda h: h.tensor_scalar(
                    mrow[0:1, :], ps_s1[0:1, :], 1.0 / D, None, OP.mult),
                    waits=[("t", tt1)], war=True)
                tok_s2_free_v = E("v", lambda h: h.tensor_scalar(
                    e2row[0:1, :], ps_s2[0:1, :], 1.0 / D, None, OP.mult),
                    waits=[("t", tt2)], war=True)
                E("v", lambda h: h.tensor_mul(vrow[0:1, :], mrow[0:1, :],
                                              mrow[0:1, :]))
                E("v", lambda h: h.tensor_sub(vrow2[0:1, :], e2row[0:1, :],
                                              vrow[0:1, :]))
                tv = E("v", lambda h: h.tensor_scalar(vrow[0:1, :], vrow2[0:1, :],
                                                      LN_EPS, None, OP.add))
                tln = E("a", lambda h: h.activation(isrow[0:1, :], vrow[0:1, :],
                                                    AF.Ln), waits=[("v", tv)])
                tlh = E("v", lambda h: h.tensor_scalar(
                    isrow2[0:1, :], isrow[0:1, :], -0.5, None, OP.mult),
                    waits=[("a", tln)])
                ti = E("a", lambda h: h.activation(isrow[0:1, :], isrow2[0:1, :],
                                                   AF.Exp),
                       waits=[("v", tlh)])
                tmb = E("v", lambda h: h.tensor_copy(m_bf[0:1, :], mrow[0:1, :]))
                tib = E("v", lambda h: h.tensor_copy(is_bf[0:1, :], isrow[0:1, :]),
                        waits=[("a", ti)])
                trm = E("t", lambda h: h.matmul(ps_m[:, :], ones1b[:, :],
                                                m_bf[0:1, :], start=True, stop=True),
                        waits=[("v", tmb), ("v", tok_m_free_v)])
                tri = E("t", lambda h: h.matmul(ps_i[:, :], ones1b[:, :],
                                                is_bf[0:1, :], start=True, stop=True),
                        waits=[("v", tib), ("v", tok_i_free_v)])
                tok_m_free_v = E("v", lambda h: h.tensor_copy(mrep_sb[:, :],
                                                              ps_m[:, :]),
                                 waits=[("t", trm)], war=True)
                tok_i_free_v = E("v", lambda h: h.tensor_copy(isrep_sb[:, :],
                                                              ps_i[:, :]),
                                 waits=[("t", tri)], war=True)
                tout = 0
                for k in range(KD):
                    E("v", lambda h, k=k: h.tensor_sub(
                        t1f[:, :], xf[:, k * T:(k + 1) * T], mrep_sb[:, :]))
                    E("v", lambda h, k=k, onepc=onepc: h.scalar_tensor_tensor(
                        t2b[:, :], t1f[:, :], onepc[:, k:k + 1],
                        isrep_sb[:, :], OP.mult, OP.mult))
                    tout = E("v", lambda h, k=k, svec_off=svec_off: h.tensor_scalar(
                        hb[:, k * T:(k + 1) * T], t2b[:, :],
                        modmat[:, svec_off + k:svec_off + k + 1], None, OP.add))
                return tout

            tok_h = emit_ln(onepca, 0, tok_xf_free_v if m == 0 else tok_x2_last)

            txp = 0
            for k in range(KD):
                w = []
                if k == 0:
                    w = [("v", tok_h), ("a", tok_f_free_a[0])]
                    if m == 0:
                        w.append(("v", tok_wx))
                txp = E("t", lambda h, k=k: h.matmul(
                    ps_f0[0:96, :], wx_sb[:, k * 96:(k + 1) * 96],
                    hb[:, k * T:(k + 1) * T],
                    start=(k == 0), stop=(k == KD - 1)), waits=w)
            ta_xp = E("a", lambda h: h.activation(xprj[:, :], ps_f0[0:96, :],
                                                  AF.Copy), waits=[("t", txp)])
            tok_f_free_a[0] = ta_xp
            tok_sp = []
            prev_sp = 0
            for k in range(KD):
                w = [("a", prev_sp)]
                if k == 0:
                    w = [("a", ta_xp), ("v", tok_s1_free_v)]
                    if m == 0:
                        w.append(("v", tok_wdt))
                tdt = E("t", lambda h, k=k: h.matmul(
                    ps_s1[:, :], wdt_sb[:, k * 128:(k + 1) * 128], xprj[0:RD, :],
                    start=True, stop=True), waits=w)
                tex = E("a", lambda h, k=k: h.activation(
                    t1f[:, :], ps_s1[:, :], AF.Exp,
                    bias=bdt_sb[:, k:k + 1]), waits=[("t", tdt)])
                DRAIN("a")
                prev_sp = E("a", lambda h, k=k: h.activation(
                    dtb[:, k * T:(k + 1) * T], t1f[:, :], AF.Ln,
                    bias=ones128f[:, 0:1]))
                tok_sp.append(prev_sp)
            tok_s1_free_a = tok_sp[-1]
            for k in range(KD):
                E("v", lambda h, k=k: h.tensor_mul(
                    vb[:, k * T:(k + 1) * T], dtb[:, k * T:(k + 1) * T],
                    hb[:, k * T:(k + 1) * T]), waits=[("a", tok_sp[k])])

            for n in range(NN):
                p = n % 2
                psb = ps_b0 if p == 0 else ps_m
                psc = ps_b1 if p == 0 else ps_i
                wb = [("a", ta_xp),
                      ("v", tok_b0_free_v if p == 0 else tok_m_free_v)]
                tb = E("t", lambda h, n=n, psb=psb: h.matmul(
                    psb[:, :], onehotm[:, n * 128:(n + 1) * 128], xprj[:, :],
                    start=True, stop=True), waits=wb)
                wc = [("v", tok_b1_free_v if p == 0 else tok_i_free_v)]
                tc = E("t", lambda h, n=n, psc=psc: h.matmul(
                    psc[:, :], onehotm[:, (16 + n) * 128:(17 + n) * 128],
                    xprj[:, :], start=True, stop=True), waits=wc)
                tbr = E("v", lambda h, n=n, psb=psb: h.tensor_copy(
                    brepall[:, n * T:(n + 1) * T], psb[:, :]),
                    waits=[("t", tb)], war=True)
                tcr = E("v", lambda h, n=n, psc=psc: h.tensor_copy(
                    crepall[:, n * T:(n + 1) * T], psc[:, :]),
                    waits=[("t", tc)], war=True)
                if p == 0:
                    tok_b0_free_v, tok_b1_free_v = tbr, tcr
                else:
                    tok_m_free_v, tok_i_free_v = tbr, tcr

            tok_x2_last = 0
            tt1 = tt2 = 0
            for k in range(KD):
                for ng in range(4):
                    gpar = (k * 4 + ng) % 2
                    te_grp = 0
                    for j4 in range(4):
                        n = ng * 4 + j4
                        slot = gpar * 4 + j4
                        da_t = da[:, slot * T:(slot + 1) * T]
                        w = [("v", tok_scan_par[gpar])] if j4 == 0 else []
                        te_grp = E("a", lambda h, k=k, n=n, da_t=da_t:
                                   h.activation(
                                       da_t, dtb[:, k * T:(k + 1) * T], AF.Exp,
                                       scale=a_sb[:, k * NN + n:k * NN + n + 1]),
                                   waits=w, defer=True)
                    for j4 in range(4):
                        n = ng * 4 + j4
                        idx = n * KD + k
                        slot = gpar * 4 + j4
                        da_t = da[:, slot * T:(slot + 1) * T]
                        sq = n % 4
                        sct = scq[:, sq * T:(sq + 1) * T]
                        E("v", lambda h, k=k, n=n: h.tensor_mul(
                            ubf[:, :], vb[:, k * T:(k + 1) * T],
                            brepall[:, n * T:(n + 1) * T]))
                        tok_scan = E("v", lambda h, da_t=da_t, idx=idx:
                                     h.tensor_tensor_scan(
                                         sbf[:, :], da_t, ubf[:, :],
                                         carry[:, idx:idx + 1], OP.mult, OP.add),
                                     waits=[("a", te_grp)] if j4 == 0 else [],
                                     war=True)
                        tok_scan_par[gpar] = tok_scan
                        tok_scm = E("v", lambda h, n=n, sct=sct: h.tensor_mul(
                            sct, sbf[:, :], crepall[:, n * T:(n + 1) * T]),
                            waits=[("t", tok_imm[sq])], defer=True)
                        E("v", lambda h, idx=idx: h.tensor_copy(
                            carry[:, idx:idx + 1], sbf[:, T - 1:T]))
                        wi = [("v", tok_scm)]
                        if n == 0:
                            wi.append(("a", tok_f_free_a[1]))
                        tok_imm[sq] = E("t", lambda h, n=n, sct=sct: h.matmul(
                            ps_f1[:, :], identb[:, :], sct,
                            start=(n == 0), stop=(n == NN - 1)), waits=wi)
                tev = E("a", lambda h, k=k: h.activation(
                    yac[:, k * T:(k + 1) * T], ps_f1[:, :], AF.Copy),
                    waits=[("t", tok_imm[(NN - 1) % 4])])
                tok_f_free_a[1] = tev
                wst1 = [("a", tev)]
                if m == 0 and k == 0:
                    wst1.append(("v", tok_ga))
                E("v", lambda h, k=k: h.scalar_tensor_tensor(
                    yac[:, k * T:(k + 1) * T], hb[:, k * T:(k + 1) * T],
                    dvec_sb[:, k:k + 1], yac[:, k * T:(k + 1) * T],
                    OP.mult, OP.add), waits=wst1)
                tok_x2_last = E("v", lambda h, k=k: h.scalar_tensor_tensor(
                    xf[:, k * T:(k + 1) * T], yac[:, k * T:(k + 1) * T],
                    ga_cl[:, k:k + 1], xf[:, k * T:(k + 1) * T],
                    OP.mult, OP.add), defer=True)
                tsq = E("a", lambda h, k=k: h.activation(
                    hb[:, k * T:(k + 1) * T], xf[:, k * T:(k + 1) * T],
                    AF.Square),
                    waits=[("v", tok_x2_last)] +
                          ([("t", tok_hb_free_t)] if k == 0 else []))
                w1 = [("v", tok_x2_last)]
                if k == 0:
                    w1 += [("v", tok_s1_free_v), ("a", tok_s1_free_a)]
                tt1 = E("t", lambda h, k=k: h.matmul(
                    ps_s1[0:1, :], ones128f[:, :], xf[:, k * T:(k + 1) * T],
                    start=(k == 0), stop=(k == KD - 1)), waits=w1)
                w2 = [("a", tsq)]
                if k == 0:
                    w2.append(("v", tok_s2_free_v))
                tt2 = E("t", lambda h, k=k: h.matmul(
                    ps_s2[0:1, :], ones128b[:, :], hb[:, k * T:(k + 1) * T],
                    start=(k == 0), stop=(k == KD - 1)), waits=w2)
            DRAIN("a")

            tok_h2 = emit_ln_finish(onepcb, 48, tt1, tt2)
            tok_hb_free_t = txp

            for j in range(KD):
                p = j % 2
                psf = ps_f0 if p == 0 else ps_f1
                col0 = p * KD * 128
                for q in range(4):
                    w = []
                    if q == 0:
                        w = [("t", tok_wqb_rd[p])]
                        if m == 0 and j < 2:
                            w += [("qw0", cnt["qw0"]), ("qw1", cnt["qw1"])]
                    E("g", lambda h, j=j, q=q, col0=col0: h.dma_start(
                        out=wqb[:, col0 + q * 512:col0 + (q + 1) * 512],
                        in_=bass.AP(wq_h, (q * 4 * 128) * D + j * 128,
                                    [[D, 128], [128 * D, 4], [1, 128]])),
                        waits=w, bump="wq%d" % p)
                tok_wq_dma = cnt["wq%d" % p]
                tmmf = 0
                for k in range(KD):
                    w = []
                    if k == 0:
                        w = [("wq%d" % p, tok_wq_dma), ("v", tok_h2),
                             ("a", tok_f_free_a[p])]
                    tmmf = E("t", lambda h, k=k, psf=psf, col0=col0: h.matmul(
                        psf[:, :], wqb[:, col0 + k * 128:col0 + (k + 1) * 128],
                        hb[:, k * T:(k + 1) * T],
                        start=(k == 0), stop=(k == KD - 1)), waits=w)
                tok_wqb_rd[p] = tmmf
                we = [("t", tmmf), ("v", tok_outst_v[p])]
                if m == 0 and j == 0:
                    we.append(("v", tok_gbb))
                te1 = E("a", lambda h, j=j, psf=psf, p=p: h.activation(
                    e1b[:, p * T:(p + 1) * T], psf[:, :], AF.Identity,
                    bias=gbb[:, j:j + 1], scale=gq[:, j:j + 1]), waits=we)
                tok_f_free_a[p] = te1
                tok_outst_v[p] = E("v", lambda h, j=j, p=p: h.tensor_add(
                    outst[:, p * T:(p + 1) * T], xf[:, j * T:(j + 1) * T],
                    e1b[:, p * T:(p + 1) * T]),
                    waits=[("a", te1), ("od%d" % p, tok_out_dma[p])])
                E("s", lambda h, j=j, p=p, c0=c0: h.dma_start(
                    out=yT[j * 128:(j + 1) * 128, c0:c0 + T],
                    in_=outst[:, p * T:(p + 1) * T]),
                    waits=[("v", tok_outst_v[p])], bump="od%d" % p)
                tok_out_dma[p] = cnt["od%d" % p]
            tok_xf_free_v = max(tok_outst_v)
            tok_hb_free_t = max(tok_wqb_rd)

        if DBG["on"]:
            wall = [("v", cnt["v"]), ("a", cnt["a"]), ("t", cnt["t"]),
                    ("od0", cnt["od0"]), ("od1", cnt["od1"])]
            E("s", lambda h: h.dma_start(out=dbg_x2[:, :], in_=xf[:, :]),
              waits=wall, bump="dbgd")
            E("s", lambda h: h.dma_start(out=dbg_ym[:, :], in_=yac[:, :]),
              bump="dbgd")
            E("s", lambda h: h.dma_start(out=dbg_dt[:, :], in_=dtb[:, :]),
              bump="dbgd")
            E("s", lambda h: h.dma_start(out=dbg_vb[:, :], in_=vb[:, :]),
              bump="dbgd")
            E("s", lambda h: h.dma_start(out=dbg_h2[:, :], in_=hb[:, :]),
              bump="dbgd")
            E("s", lambda h: h.dma_start(out=dbg_xp[:, :], in_=xprj[:, :]),
              bump="dbgd")
            E("s", lambda h: h.dma_start(out=dbg_mm[:, :], in_=modmat[:, :]),
              bump="dbgd")
            E("s", lambda h: h.dma_start(out=dbg_g[:, 0:KD], in_=ga_cl[:, :]),
              bump="dbgd")
            E("s", lambda h: h.dma_start(out=dbg_g[:, KD:2 * KD], in_=gb_cl[:, :]),
              bump="dbgd")
            E("s", lambda h: h.dma_start(out=dbg_g[:, 2 * KD:3 * KD], in_=gq[:, :]),
              bump="dbgd")
            E("s", lambda h: h.dma_start(out=dbg_g[:, 3 * KD:4 * KD],
                                         in_=gbb[:, :]), bump="dbgd")
            E("s", lambda h: h.dma_start(out=dbg_cy[:, :], in_=carry[:, :]),
              bump="dbgd")
            E("s", lambda h: h.dma_start(out=dbg_as[:, :], in_=a_sb[:, :]),
              bump="dbgd")

        E("s", lambda h: None,
          waits=[("od0", cnt["od0"]), ("od1", cnt["od1"]),
                 ("dbgd", cnt["dbgd"])], bump=None)

        def runner(tasks, eng_name):
            compute = eng_name in ("vector", "scalar", "tensor")

            def f(h):
                pending = None
                for (waits, fn, bump, war, defer) in tasks:
                    if pending is not None and waits:
                        h.drain().then_inc(*pending)
                        pending = None
                    for (sk, val) in waits:
                        h.wait_ge(sems[SEM_OF_CNT[sk]], val)
                    if pending is not None and bump is not None and not defer:
                        h.drain().then_inc(*pending)
                        pending = None
                    instr = fn(h)
                    if instr is None:
                        continue
                    if pending is not None:
                        instr.then_inc(*pending)
                        pending = None
                    if bump is None:
                        continue
                    if defer:
                        pending = (sems[bump], STEP[bump])
                        continue
                    needs_drain = (compute and not war and
                                   cnt_tok[id(fn)] in waited[bump])
                    if needs_drain:
                        h.drain().then_inc(sems[bump], STEP[bump])
                    else:
                        instr.then_inc(sems[bump], STEP[bump])
                if pending is not None:
                    h.drain().then_inc(*pending)
            return f

        with nc.Block() as block:
            block.sync(runner(script["sync"], "sync"))
            block.gpsimd(runner(script["gpsimd"], "gpsimd"))
            block.tensor(runner(script["tensor"], "tensor"))
            block.vector(runner(script["vector"], "vector"))
            block.scalar(runner(script["scalar"], "scalar"))

    return nc


def kernel(L0_attn_x, L0_attn_W_x, L0_attn_W_dt, L0_attn_b_dt, L0_attn_A_log,
           L0_attn_D, L0_ffn_weight, L0_ffn_bias, L0_c, L0_wmod):
    nc = build_nc()

    wxT = np.ascontiguousarray(L0_attn_W_x.T)
    wdtT = np.ascontiguousarray(L0_attn_W_dt.T)
    bdt_a = np.ascontiguousarray(L0_attn_b_dt.reshape(KD, 128).T)
    alog_a = np.ascontiguousarray(
        L0_attn_A_log.reshape(KD, 128, NN).transpose(1, 0, 2).reshape(128, KD * NN))
    dvec_a = np.ascontiguousarray(L0_attn_D.reshape(KD, 128).T)
    fbias_a = np.ascontiguousarray(L0_ffn_bias.reshape(KD, 128).T)
    wT_a = np.ascontiguousarray(L0_ffn_weight.T)
    wmod_a = np.ascontiguousarray(
        L0_wmod.reshape(KD, 128, NFC, 128).transpose(2, 1, 0, 3)
        .reshape(NFC * 128, KD * 128))

    import ml_dtypes
    onehot = np.zeros((96, 32 * 128), dtype=ml_dtypes.bfloat16)
    ident = np.eye(128, dtype=ml_dtypes.bfloat16)
    for j in range(32):
        onehot[64 + j, j * 128:(j + 1) * 128] = 1.0

    in_maps = []
    for b in range(NB):
        in_maps.append({
            "xT": np.ascontiguousarray(L0_attn_x[b].T),
            "wxT": wxT, "wdtT": wdtT, "bdt": bdt_a, "alog": alog_a,
            "dvec": dvec_a, "fbias": fbias_a, "wT": wT_a,
            "cb_in": np.ascontiguousarray(L0_c[b].reshape(KD, 128).T),
            "wmod_r": wmod_a, "onehot_bc": onehot, "ident_bc": ident,
        })
    res = run_bass_kernel_spmd(nc, in_maps, core_ids=list(range(NB)))
    if DBG["on"]:
        DBG["res"] = res.results
    out = np.stack([np.ascontiguousarray(r["yT"].T) for r in res.results], axis=0)
    return out.astype(np.float32)


if __name__ == "__main__":
    build_nc()
    print("build ok")


# version 22
# speedup vs baseline: 1.2950x; 1.2950x over previous
import numpy as np
import contextlib
import concourse.bass as bass
import concourse.mybir as mybir
from concourse.bass_utils import run_bass_kernel_spmd

F32 = mybir.dt.float32
BF16 = mybir.dt.bfloat16
AF = mybir.ActivationFunctionType
OP = mybir.AluOpType
AX = mybir.AxisListType

D, L, NN, RD = 2048, 4096, 16, 64
NB = 8
KD = D // 128
T = 512
NM = L // T
NFC = 96
LN_EPS = 1e-5
Q_EPS = 1e-10
WSEG = T + 1
WGRP = 4 * WSEG

ENGS = ("sync", "gpsimd", "tensor", "vector", "scalar")
NM_RUN = NM
ENG_OF = dict(v="vector", a="scalar", t="tensor", g="gpsimd", s="sync")
DMA_SEMS = ("sw", "ws0", "ws1", "qw0", "qw1", "wm0", "wm1", "md",
            "xd", "wq0", "wq1", "od0", "od1")
STEP = dict(v=1, a=1, t=1, g=1, s=1, **{k: 16 for k in DMA_SEMS})
SEM_OF_CNT = dict(v="v", a="a", t="t", **{k: k for k in DMA_SEMS})


def build_nc():
    nc = bass.Bass("TRN2", target_bir_lowering=False)

    xT = nc.dram_tensor("xT", [D, L], F32, kind="ExternalInput").ap()
    wxT = nc.dram_tensor("wxT", [D, 96], F32, kind="ExternalInput").ap()
    wdtT = nc.dram_tensor("wdtT", [RD, D], F32, kind="ExternalInput").ap()
    bdt = nc.dram_tensor("bdt", [128, KD], F32, kind="ExternalInput").ap()
    alog = nc.dram_tensor("alog", [128, KD * NN], F32, kind="ExternalInput").ap()
    dvec = nc.dram_tensor("dvec", [128, KD], F32, kind="ExternalInput").ap()
    fbias = nc.dram_tensor("fbias", [128, KD], F32, kind="ExternalInput").ap()
    wT = nc.dram_tensor("wT", [D, D], F32, kind="ExternalInput").ap()
    cb_in = nc.dram_tensor("cb_in", [128, KD], F32, kind="ExternalInput").ap()
    wmod = nc.dram_tensor("wmod_r", [NFC * 128, KD * 128], F32,
                          kind="ExternalInput").ap()
    onehot_d = nc.dram_tensor("onehot_bc", [96, 32 * 128], BF16,
                              kind="ExternalInput").ap()
    ident_d = nc.dram_tensor("ident_bc", [128, 128], BF16,
                             kind="ExternalInput").ap()
    yT = nc.dram_tensor("yT", [D, L], F32, kind="ExternalOutput").ap()
    wq_h = nc.dram_tensor("wq_scratch", [D, D], BF16)
    wq_d = wq_h.ap()
    modscr_h = nc.dram_tensor("mod_scratch", [6 * D], F32)
    modscr = modscr_h.ap()

    script = {e: [] for e in ENGS}
    cnt = dict(v=0, a=0, t=0, **{k: 0 for k in DMA_SEMS})
    DEFAULT_BUMP = dict(v="v", a="a", t="t", g=None, s=None)

    waited = {k: set() for k in cnt}
    cnt_tok = {}

    def E(eng, fn, waits=(), bump="default", war=False, defer=False):
        if bump == "default":
            bump = DEFAULT_BUMP[eng]
        w = tuple((k, val) for (k, val) in waits if val and val > 0)
        for (k, val) in w:
            waited[k].add(val)
        tok = None
        if bump is not None:
            cnt[bump] += STEP[bump]
            tok = cnt[bump]
        cnt_tok[id(fn)] = tok
        script[ENG_OF[eng]].append((w, fn, bump, war, defer))
        return tok

    def FLUSHV():
        return E("v", lambda h: h.tensor_copy(flsh[:, 128:256], flsh[:, 0:128]))

    def DRAIN(eng):
        E(eng, lambda h: h.drain(), bump=None)

    def FLUSHA():
        return E("a", lambda h: h.activation(flsh[:, 128:256], flsh[:, 0:128],
                                             AF.Copy))

    stack = contextlib.ExitStack()
    with stack:
        sems = {}
        for k in ("v", "a", "t") + DMA_SEMS:
            sems[k] = stack.enter_context(nc.semaphore("sem_" + k))

        def ST(name, shape, dt):
            return stack.enter_context(nc.sbuf_tensor(name, shape, dt))

        def PT(name, shape, dt):
            return stack.enter_context(nc.psum_tensor(name, shape, dt))

        xf = ST("xf", [128, KD * T], F32)
        hb = ST("hb", [128, KD * T], BF16)
        dtb = ST("dtb", [128, KD * T], BF16)
        vb = ST("vb", [128, KD * T], BF16)
        da2 = ST("da2", [128, 2 * WGRP], BF16)
        sb2 = ST("sb2", [128, 2 * WGRP], BF16)
        yac2 = ST("yac2", [128, 2 * T], BF16)
        brepall = ST("brepall", [128, NN * WSEG], BF16)
        crepall = ST("crepall", [128, NN * WSEG], BF16)
        identb = ST("identb", [128, 128], BF16)
        carry = ST("carry", [128, NN * KD], BF16)
        xprj = ST("xprj", [96, T], BF16)
        e1b = ST("e1b", [128, 2 * 1024], BF16)
        outst = ST("outst", [128, 2 * 1024], F32)
        wqb = ST("wqb", [128, 2 * KD * 128], BF16)
        t1f = ST("t1f", [128, T], F32)
        mrow = ST("mrow", [1, T], F32)
        e2row = ST("e2row", [1, T], F32)
        vrow = ST("vrow", [1, T], F32)
        isrow = ST("isrow", [1, T], F32)
        m_bf = ST("m_bf", [1, T], BF16)
        is_bf = ST("is_bf", [1, T], BF16)
        mrep_sb = ST("mrep_sb", [128, T], BF16)
        isrep_sb = ST("isrep_sb", [128, T], BF16)
        wx_sb = ST("wx_sb", [128, KD * 96], BF16)
        wdt_sb = ST("wdt_sb", [RD, D], BF16)
        a_sb = ST("a_sb", [128, KD * NN], F32)
        bdt_sb = ST("bdt_sb", [128, KD], F32)
        dvec_sb = ST("dvec_sb", [128, KD], F32)
        fbias_sb = ST("fbias_sb", [128, KD], F32)
        cbq = ST("cbq", [128, KD], BF16)
        modmat = ST("modmat", [128, 96], F32)
        onepca = ST("onepca", [128, KD], F32)
        ga_cl = ST("ga_cl", [128, KD], F32)
        onepcb = ST("onepcb", [128, KD], F32)
        gb_cl = ST("gb_cl", [128, KD], F32)
        gq = ST("gq", [128, KD], F32)
        gbb = ST("gbb", [128, KD], F32)
        invcol = ST("invcol", [128, 1], F32)
        scol = ST("scol", [128, 1], F32)
        sc_sc = ST("sc_sc", [1, 2], F32)
        invr = ST("invr", [1, 1], F32)
        abscol = ST("abscol", [128, KD], F32)
        abssum = ST("abssum", [128, 1], F32)
        modrow = ST("modrow", [1, 256], F32)
        ones128f = ST("ones128f", [128, 1], F32)
        ones128b = ST("ones128b", [128, 1], BF16)
        ones1f = ST("ones1f", [1, 128], F32)
        ones1b = ST("ones1b", [1, 128], BF16)
        onehotm = ST("onehotm", [96, 32 * 128], BF16)
        qtmp = ST("qtmp", [128, 4 * KD], F32)
        t2b = ST("t2b", [128, T], BF16)
        flsh = ST("flsh", [128, 256], F32)

        ps_s1 = PT("ps_s1", [128, T], F32)
        ps_s2 = PT("ps_s2", [128, T], F32)
        ps_m = PT("ps_m", [128, T], F32)
        ps_i = PT("ps_i", [128, T], F32)
        ps_b0 = PT("ps_b0", [128, T], F32)
        ps_b1 = PT("ps_b1", [128, T], F32)
        ps_f0 = PT("ps_f0", [128, T], F32)
        ps_f1 = PT("ps_f1", [128, T], F32)

        E("v", lambda h: h.memset(ones128f[:, :], 1.0))
        E("v", lambda h: h.memset(ones128b[:, :], 1.0))
        E("v", lambda h: h.memset(ones1f[:, :], 1.0))
        E("v", lambda h: h.memset(ones1b[:, :], 1.0))

        E("v", lambda h: h.memset(carry[:, :], 0.0))
        E("v", lambda h: h.memset(da2[:, :], 0.0))
        E("v", lambda h: h.memset(crepall[:, :], 0.0))

        E("g", lambda h: h.dma_start(
            out=xf[:, 0:KD * 96].rearrange("p (k c) -> p k c", k=KD),
            in_=wxT.rearrange("(k p) c -> p k c", p=128)), bump="sw")
        E("g", lambda h: h.dma_start(
            out=xf[0:RD, 2048:2048 + D], in_=wdtT[:, :]), bump="sw")
        E("g", lambda h: h.dma_start(
            out=xf[:, 4096:4096 + KD * NN], in_=alog[:, :]), bump="sw")
        E("g", lambda h: h.dma_start(out=bdt_sb[:, :], in_=bdt[:, :]), bump="sw")
        E("g", lambda h: h.dma_start(out=dvec_sb[:, :], in_=dvec[:, :]), bump="sw")
        E("g", lambda h: h.dma_start(out=fbias_sb[:, :], in_=fbias[:, :]),
          bump="sw")
        E("g", lambda h: h.dma_start(
            out=xf[:, 4624:4624 + KD], in_=cb_in[:, :]), bump="sw")
        E("g", lambda h: h.dma_start(out=onehotm[:, :], in_=onehot_d[:, :]),
          bump="sw")
        E("g", lambda h: h.dma_start(out=identb[:, :], in_=ident_d[:, :]),
          bump="sw")
        tok_sw = cnt["sw"]

        tok_wx = E("v", lambda h: h.tensor_copy(wx_sb[:, :], xf[:, 0:KD * 96]),
                   waits=[("sw", tok_sw)])
        tok_wdt = E("v", lambda h: h.tensor_copy(wdt_sb[:, :],
                                                 xf[0:RD, 2048:2048 + D]),
                    waits=[("sw", tok_sw)])
        tok_ae = E("a", lambda h: h.activation(xf[:, 4352:4352 + KD * NN],
                                               xf[:, 4096:4096 + KD * NN], AF.Exp),
                   waits=[("sw", tok_sw)])
        tok_a = E("v", lambda h: h.tensor_scalar(
            a_sb[:, :], xf[:, 4352:4352 + KD * NN], -1.0, None, OP.mult),
            waits=[("a", tok_ae)])
        tok_af = FLUSHV()
        tok_cs = E("a", lambda h: h.activation(qtmp[:, 0:KD],
                                               xf[:, 4624:4624 + KD], AF.Silu),
                   waits=[("sw", tok_sw)])
        tok_cb = E("v", lambda h: h.tensor_copy(cbq[:, :], qtmp[:, 0:KD]),
                   waits=[("a", tok_cs)])

        wst = [xf[:, 0:D], xf[:, D:2 * D]]
        tok_v_wst = [tok_wx, tok_wdt]
        tok_wst_d = [0, 0]
        for k in range(KD):
            p = k % 2
            tok_wst_d[p] = E("g", lambda h, k=k, p=p: h.dma_start(
                out=wst[p], in_=wT[k * 128:(k + 1) * 128, :]),
                waits=[("v", tok_v_wst[p])], bump="ws%d" % p)
            tok_v_wst[p] = E("v", lambda h, k=k, p=p: h.reduce_sum(
                abscol[:, k:k + 1], wst[p], axis=AX.X, apply_absolute_value=True),
                waits=[("ws%d" % p, tok_wst_d[p])])
        FLUSHV()
        E("v", lambda h: h.reduce_sum(abssum[:, :], abscol[:, :], axis=AX.X))
        tok_absc = FLUSHV()
        tok_sred = E("t", lambda h: h.matmul(ps_s1[0:1, 0:1], ones128f[:, :],
                                             abssum[:, :], start=True, stop=True),
                     waits=[("v", tok_absc)])
        E("v", lambda h: h.tensor_scalar(sc_sc[0:1, 0:1], ps_s1[0:1, 0:1],
                                         1.0 / (D * D), None, OP.mult),
          waits=[("t", tok_sred)])
        FLUSHV()
        E("v", lambda h: h.tensor_scalar(sc_sc[0:1, 1:2], sc_sc[0:1, 0:1],
                                         Q_EPS, None, OP.add))
        FLUSHV()
        E("v", lambda h: h.reciprocal(invr[0:1, :], sc_sc[0:1, 1:2]))
        tok_inv = FLUSHV()
        tok_bc1 = E("t", lambda h: h.matmul(ps_m[:, 0:1], ones1f[:, :],
                                            invr[0:1, :], start=True, stop=True),
                    waits=[("v", tok_inv)])
        tok_bc2 = E("t", lambda h: h.matmul(ps_i[:, 0:1], ones1f[:, :],
                                            sc_sc[0:1, 0:1], start=True, stop=True),
                    waits=[("v", tok_inv)])
        E("v", lambda h: h.tensor_copy(invcol[:, :], ps_m[:, 0:1]),
          waits=[("t", tok_bc1)])
        E("v", lambda h: h.tensor_copy(scol[:, :], ps_i[:, 0:1]),
          waits=[("t", tok_bc2)])
        tok_invc = FLUSHV()
        tok_scol = tok_invc

        qabs = [xf[:, 4096 + 0:4096 + D], xf[:, 6144:6144 + D]]
        qsgn = [hb[:, 0:D], hb[:, D:2 * D]]
        qmsk = [hb[:, 2 * D:3 * D], hb[:, 3 * D:4 * D]]
        wqt = [dtb[:, 0:D], dtb[:, D:2 * D]]
        tok_wq_wr = [0, 0]
        tok_v_wq = [0, 0]
        tok_a_abs = [0, 0]
        tok_msk_v = [0, 0]
        for k in range(KD):
            p = k % 2
            tok_wst_d[p] = E("g", lambda h, k=k, p=p: h.dma_start(
                out=wst[p], in_=wT[k * 128:(k + 1) * 128, :]),
                waits=[("v", tok_v_wst[p]), ("a", tok_a_abs[p])],
                bump="ws%d" % p)
            E("a", lambda h, p=p: h.activation(qsgn[p], wst[p], AF.Sign),
              waits=[("ws%d" % p, tok_wst_d[p]), ("v", tok_v_wq[p])])
            tok_a_abs[p] = E("a", lambda h, p=p: h.activation(
                qabs[p], wst[p], AF.Abs, scale=invcol[:, 0:1]),
                waits=[("v", tok_invc), ("v", tok_msk_v[p])])
            tok_msk_v[p] = E("v", lambda h, p=p: h.tensor_single_scalar(
                qmsk[p], qabs[p], 0.5, OP.is_ge),
                waits=[("a", tok_a_abs[p])], war=True)
            tok_v_wst[p] = tok_msk_v[p]
            tok_v_wq[p] = E("v", lambda h, p=p: h.tensor_mul(
                wqt[p], qsgn[p], qmsk[p]),
                waits=[("qw%d" % p, tok_wq_wr[p])])
            tok_wq_wr[p] = E("g", lambda h, k=k, p=p: h.dma_start(
                out=wq_d[k * 128:(k + 1) * 128, :], in_=wqt[p]),
                waits=[("v", tok_v_wq[p])], bump="qw%d" % p)
        tok_quant_v = cnt["v"]

        wmf = [xf[:, 0:D], xf[:, D:2 * D]]
        wmb = [dtb[:, 0:D], dtb[:, D:2 * D]]
        tok_wm_d = [0, 0]
        tok_wm_v = [tok_quant_v, tok_quant_v]
        tok_wm_t = [0, 0]
        tok_mod_a = 0
        tok_mr_d = [0, 0]
        for fc in range(NFC):
            p = fc % 2
            tok_wm_d[p] = E("g", lambda h, fc=fc, p=p: h.dma_start(
                out=wmf[p], in_=wmod[fc * 128:(fc + 1) * 128, :]),
                waits=[("v", tok_wm_v[p])], bump="wm%d" % p)
            tok_wm_v[p] = E("v", lambda h, p=p: h.tensor_copy(wmb[p], wmf[p]),
                            waits=[("wm%d" % p, tok_wm_d[p]),
                                   ("t", tok_wm_t[p])])
            tmod = 0
            for k in range(KD):
                w = []
                if k == 0:
                    w = [("v", tok_wm_v[p]), ("v", tok_cb), ("a", tok_mod_a)]
                tmod = E("t", lambda h, p=p, k=k: h.matmul(
                    ps_f0[0:1, 0:128], cbq[:, k:k + 1],
                    wmb[p][:, k * 128:(k + 1) * 128],
                    start=(k == 0), stop=(k == KD - 1)), waits=w)
            tok_wm_t[p] = tmod
            DRAIN("t")
            tok_mod_a = E("a", lambda h, fc=fc, p=p: h.activation(
                modrow[0:1, p * 128:(p + 1) * 128], ps_f0[0:1, 0:128], AF.Copy),
                waits=[("t", tmod), ("md", tok_mr_d[p])])
            tok_mr_d[p] = E("s", lambda h, fc=fc, p=p: h.dma_start(
                out=modscr[fc * 128:(fc + 1) * 128].rearrange("(a b) -> a b", a=1),
                in_=modrow[0:1, p * 128:(p + 1) * 128]),
                waits=[("a", tok_mod_a)], bump="md")
        tok_wm_full = [cnt["wm0"], cnt["wm1"]]

        tok_mm_d = E("s", lambda h: h.dma_start(
            out=modmat[:, :].rearrange("p (v k) -> p v k", v=6),
            in_=bass.AP(modscr_h, 0, [[1, 128], [D, 6], [128, KD]]),
            allow_slow_non_contiguous=True),
            waits=[("md", cnt["md"])], bump="md")

        E("v", lambda h: h.tensor_scalar(onepca[:, :], modmat[:, 16:32], 1.0, None,
                                         OP.add), waits=[("md", tok_mm_d)])
        E("v", lambda h: h.tensor_scalar(onepcb[:, :], modmat[:, 64:80], 1.0, None,
                                         OP.add))

        def emit_clamp(src_off, dst, prev_tok):
            E("v", lambda h: h.tensor_single_scalar(
                qtmp[:, 0:KD], modmat[:, src_off:src_off + KD], 0.0, OP.is_ge),
              waits=[("md", tok_mm_d)])
            FLUSHV()
            E("v", lambda h: h.tensor_scalar(qtmp[:, 2 * KD:3 * KD], qtmp[:, 0:KD],
                                             2.0, -1.0, OP.mult, OP.add))
            ta = E("a", lambda h: h.activation(
                qtmp[:, KD:2 * KD], modmat[:, src_off:src_off + KD], AF.Abs),
                waits=[("md", tok_mm_d), ("v", prev_tok)])
            FLUSHA()
            ta = FLUSHA()
            E("v", lambda h: h.tensor_scalar(qtmp[:, 3 * KD:4 * KD],
                                             qtmp[:, KD:2 * KD],
                                             0.25, None, OP.max),
              waits=[("a", ta)])
            FLUSHV()
            return E("v", lambda h, dst=dst: h.tensor_mul(
                dst[:, :], qtmp[:, 2 * KD:3 * KD], qtmp[:, 3 * KD:4 * KD]))

        tok_ga = emit_clamp(32, ga_cl, 0)
        FLUSHV()
        tok_gb = emit_clamp(80, gb_cl, tok_ga)
        FLUSHV()
        E("v", lambda h: h.tensor_scalar(gq[:, :], gb_cl[:, :], scol[:, 0:1], None,
                                         OP.mult), waits=[("v", tok_scol)])
        FLUSHV()
        E("v", lambda h: h.tensor_mul(gbb[:, :], gb_cl[:, :], fbias_sb[:, :]))
        tok_gbb = FLUSHV()
        tok_pro_v = cnt["v"]
        tok_pro_a = cnt["a"]
        tok_pro_t = cnt["t"]

        tok_xf_free_v = tok_pro_v
        tok_hb_free_t = tok_pro_t
        tok_s1_free_a = tok_pro_a
        tok_s1_free_v = tok_pro_v
        tok_s2_free_v = tok_pro_v
        tok_m_free = ("v", tok_pro_v)
        tok_i_free = ("v", tok_pro_v)
        tok_b0_free = ("v", tok_pro_v)
        tok_b1_free = ("v", tok_pro_v)
        tok_f_free_a = [tok_pro_a, 0]
        last_scan_par = {0: tok_af, 1: tok_af}
        last_idmm_par = {0: tok_pro_t, 1: tok_pro_t}
        idmm_k_tok = {}
        tok_wqb_rd = [0, 0]
        tok_out_dma = [0, 0]
        add_hist = [0] * 8
        xd_grp = [0] * 8
        tok_x = 0

        carry_v = carry[:, :].rearrange("p (n k) -> p n k", n=NN)

        for m in range(NM_RUN):
            c0 = m * T
            if m == 0:
                for k in range(KD):
                    w = []
                    if k == 0:
                        w = [("v", tok_xf_free_v),
                             ("wm0", tok_wm_full[0]), ("wm1", tok_wm_full[1])]
                    E("s", lambda h, k=k, c0=c0: h.dma_start(
                        out=xf[:, k * T:(k + 1) * T],
                        in_=xT[k * 128:(k + 1) * 128, c0:c0 + T]),
                        waits=w, bump="xd")
                tok_x = cnt["xd"]
                ktok = lambda k: [("xd", tok_x)]
            else:
                ktok = lambda k, xg=list(xd_grp): [("xd", xg[k // 2])]

            def emit_ln_finish(onepc, svec_off, tt1, tt2):
                nonlocal tok_s1_free_a, tok_s1_free_v, tok_s2_free_v
                nonlocal tok_m_free, tok_i_free, tok_hb_free_t
                tok_s1_free_v = E("v", lambda h: h.tensor_scalar(
                    mrow[0:1, :], ps_s1[0:1, :], 1.0 / D, None, OP.mult),
                    waits=[("t", tt1)], war=True)
                tok_s2_free_v = E("v", lambda h: h.tensor_scalar(
                    e2row[0:1, :], ps_s2[0:1, :], 1.0 / D, None, OP.mult),
                    waits=[("t", tt2)], war=True)
                E("v", lambda h: h.tensor_mul(vrow[0:1, :], mrow[0:1, :],
                                              mrow[0:1, :]))
                E("v", lambda h: h.tensor_sub(e2row[0:1, :], e2row[0:1, :],
                                              vrow[0:1, :]))
                tv = E("v", lambda h: h.tensor_scalar(vrow[0:1, :], e2row[0:1, :],
                                                      LN_EPS, None, OP.add))
                tln = E("a", lambda h: h.activation(isrow[0:1, :], vrow[0:1, :],
                                                    AF.Ln), waits=[("v", tv)])
                tlh = E("v", lambda h: h.tensor_scalar(
                    vrow[0:1, :], isrow[0:1, :], -0.5, None, OP.mult),
                    waits=[("a", tln)])
                ti = E("a", lambda h: h.activation(isrow[0:1, :], vrow[0:1, :],
                                                   AF.Exp),
                       waits=[("v", tlh)])
                tmb = E("v", lambda h: h.tensor_copy(m_bf[0:1, :], mrow[0:1, :]))
                tib = E("v", lambda h: h.tensor_copy(is_bf[0:1, :], isrow[0:1, :]),
                        waits=[("a", ti)])
                trm = E("t", lambda h: h.matmul(ps_m[:, :], ones1b[:, :],
                                                m_bf[0:1, :], start=True, stop=True),
                        waits=[("v", tmb), tok_m_free])
                tri = E("t", lambda h: h.matmul(ps_i[:, :], ones1b[:, :],
                                                is_bf[0:1, :], start=True, stop=True),
                        waits=[("v", tib), tok_i_free])
                tmr = E("v", lambda h: h.tensor_copy(mrep_sb[:, :], ps_m[:, :]),
                        waits=[("t", trm)], war=True)
                tir = E("v", lambda h: h.tensor_copy(isrep_sb[:, :], ps_i[:, :]),
                        waits=[("t", tri)], war=True)
                tok_m_free = ("v", tmr)
                tok_i_free = ("v", tir)
                tout = 0
                for k in range(KD):
                    E("v", lambda h, k=k: h.tensor_sub(
                        t1f[:, :], xf[:, k * T:(k + 1) * T], mrep_sb[:, :]))
                    E("v", lambda h, k=k, onepc=onepc: h.scalar_tensor_tensor(
                        t2b[:, :], t1f[:, :], onepc[:, k:k + 1],
                        isrep_sb[:, :], OP.mult, OP.mult))
                    tout = E("v", lambda h, k=k, svec_off=svec_off: h.tensor_scalar(
                        hb[:, k * T:(k + 1) * T], t2b[:, :],
                        modmat[:, svec_off + k:svec_off + k + 1], None, OP.add))
                return tout

            ta_last = 0
            for k in range(KD):
                w = ktok(k)
                if k == 0:
                    w = w + [("t", tok_hb_free_t)]
                    if m == 0:
                        w.append(("v", tok_xf_free_v))
                ta_last = E("a", lambda h, k=k: h.activation(
                    hb[:, k * T:(k + 1) * T], xf[:, k * T:(k + 1) * T],
                    AF.Square), waits=w)
            tt1 = 0
            for k in range(KD):
                w = ktok(k)
                if k == 0:
                    w = w + [("v", tok_s1_free_v), ("a", tok_s1_free_a)]
                tt1 = E("t", lambda h, k=k: h.matmul(
                    ps_s1[0:1, :], ones128f[:, :], xf[:, k * T:(k + 1) * T],
                    start=(k == 0), stop=(k == KD - 1)), waits=w)
            tt2 = 0
            for k in range(KD):
                w = [("a", ta_last), ("v", tok_s2_free_v)] if k == 0 else []
                tt2 = E("t", lambda h, k=k: h.matmul(
                    ps_s2[0:1, :], ones128b[:, :], hb[:, k * T:(k + 1) * T],
                    start=(k == 0), stop=(k == KD - 1)), waits=w)
            tok_h = emit_ln_finish(onepca, 0, tt1, tt2)

            txp = 0
            for k in range(KD):
                w = []
                if k == 0:
                    w = [("v", tok_h), ("a", tok_f_free_a[0])]
                    if m == 0:
                        w.append(("v", tok_wx))
                txp = E("t", lambda h, k=k: h.matmul(
                    ps_f0[0:96, :], wx_sb[:, k * 96:(k + 1) * 96],
                    hb[:, k * T:(k + 1) * T],
                    start=(k == 0), stop=(k == KD - 1)), waits=w)
            ta_xp = E("a", lambda h: h.activation(xprj[:, :], ps_f0[0:96, :],
                                                  AF.Copy), waits=[("t", txp)])
            tok_f_free_a[0] = ta_xp
            tok_sp = []
            prev_sp = 0
            for k in range(KD):
                w = [("a", prev_sp)]
                if k == 0:
                    w = [("a", ta_xp), ("v", tok_s1_free_v)]
                    if m == 0:
                        w.append(("v", tok_wdt))
                tdt = E("t", lambda h, k=k: h.matmul(
                    ps_s1[:, :], wdt_sb[:, k * 128:(k + 1) * 128], xprj[0:RD, :],
                    start=True, stop=True), waits=w)
                tex = E("a", lambda h, k=k: h.activation(
                    t1f[:, :], ps_s1[:, :], AF.Exp,
                    bias=bdt_sb[:, k:k + 1]), waits=[("t", tdt)])
                DRAIN("a")
                prev_sp = E("a", lambda h, k=k: h.activation(
                    dtb[:, k * T:(k + 1) * T], t1f[:, :], AF.Ln,
                    bias=ones128f[:, 0:1]))
                tok_sp.append(prev_sp)
            tok_s1_free_a = tok_sp[-1]
            for k in range(KD):
                E("v", lambda h, k=k: h.tensor_mul(
                    vb[:, k * T:(k + 1) * T], dtb[:, k * T:(k + 1) * T],
                    hb[:, k * T:(k + 1) * T]), waits=[("a", tok_sp[k])])

            tok_bc = 0
            for n in range(NN):
                p = n % 2
                psb = ps_b0 if p == 0 else ps_m
                psc = ps_b1 if p == 0 else ps_i
                wb = [("a", ta_xp), (tok_b0_free if p == 0 else tok_m_free)]
                tb = E("t", lambda h, n=n, psb=psb: h.matmul(
                    psb[:, :], onehotm[:, n * 128:(n + 1) * 128], xprj[:, :],
                    start=True, stop=True), waits=wb)
                wc = [(tok_b1_free if p == 0 else tok_i_free)]
                tc = E("t", lambda h, n=n, psc=psc: h.matmul(
                    psc[:, :], onehotm[:, (16 + n) * 128:(17 + n) * 128],
                    xprj[:, :], start=True, stop=True), waits=wc)
                tbr = E("a", lambda h, n=n, psb=psb: h.activation(
                    brepall[:, n * WSEG + 1:n * WSEG + 1 + T], psb[:, :],
                    AF.Copy), waits=[("t", tb)])
                tcr = E("a", lambda h, n=n, psc=psc: h.activation(
                    crepall[:, n * WSEG + 1:n * WSEG + 1 + T], psc[:, :],
                    AF.Copy), waits=[("t", tc)])
                if p == 0:
                    tok_b0_free, tok_b1_free = ("a", tbr), ("a", tcr)
                else:
                    tok_m_free, tok_i_free = ("a", tbr), ("a", tcr)
                tok_bc = tcr

            pend_idmm = None
            pend_epi = None
            ltt1 = ltt2 = 0
            sq_tok = {}

            def emit_idmm(pend, vtok):
                nonlocal tok_f_free_a
                (kk, gg, sbase) = pend
                psf = ps_f0 if kk % 2 == 0 else ps_f1
                tokm = 0
                for j in range(4):
                    w = []
                    if j == 0:
                        w = [("v", vtok)]
                        if gg == 0:
                            w.append(("a", tok_f_free_a[kk % 2]))
                    tokm = E("t", lambda h, j=j, psf=psf, sbase=sbase: h.matmul(
                        psf[:, :], identb[:, :],
                        sb2[:, sbase + j * WSEG + 1:sbase + j * WSEG + 1 + T],
                        start=(gg == 0 and j == 0), stop=(gg == 3 and j == 3)),
                        waits=w)
                last_idmm_par[(kk * 4 + gg) % 2] = tokm
                if gg == 3:
                    idmm_k_tok[kk] = tokm

            def emit_epiV(kk):
                nonlocal tok_f_free_a
                psf = ps_f0 if kk % 2 == 0 else ps_f1
                ys = yac2[:, (kk % 2) * T:(kk % 2 + 1) * T]
                ev = E("a", lambda h, psf=psf, ys=ys: h.activation(
                    ys, psf[:, :], AF.Copy), waits=[("t", idmm_k_tok[kk])])
                tok_f_free_a[kk % 2] = ev
                E("v", lambda h, kk=kk, ys=ys: h.scalar_tensor_tensor(
                    ys, hb[:, kk * T:(kk + 1) * T], dvec_sb[:, kk:kk + 1],
                    ys, OP.mult, OP.add), waits=[("a", ev)])
                wga = [("v", tok_ga)] if (m == 0 and kk == 0) else []
                tga = E("v", lambda h, kk=kk, ys=ys: h.scalar_tensor_tensor(
                    xf[:, kk * T:(kk + 1) * T], ys, ga_cl[:, kk:kk + 1],
                    xf[:, kk * T:(kk + 1) * T], OP.mult, OP.add), waits=wga)
                return tga

            def emit_epiAT(kk, tga, vtok):
                nonlocal ltt1, ltt2, tok_hb_free_t
                x2pub = vtok if vtok else tga
                tsq = E("a", lambda h, kk=kk: h.activation(
                    hb[:, kk * T:(kk + 1) * T], xf[:, kk * T:(kk + 1) * T],
                    AF.Square), waits=[("v", x2pub)])
                w1 = [("v", x2pub)]
                if kk == 0:
                    w1 += [("v", tok_s1_free_v), ("a", tok_s1_free_a)]
                ltt1 = E("t", lambda h, kk=kk: h.matmul(
                    ps_s1[0:1, :], ones128f[:, :], xf[:, kk * T:(kk + 1) * T],
                    start=(kk == 0), stop=(kk == KD - 1)), waits=w1)
                w2 = [("a", tsq)]
                if kk == 0:
                    w2.append(("v", tok_s2_free_v))
                ltt2 = E("t", lambda h, kk=kk: h.matmul(
                    ps_s2[0:1, :], ones128b[:, :], hb[:, kk * T:(kk + 1) * T],
                    start=(kk == 0), stop=(kk == KD - 1)), waits=w2)

            for G in range(64):
                k, g = G // 4, G % 4
                slot = G % 2
                base = slot * WGRP
                n0 = 4 * g
                for j in range(4):
                    n = n0 + j
                    w = [("v", last_scan_par[slot])] if j == 0 else []
                    E("a", lambda h, k=k, n=n, base=base, j=j: h.activation(
                        da2[:, base + j * WSEG + 1:base + j * WSEG + 1 + T],
                        dtb[:, k * T:(k + 1) * T], AF.Exp,
                        scale=a_sb[:, k * NN + n:k * NN + n + 1]),
                        waits=w, defer=True)
                exp_tok = cnt["a"]
                sb2v = sb2[:, base:base + WGRP].rearrange(
                    "p (s w) -> p s w", s=4)
                w = [("t", last_idmm_par[slot])]
                if G == 0:
                    w.append(("a", tok_bc))
                E("v", lambda h, sb2v=sb2v, n0=n0, k=k: h.tensor_copy(
                    sb2v[:, :, 0:1], carry_v[:, n0:n0 + 4, k:k + 1]), waits=w)
                for j in range(4):
                    n = n0 + j
                    E("v", lambda h, k=k, n=n, base=base, j=j: h.tensor_mul(
                        sb2[:, base + j * WSEG + 1:base + j * WSEG + 1 + T],
                        vb[:, k * T:(k + 1) * T],
                        brepall[:, n * WSEG + 1:n * WSEG + 1 + T]))
                tok_scan = E("v", lambda h, base=base: h.tensor_tensor_scan(
                    sb2[:, base:base + WGRP], da2[:, base:base + WGRP],
                    sb2[:, base:base + WGRP], 0.0, OP.mult, OP.add),
                    waits=[("a", exp_tok)], war=True)
                last_scan_par[slot] = tok_scan
                E("v", lambda h, sb2v=sb2v, n0=n0, k=k: h.tensor_copy(
                    carry_v[:, n0:n0 + 4, k:k + 1], sb2v[:, :, T:T + 1]))
                E("v", lambda h, base=base, n0=n0: h.tensor_mul(
                    sb2[:, base:base + WGRP], sb2[:, base:base + WGRP],
                    crepall[:, n0 * WSEG:(n0 + 4) * WSEG]))
                if pend_idmm is not None:
                    emit_idmm(pend_idmm, tok_scan)
                    pend_idmm = None
                if pend_epi is not None:
                    emit_epiAT(pend_epi[0], pend_epi[1], tok_scan)
                    pend_epi = None
                pend_idmm = (k, g, base)
                if G % 4 == 2 and G >= 6:
                    kk = G // 4 - 1
                    tga = emit_epiV(kk)
                    pend_epi = (kk, tga)

            t_tail = FLUSHV()
            emit_idmm(pend_idmm, t_tail)
            pend_idmm = None
            if pend_epi is not None:
                emit_epiAT(pend_epi[0], pend_epi[1], t_tail)
                pend_epi = None
            tga15 = emit_epiV(KD - 1)
            emit_epiAT(KD - 1, tga15, 0)

            tok_h2 = emit_ln_finish(onepcb, 48, ltt1, ltt2)
            tok_hb_free_t = txp

            for gj in range(8):
                pg = gj % 2
                te1_last = 0
                for jj in range(2):
                    j = 2 * gj + jj
                    p = j % 2
                    psf = ps_f0 if p == 0 else ps_f1
                    col0 = p * KD * 128
                    for q in range(4):
                        w = []
                        if q == 0:
                            w = [("t", tok_wqb_rd[p])]
                            if m == 0 and j < 2:
                                w += [("qw0", cnt["qw0"]), ("qw1", cnt["qw1"])]
                        E("g", lambda h, j=j, q=q, col0=col0: h.dma_start(
                            out=wqb[:, col0 + q * 512:col0 + (q + 1) * 512],
                            in_=bass.AP(wq_h, (q * 4 * 128) * D + j * 128,
                                        [[D, 128], [128 * D, 4], [1, 128]])),
                            waits=w, bump="wq%d" % p)
                    tok_wq_dma = cnt["wq%d" % p]
                    tmmf = 0
                    for k in range(KD):
                        w = []
                        if k == 0:
                            w = [("wq%d" % p, tok_wq_dma), ("v", tok_h2),
                                 ("a", tok_f_free_a[p])]
                        tmmf = E("t", lambda h, k=k, psf=psf, col0=col0: h.matmul(
                            psf[:, :], wqb[:, col0 + k * 128:col0 + (k + 1) * 128],
                            hb[:, k * T:(k + 1) * T],
                            start=(k == 0), stop=(k == KD - 1)), waits=w)
                    tok_wqb_rd[p] = tmmf
                    we = [("t", tmmf)]
                    if jj == 0 and add_hist[(gj - 2) % 8] and gj >= 2:
                        we.append(("v", add_hist[gj - 2]))
                    if m == 0 and j == 0:
                        we.append(("v", tok_gbb))
                    dst = e1b[:, pg * 1024 + jj * T:pg * 1024 + (jj + 1) * T]
                    te1_last = E("a", lambda h, j=j, psf=psf, dst=dst: h.activation(
                        dst, psf[:, :], AF.Identity,
                        bias=gbb[:, j:j + 1], scale=gq[:, j:j + 1]), waits=we)
                    tok_f_free_a[p] = te1_last
                tadd = E("v", lambda h, gj=gj, pg=pg: h.tensor_add(
                    outst[:, pg * 1024:(pg + 1) * 1024],
                    xf[:, gj * 1024:(gj + 1) * 1024],
                    e1b[:, pg * 1024:(pg + 1) * 1024]),
                    waits=[("a", te1_last), ("od%d" % pg, tok_out_dma[pg])])
                add_hist[gj] = tadd
                E("s", lambda h, gj=gj, pg=pg, c0=c0: h.dma_start(
                    out=yT[gj * 256:(gj + 1) * 256, c0:c0 + T].rearrange(
                        "(a p) c -> p a c", p=128),
                    in_=outst[:, pg * 1024:(pg + 1) * 1024].rearrange(
                        "p (a c) -> p a c", a=2)),
                    waits=[("v", tadd)], bump="od%d" % pg)
                tok_out_dma[pg] = cnt["od%d" % pg]
                if m < NM_RUN - 1:
                    c0n = (m + 1) * T
                    E("s", lambda h, gj=gj, c0n=c0n: h.dma_start(
                        out=xf[:, gj * 1024:(gj + 1) * 1024].rearrange(
                            "p (a c) -> p a c", a=2),
                        in_=xT[gj * 256:(gj + 1) * 256, c0n:c0n + T].rearrange(
                            "(a p) c -> p a c", p=128)),
                        waits=[("v", tadd)], bump="xd")
                    xd_grp[gj] = cnt["xd"]
            tok_hb_free_t = max(tok_wqb_rd)

        E("s", lambda h: None,
          waits=[("od0", cnt["od0"]), ("od1", cnt["od1"])], bump=None)

        def runner(tasks, eng_name):
            compute = eng_name in ("vector", "scalar", "tensor")

            def f(h):
                pending = None
                for (waits, fn, bump, war, defer) in tasks:
                    if pending is not None and waits:
                        h.drain().then_inc(*pending)
                        pending = None
                    for (sk, val) in waits:
                        h.wait_ge(sems[SEM_OF_CNT[sk]], val)
                    if pending is not None and bump is not None and not defer:
                        h.drain().then_inc(*pending)
                        pending = None
                    instr = fn(h)
                    if instr is None:
                        continue
                    if pending is not None:
                        instr.then_inc(*pending)
                        pending = None
                    if bump is None:
                        continue
                    if defer:
                        pending = (sems[bump], STEP[bump])
                        continue
                    needs_drain = (compute and not war and
                                   cnt_tok[id(fn)] in waited[bump])
                    if needs_drain:
                        h.drain().then_inc(sems[bump], STEP[bump])
                    else:
                        instr.then_inc(sems[bump], STEP[bump])
                if pending is not None:
                    h.drain().then_inc(*pending)
            return f

        with nc.Block() as block:
            block.sync(runner(script["sync"], "sync"))
            block.gpsimd(runner(script["gpsimd"], "gpsimd"))
            block.tensor(runner(script["tensor"], "tensor"))
            block.vector(runner(script["vector"], "vector"))
            block.scalar(runner(script["scalar"], "scalar"))

    return nc


def kernel(L0_attn_x, L0_attn_W_x, L0_attn_W_dt, L0_attn_b_dt, L0_attn_A_log,
           L0_attn_D, L0_ffn_weight, L0_ffn_bias, L0_c, L0_wmod):
    nc = build_nc()

    wxT = np.ascontiguousarray(L0_attn_W_x.T)
    wdtT = np.ascontiguousarray(L0_attn_W_dt.T)
    bdt_a = np.ascontiguousarray(L0_attn_b_dt.reshape(KD, 128).T)
    alog_a = np.ascontiguousarray(
        L0_attn_A_log.reshape(KD, 128, NN).transpose(1, 0, 2).reshape(128, KD * NN))
    dvec_a = np.ascontiguousarray(L0_attn_D.reshape(KD, 128).T)
    fbias_a = np.ascontiguousarray(L0_ffn_bias.reshape(KD, 128).T)
    wT_a = np.ascontiguousarray(L0_ffn_weight.T)
    wmod_a = np.ascontiguousarray(
        L0_wmod.reshape(KD, 128, NFC, 128).transpose(2, 1, 0, 3)
        .reshape(NFC * 128, KD * 128))

    import ml_dtypes
    onehot = np.zeros((96, 32 * 128), dtype=ml_dtypes.bfloat16)
    ident = np.eye(128, dtype=ml_dtypes.bfloat16)
    for j in range(32):
        onehot[64 + j, j * 128:(j + 1) * 128] = 1.0

    in_maps = []
    for b in range(NB):
        in_maps.append({
            "xT": np.ascontiguousarray(L0_attn_x[b].T),
            "wxT": wxT, "wdtT": wdtT, "bdt": bdt_a, "alog": alog_a,
            "dvec": dvec_a, "fbias": fbias_a, "wT": wT_a,
            "cb_in": np.ascontiguousarray(L0_c[b].reshape(KD, 128).T),
            "wmod_r": wmod_a, "onehot_bc": onehot, "ident_bc": ident,
        })
    res = run_bass_kernel_spmd(nc, in_maps, core_ids=list(range(NB)))
    out = np.stack([np.ascontiguousarray(r["yT"].T) for r in res.results], axis=0)
    return out.astype(np.float32)


if __name__ == "__main__":
    build_nc()
    print("build ok")


# version 23
# speedup vs baseline: 1.4357x; 1.1086x over previous
import numpy as np
import contextlib
import concourse.bass as bass
import concourse.mybir as mybir
from concourse.bass_utils import run_bass_kernel_spmd

F32 = mybir.dt.float32
BF16 = mybir.dt.bfloat16
AF = mybir.ActivationFunctionType
OP = mybir.AluOpType
AX = mybir.AxisListType

D, L, NN, RD = 2048, 4096, 16, 64
NB = 8
KD = D // 128
T = 512
NM = L // T
NFC = 96
LN_EPS = 1e-5
Q_EPS = 1e-10
WSEG = T + 1
WGRP = 4 * WSEG

ENGS = ("sync", "gpsimd", "tensor", "vector", "scalar")
NM_RUN = NM
ENG_OF = dict(v="vector", a="scalar", t="tensor", g="gpsimd", s="sync")
DMA_SEMS = ("sw", "ws0", "ws1", "qw0", "qw1", "wm0", "wm1", "md",
            "xd", "xg", "wq0", "wq1", "od0", "od1")
STEP = dict(v=1, a=1, t=1, g=1, s=1, **{k: 16 for k in DMA_SEMS})
SEM_OF_CNT = dict(v="v", a="a", t="t", **{k: k for k in DMA_SEMS})


def build_nc():
    nc = bass.Bass("TRN2", target_bir_lowering=False)

    xT = nc.dram_tensor("xT", [D, L], F32, kind="ExternalInput").ap()
    wxT = nc.dram_tensor("wxT", [D, 96], F32, kind="ExternalInput").ap()
    wdtT = nc.dram_tensor("wdtT", [RD, D], F32, kind="ExternalInput").ap()
    bdt = nc.dram_tensor("bdt", [128, KD], F32, kind="ExternalInput").ap()
    alog = nc.dram_tensor("alog", [128, KD * NN], F32, kind="ExternalInput").ap()
    dvec = nc.dram_tensor("dvec", [128, KD], F32, kind="ExternalInput").ap()
    fbias = nc.dram_tensor("fbias", [128, KD], F32, kind="ExternalInput").ap()
    wT = nc.dram_tensor("wT", [D, D], F32, kind="ExternalInput").ap()
    cb_in = nc.dram_tensor("cb_in", [128, KD], F32, kind="ExternalInput").ap()
    wmod = nc.dram_tensor("wmod_r", [NFC * 128, KD * 128], F32,
                          kind="ExternalInput").ap()
    onehot_d = nc.dram_tensor("onehot_bc", [96, 32 * 128], BF16,
                              kind="ExternalInput").ap()
    ident_d = nc.dram_tensor("ident_bc", [128, 128], BF16,
                             kind="ExternalInput").ap()
    yT = nc.dram_tensor("yT", [D, L], F32, kind="ExternalOutput").ap()
    wq_h = nc.dram_tensor("wq_scratch", [D, D], BF16)
    wq_d = wq_h.ap()
    modscr_h = nc.dram_tensor("mod_scratch", [6 * D], F32)
    modscr = modscr_h.ap()

    script = {e: [] for e in ENGS}
    cnt = dict(v=0, a=0, t=0, **{k: 0 for k in DMA_SEMS})
    DEFAULT_BUMP = dict(v="v", a="a", t="t", g=None, s=None)

    waited = {k: set() for k in cnt}
    cnt_tok = {}

    def E(eng, fn, waits=(), bump="default", war=False, defer=False):
        if bump == "default":
            bump = DEFAULT_BUMP[eng]
        w = tuple((k, val) for (k, val) in waits if val and val > 0)
        for (k, val) in w:
            waited[k].add(val)
        tok = None
        if bump is not None:
            cnt[bump] += STEP[bump]
            tok = cnt[bump]
        cnt_tok[id(fn)] = tok
        script[ENG_OF[eng]].append((w, fn, bump, war, defer))
        return tok

    def FLUSHV():
        return E("v", lambda h: h.tensor_copy(flsh[:, 128:256], flsh[:, 0:128]))

    def DRAIN(eng):
        E(eng, lambda h: h.drain(), bump=None)

    def FLUSHA():
        return E("a", lambda h: h.activation(flsh[:, 128:256], flsh[:, 0:128],
                                             AF.Copy))

    stack = contextlib.ExitStack()
    with stack:
        sems = {}
        for k in ("v", "a", "t") + DMA_SEMS:
            sems[k] = stack.enter_context(nc.semaphore("sem_" + k))

        def ST(name, shape, dt):
            return stack.enter_context(nc.sbuf_tensor(name, shape, dt))

        def PT(name, shape, dt):
            return stack.enter_context(nc.psum_tensor(name, shape, dt))

        xf = ST("xf", [128, KD * T], F32)
        hb = ST("hb", [128, KD * T], BF16)
        dtb = ST("dtb", [128, KD * T], BF16)
        vb = ST("vb", [128, KD * T], BF16)
        da2 = ST("da2", [128, 2 * WGRP], BF16)
        sb2 = ST("sb2", [128, 4 * WGRP], BF16)
        yac2 = ST("yac2", [128, 2 * T], BF16)
        brepall = ST("brepall", [128, NN * WSEG], BF16)
        crepall = ST("crepall", [128, NN * WSEG], BF16)
        identb = ST("identb", [128, 128], BF16)
        carry = ST("carry", [128, NN * KD], BF16)
        xprj = ST("xprj", [96, T], BF16)
        e1b = ST("e1b", [128, 2 * 1024], BF16)
        outst = ST("outst", [128, 2 * 1024], F32)
        wqb = ST("wqb", [128, 2 * KD * 128], BF16)
        t1f = ST("t1f", [128, T], F32)
        mrow = ST("mrow", [1, T], F32)
        e2row = ST("e2row", [1, T], F32)
        vrow = ST("vrow", [1, T], F32)
        isrow = ST("isrow", [1, T], F32)
        m_bf = ST("m_bf", [1, T], BF16)
        is_bf = ST("is_bf", [1, T], BF16)
        mrep_sb = ST("mrep_sb", [128, T], BF16)
        isrep_sb = ST("isrep_sb", [128, T], BF16)
        wx_sb = ST("wx_sb", [128, KD * 96], BF16)
        wdt_sb = ST("wdt_sb", [RD, D], BF16)
        a_sb = ST("a_sb", [128, KD * NN], F32)
        bdt_sb = ST("bdt_sb", [128, KD], F32)
        dvec_sb = ST("dvec_sb", [128, KD], F32)
        fbias_sb = ST("fbias_sb", [128, KD], F32)
        cbq = ST("cbq", [128, KD], BF16)
        modmat = ST("modmat", [128, 96], F32)
        onepca = ST("onepca", [128, KD], F32)
        ga_cl = ST("ga_cl", [128, KD], F32)
        onepcb = ST("onepcb", [128, KD], F32)
        gb_cl = ST("gb_cl", [128, KD], F32)
        gq = ST("gq", [128, KD], F32)
        gbb = ST("gbb", [128, KD], F32)
        invcol = ST("invcol", [128, 1], F32)
        scol = ST("scol", [128, 1], F32)
        sc_sc = ST("sc_sc", [1, 2], F32)
        invr = ST("invr", [1, 1], F32)
        abscol = ST("abscol", [128, KD], F32)
        abssum = ST("abssum", [128, 1], F32)
        modrow = ST("modrow", [1, 12 * 128], F32)
        ones128f = ST("ones128f", [128, 1], F32)
        ones128b = ST("ones128b", [128, 1], BF16)
        ones1f = ST("ones1f", [1, 128], F32)
        ones1b = ST("ones1b", [1, 128], BF16)
        onehotm = ST("onehotm", [96, 32 * 128], BF16)
        qtmp = ST("qtmp", [128, 4 * KD], F32)
        t2b = ST("t2b", [128, T], BF16)
        flsh = ST("flsh", [128, 256], F32)

        ps_s1 = PT("ps_s1", [128, T], F32)
        ps_s2 = PT("ps_s2", [128, T], F32)
        ps_m = PT("ps_m", [128, T], F32)
        ps_i = PT("ps_i", [128, T], F32)
        ps_b0 = PT("ps_b0", [128, T], F32)
        ps_b1 = PT("ps_b1", [128, T], F32)
        ps_f0 = PT("ps_f0", [128, T], F32)
        ps_f1 = PT("ps_f1", [128, T], F32)

        E("v", lambda h: h.memset(ones128f[:, :], 1.0))
        E("v", lambda h: h.memset(ones128b[:, :], 1.0))
        E("v", lambda h: h.memset(ones1f[:, :], 1.0))
        E("v", lambda h: h.memset(ones1b[:, :], 1.0))

        E("v", lambda h: h.memset(carry[:, :], 0.0))
        E("v", lambda h: h.memset(da2[:, :], 0.0))
        E("v", lambda h: h.memset(crepall[:, :], 0.0))

        E("g", lambda h: h.dma_start(
            out=xf[:, 0:KD * 96].rearrange("p (k c) -> p k c", k=KD),
            in_=wxT.rearrange("(k p) c -> p k c", p=128)), bump="sw")
        E("g", lambda h: h.dma_start(
            out=xf[0:RD, 2048:2048 + D], in_=wdtT[:, :]), bump="sw")
        E("g", lambda h: h.dma_start(
            out=xf[:, 4096:4096 + KD * NN], in_=alog[:, :]), bump="sw")
        E("g", lambda h: h.dma_start(out=bdt_sb[:, :], in_=bdt[:, :]), bump="sw")
        E("g", lambda h: h.dma_start(out=dvec_sb[:, :], in_=dvec[:, :]), bump="sw")
        E("g", lambda h: h.dma_start(out=fbias_sb[:, :], in_=fbias[:, :]),
          bump="sw")
        E("g", lambda h: h.dma_start(
            out=xf[:, 4624:4624 + KD], in_=cb_in[:, :]), bump="sw")
        E("g", lambda h: h.dma_start(out=onehotm[:, :], in_=onehot_d[:, :]),
          bump="sw")
        E("g", lambda h: h.dma_start(out=identb[:, :], in_=ident_d[:, :]),
          bump="sw")
        tok_sw = cnt["sw"]

        tok_wx = E("v", lambda h: h.tensor_copy(wx_sb[:, :], xf[:, 0:KD * 96]),
                   waits=[("sw", tok_sw)])
        tok_wdt = E("v", lambda h: h.tensor_copy(wdt_sb[:, :],
                                                 xf[0:RD, 2048:2048 + D]),
                    waits=[("sw", tok_sw)])
        tok_ae = E("a", lambda h: h.activation(xf[:, 4352:4352 + KD * NN],
                                               xf[:, 4096:4096 + KD * NN], AF.Exp),
                   waits=[("sw", tok_sw)])
        tok_a = E("v", lambda h: h.tensor_scalar(
            a_sb[:, :], xf[:, 4352:4352 + KD * NN], -1.0, None, OP.mult),
            waits=[("a", tok_ae)])
        tok_af = FLUSHV()
        tok_cs = E("a", lambda h: h.activation(qtmp[:, 0:KD],
                                               xf[:, 4624:4624 + KD], AF.Silu),
                   waits=[("sw", tok_sw)])
        tok_cb = E("v", lambda h: h.tensor_copy(cbq[:, :], qtmp[:, 0:KD]),
                   waits=[("a", tok_cs)])

        wst = [xf[:, 0:D], xf[:, D:2 * D]]
        tok_v_wst = [tok_wx, tok_wdt]
        tok_wst_d = [0, 0]
        for k in range(KD):
            p = k % 2
            tok_wst_d[p] = E("g" if p == 0 else "s", lambda h, k=k, p=p: h.dma_start(
                out=wst[p], in_=wT[k * 128:(k + 1) * 128, :]),
                waits=[("v", tok_v_wst[p])], bump="ws%d" % p)
            tok_v_wst[p] = E("v", lambda h, k=k, p=p: h.reduce_sum(
                abscol[:, k:k + 1], wst[p], axis=AX.X, apply_absolute_value=True),
                waits=[("ws%d" % p, tok_wst_d[p])])
        FLUSHV()
        E("v", lambda h: h.reduce_sum(abssum[:, :], abscol[:, :], axis=AX.X))
        tok_absc = FLUSHV()
        tok_sred = E("t", lambda h: h.matmul(ps_s1[0:1, 0:1], ones128f[:, :],
                                             abssum[:, :], start=True, stop=True),
                     waits=[("v", tok_absc)])
        E("v", lambda h: h.tensor_scalar(sc_sc[0:1, 0:1], ps_s1[0:1, 0:1],
                                         1.0 / (D * D), None, OP.mult),
          waits=[("t", tok_sred)])
        FLUSHV()
        E("v", lambda h: h.tensor_scalar(sc_sc[0:1, 1:2], sc_sc[0:1, 0:1],
                                         Q_EPS, None, OP.add))
        FLUSHV()
        E("v", lambda h: h.reciprocal(invr[0:1, :], sc_sc[0:1, 1:2]))
        tok_inv = FLUSHV()
        tok_bc1 = E("t", lambda h: h.matmul(ps_m[:, 0:1], ones1f[:, :],
                                            invr[0:1, :], start=True, stop=True),
                    waits=[("v", tok_inv)])
        tok_bc2 = E("t", lambda h: h.matmul(ps_i[:, 0:1], ones1f[:, :],
                                            sc_sc[0:1, 0:1], start=True, stop=True),
                    waits=[("v", tok_inv)])
        E("v", lambda h: h.tensor_copy(invcol[:, :], ps_m[:, 0:1]),
          waits=[("t", tok_bc1)])
        E("v", lambda h: h.tensor_copy(scol[:, :], ps_i[:, 0:1]),
          waits=[("t", tok_bc2)])
        tok_invc = FLUSHV()
        tok_scol = tok_invc

        qabs = [xf[:, 4096 + 0:4096 + D], xf[:, 6144:6144 + D]]
        qsgn = [hb[:, 0:D], hb[:, D:2 * D]]
        qmsk = [hb[:, 2 * D:3 * D], hb[:, 3 * D:4 * D]]
        wqt = [dtb[:, 0:D], dtb[:, D:2 * D]]
        tok_wq_wr = [0, 0]
        tok_v_wq = [0, 0]
        tok_a_abs = [0, 0]
        tok_msk_v = [0, 0]
        for k in range(KD):
            p = k % 2
            tok_wst_d[p] = E("g" if p == 0 else "s", lambda h, k=k, p=p: h.dma_start(
                out=wst[p], in_=wT[k * 128:(k + 1) * 128, :]),
                waits=[("v", tok_v_wst[p]), ("a", tok_a_abs[p])],
                bump="ws%d" % p)
            E("a", lambda h, p=p: h.activation(qsgn[p], wst[p], AF.Sign),
              waits=[("ws%d" % p, tok_wst_d[p]), ("v", tok_v_wq[p])])
            tok_a_abs[p] = E("a", lambda h, p=p: h.activation(
                qabs[p], wst[p], AF.Abs, scale=invcol[:, 0:1]),
                waits=[("v", tok_invc), ("v", tok_msk_v[p])])
            tok_msk_v[p] = E("v", lambda h, p=p: h.tensor_single_scalar(
                qmsk[p], qabs[p], 0.5, OP.is_ge),
                waits=[("a", tok_a_abs[p])], war=True)
            tok_v_wst[p] = tok_msk_v[p]
            tok_v_wq[p] = E("v", lambda h, p=p: h.tensor_mul(
                wqt[p], qsgn[p], qmsk[p]),
                waits=[("qw%d" % p, tok_wq_wr[p])])
            tok_wq_wr[p] = E("g", lambda h, k=k, p=p: h.dma_start(
                out=bass.AP(wq_h, k * 128,
                            [[KD * 128, 128], [128 * KD * 128, KD], [1, 128]]),
                in_=wqt[p].rearrange("p (j c) -> p j c", j=KD)),
                waits=[("v", tok_v_wq[p])], bump="qw%d" % p)
        tok_quant_v = cnt["v"]

        wmf = [xf[:, 0:D], xf[:, D:2 * D]]
        tok_wm_d = [0, 0]
        tok_wm_t = [0, 0]
        tok_mod_a = 0
        md_last = 0
        for fc in range(NFC):
            p = fc % 2
            s12 = fc % 12
            w = [("t", tok_wm_t[p])]
            if fc < 2:
                w.append(("v", tok_quant_v))
            tok_wm_d[p] = E("g" if p == 0 else "s", lambda h, fc=fc, p=p: h.dma_start(
                out=wmf[p], in_=wmod[fc * 128:(fc + 1) * 128, :]),
                waits=w, bump="wm%d" % p)
            tmod = 0
            for k in range(KD):
                w = []
                if k == 0:
                    w = [("wm%d" % p, tok_wm_d[p]), ("a", tok_cs),
                         ("a", tok_mod_a)]
                tmod = E("t", lambda h, p=p, k=k: h.matmul(
                    ps_f0[0:1, 0:128], qtmp[:, k:k + 1],
                    wmf[p][:, k * 128:(k + 1) * 128],
                    start=(k == 0), stop=(k == KD - 1)), waits=w)
            tok_wm_t[p] = tmod
            DRAIN("t")
            tok_mod_a = E("a", lambda h, s12=s12: h.activation(
                modrow[0:1, s12 * 128:(s12 + 1) * 128], ps_f0[0:1, 0:128],
                AF.Copy), waits=[("t", tmod), ("md", md_last)])
            if s12 == 11:
                b = fc // 12
                E("s", lambda h, b=b: h.dma_start(
                    out=modscr[b * 1536:(b + 1) * 1536].rearrange(
                        "(a b) -> a b", a=1),
                    in_=modrow[0:1, :]),
                    waits=[("a", tok_mod_a)], bump="md")
                md_last = cnt["md"]
        tok_wm_full = [cnt["wm0"], cnt["wm1"]]

        tok_mm_d = E("s", lambda h: h.dma_start(
            out=modmat[:, :].rearrange("p (v k) -> p v k", v=6),
            in_=bass.AP(modscr_h, 0, [[1, 128], [D, 6], [128, KD]]),
            allow_slow_non_contiguous=True),
            waits=[("md", cnt["md"])], bump="md")

        E("v", lambda h: h.tensor_scalar(onepca[:, :], modmat[:, 16:32], 1.0, None,
                                         OP.add), waits=[("md", tok_mm_d)])
        E("v", lambda h: h.tensor_scalar(onepcb[:, :], modmat[:, 64:80], 1.0, None,
                                         OP.add))

        def emit_clamp(src_off, dst, prev_tok):
            E("v", lambda h: h.tensor_single_scalar(
                qtmp[:, 0:KD], modmat[:, src_off:src_off + KD], 0.0, OP.is_ge),
              waits=[("md", tok_mm_d)])
            FLUSHV()
            E("v", lambda h: h.tensor_scalar(qtmp[:, 2 * KD:3 * KD], qtmp[:, 0:KD],
                                             2.0, -1.0, OP.mult, OP.add))
            ta = E("a", lambda h: h.activation(
                qtmp[:, KD:2 * KD], modmat[:, src_off:src_off + KD], AF.Abs),
                waits=[("md", tok_mm_d), ("v", prev_tok)])
            FLUSHA()
            ta = FLUSHA()
            E("v", lambda h: h.tensor_scalar(qtmp[:, 3 * KD:4 * KD],
                                             qtmp[:, KD:2 * KD],
                                             0.25, None, OP.max),
              waits=[("a", ta)])
            FLUSHV()
            return E("v", lambda h, dst=dst: h.tensor_mul(
                dst[:, :], qtmp[:, 2 * KD:3 * KD], qtmp[:, 3 * KD:4 * KD]))

        tok_ga = emit_clamp(32, ga_cl, 0)
        FLUSHV()
        tok_gb = emit_clamp(80, gb_cl, tok_ga)
        FLUSHV()
        E("v", lambda h: h.tensor_scalar(gq[:, :], gb_cl[:, :], scol[:, 0:1], None,
                                         OP.mult), waits=[("v", tok_scol)])
        FLUSHV()
        E("v", lambda h: h.tensor_mul(gbb[:, :], gb_cl[:, :], fbias_sb[:, :]))
        tok_gbb = FLUSHV()
        tok_pro_v = cnt["v"]
        tok_pro_a = cnt["a"]
        tok_pro_t = cnt["t"]

        tok_xf_free_v = tok_pro_v
        tok_hb_free_t = tok_pro_t
        tok_s1_free_a = tok_pro_a
        tok_s1_free_v = tok_pro_v
        tok_s2_free_v = tok_pro_v
        tok_m_free = ("v", tok_pro_v)
        tok_i_free = ("v", tok_pro_v)
        tok_b0_free = ("v", tok_pro_v)
        tok_b1_free = ("v", tok_pro_v)
        tok_f_free_a = [tok_pro_a, tok_pro_a]
        last_scan_par = {0: tok_af, 1: tok_af}
        last_idmm_par = {i: tok_pro_t for i in range(4)}
        idmm_k_tok = {}
        tok_wqb_rd = [0, 0]
        tok_out_dma = [0, 0]
        add_hist = [0] * 8
        xd_grp = [0] * 8
        tok_x = 0

        carry_v = carry[:, :].rearrange("p (n k) -> p n k", n=NN)

        for m in range(NM_RUN):
            c0 = m * T
            if m == 0:
                for k in range(KD):
                    w = []
                    if k == 0:
                        w = [("v", tok_xf_free_v),
                             ("t", tok_wm_t[0]), ("t", tok_wm_t[1]),
                             ("wm0", tok_wm_full[0]), ("wm1", tok_wm_full[1])]
                    E("s", lambda h, k=k, c0=c0: h.dma_start(
                        out=xf[:, k * T:(k + 1) * T],
                        in_=xT[k * 128:(k + 1) * 128, c0:c0 + T]),
                        waits=w, bump="xd")
                tok_x = cnt["xd"]
                ktok = lambda k: [("xd", tok_x)]
            else:
                ktok = lambda k, xd=cnt["xd"]: (
                    [("xd", xd)] if k == 0 else [])

            def emit_ln_finish(onepc, svec_off, tt1, tt2):
                nonlocal tok_s1_free_a, tok_s1_free_v, tok_s2_free_v
                nonlocal tok_m_free, tok_i_free, tok_hb_free_t
                tok_s1_free_v = E("v", lambda h: h.tensor_scalar(
                    mrow[0:1, :], ps_s1[0:1, :], 1.0 / D, None, OP.mult),
                    waits=[("t", tt1)], war=True)
                tok_s2_free_v = E("v", lambda h: h.tensor_scalar(
                    e2row[0:1, :], ps_s2[0:1, :], 1.0 / D, None, OP.mult),
                    waits=[("t", tt2)], war=True)
                E("v", lambda h: h.tensor_mul(vrow[0:1, :], mrow[0:1, :],
                                              mrow[0:1, :]))
                E("v", lambda h: h.tensor_sub(e2row[0:1, :], e2row[0:1, :],
                                              vrow[0:1, :]))
                tv = E("v", lambda h: h.tensor_scalar(vrow[0:1, :], e2row[0:1, :],
                                                      LN_EPS, None, OP.add))
                tln = E("a", lambda h: h.activation(isrow[0:1, :], vrow[0:1, :],
                                                    AF.Ln), waits=[("v", tv)])
                tlh = E("v", lambda h: h.tensor_scalar(
                    vrow[0:1, :], isrow[0:1, :], -0.5, None, OP.mult),
                    waits=[("a", tln)])
                ti = E("a", lambda h: h.activation(isrow[0:1, :], vrow[0:1, :],
                                                   AF.Exp),
                       waits=[("v", tlh)])
                tmb = E("v", lambda h: h.tensor_copy(m_bf[0:1, :], mrow[0:1, :]))
                tib = E("v", lambda h: h.tensor_copy(is_bf[0:1, :], isrow[0:1, :]),
                        waits=[("a", ti)])
                trm = E("t", lambda h: h.matmul(ps_m[:, :], ones1b[:, :],
                                                m_bf[0:1, :], start=True, stop=True),
                        waits=[("v", tmb), tok_m_free])
                tri = E("t", lambda h: h.matmul(ps_i[:, :], ones1b[:, :],
                                                is_bf[0:1, :], start=True, stop=True),
                        waits=[("v", tib), tok_i_free])
                tmr = E("v", lambda h: h.tensor_copy(mrep_sb[:, :], ps_m[:, :]),
                        waits=[("t", trm)], war=True)
                tir = E("v", lambda h: h.tensor_copy(isrep_sb[:, :], ps_i[:, :]),
                        waits=[("t", tri)], war=True)
                tok_m_free = ("v", tmr)
                tok_i_free = ("v", tir)
                tout = 0
                for k in range(KD):
                    E("v", lambda h, k=k: h.tensor_sub(
                        t1f[:, :], xf[:, k * T:(k + 1) * T], mrep_sb[:, :]))
                    E("v", lambda h, k=k, onepc=onepc: h.scalar_tensor_tensor(
                        t2b[:, :], t1f[:, :], onepc[:, k:k + 1],
                        isrep_sb[:, :], OP.mult, OP.mult))
                    tout = E("v", lambda h, k=k, svec_off=svec_off: h.tensor_scalar(
                        hb[:, k * T:(k + 1) * T], t2b[:, :],
                        modmat[:, svec_off + k:svec_off + k + 1], None, OP.add))
                return tout

            ta_last = 0
            for k in range(KD):
                w = ktok(k)
                if k == 0:
                    w = w + [("t", tok_hb_free_t)]
                    if m == 0:
                        w.append(("v", tok_xf_free_v))
                ta_last = E("a", lambda h, k=k: h.activation(
                    hb[:, k * T:(k + 1) * T], xf[:, k * T:(k + 1) * T],
                    AF.Square), waits=w)
            tt1 = 0
            for k in range(KD):
                w = ktok(k)
                if k == 0:
                    w = w + [("v", tok_s1_free_v), ("a", tok_s1_free_a)]
                tt1 = E("t", lambda h, k=k: h.matmul(
                    ps_s1[0:1, :], ones128f[:, :], xf[:, k * T:(k + 1) * T],
                    start=(k == 0), stop=(k == KD - 1)), waits=w)
            tt2 = 0
            for k in range(KD):
                w = [("a", ta_last), ("v", tok_s2_free_v)] if k == 0 else []
                tt2 = E("t", lambda h, k=k: h.matmul(
                    ps_s2[0:1, :], ones128b[:, :], hb[:, k * T:(k + 1) * T],
                    start=(k == 0), stop=(k == KD - 1)), waits=w)
            tok_h = emit_ln_finish(onepca, 0, tt1, tt2)

            txp = 0
            for k in range(KD):
                w = []
                if k == 0:
                    w = [("v", tok_h), ("a", tok_f_free_a[0])]
                    if m == 0:
                        w.append(("v", tok_wx))
                txp = E("t", lambda h, k=k: h.matmul(
                    ps_f0[0:96, :], wx_sb[:, k * 96:(k + 1) * 96],
                    hb[:, k * T:(k + 1) * T],
                    start=(k == 0), stop=(k == KD - 1)), waits=w)
            ta_xp = E("a", lambda h: h.activation(xprj[:, :], ps_f0[0:96, :],
                                                  AF.Copy), waits=[("t", txp)])
            tok_f_free_a[0] = ta_xp
            tok_sp = []
            prev_sp = 0
            for k in range(KD):
                w = [("a", prev_sp)]
                if k == 0:
                    w = [("a", ta_xp), ("v", tok_s1_free_v)]
                    if m == 0:
                        w.append(("v", tok_wdt))
                tdt = E("t", lambda h, k=k: h.matmul(
                    ps_s1[:, :], wdt_sb[:, k * 128:(k + 1) * 128], xprj[0:RD, :],
                    start=True, stop=True), waits=w)
                tex = E("a", lambda h, k=k: h.activation(
                    t1f[:, :], ps_s1[:, :], AF.Exp,
                    bias=bdt_sb[:, k:k + 1]), waits=[("t", tdt)])
                DRAIN("a")
                prev_sp = E("a", lambda h, k=k: h.activation(
                    dtb[:, k * T:(k + 1) * T], t1f[:, :], AF.Ln,
                    bias=ones128f[:, 0:1]))
                tok_sp.append(prev_sp)
            tok_s1_free_a = tok_sp[-1]
            for k in range(KD):
                E("v", lambda h, k=k: h.tensor_mul(
                    vb[:, k * T:(k + 1) * T], dtb[:, k * T:(k + 1) * T],
                    hb[:, k * T:(k + 1) * T]), waits=[("a", tok_sp[k])])

            tok_bc = 0
            for n in range(NN):
                p = n % 2
                psb = ps_b0 if p == 0 else ps_m
                psc = ps_b1 if p == 0 else ps_i
                wb = [("a", ta_xp), (tok_b0_free if p == 0 else tok_m_free)]
                tb = E("t", lambda h, n=n, psb=psb: h.matmul(
                    psb[:, :], onehotm[:, n * 128:(n + 1) * 128], xprj[:, :],
                    start=True, stop=True), waits=wb)
                wc = [(tok_b1_free if p == 0 else tok_i_free)]
                tc = E("t", lambda h, n=n, psc=psc: h.matmul(
                    psc[:, :], onehotm[:, (16 + n) * 128:(17 + n) * 128],
                    xprj[:, :], start=True, stop=True), waits=wc)
                tbr = E("a", lambda h, n=n, psb=psb: h.activation(
                    brepall[:, n * WSEG + 1:n * WSEG + 1 + T], psb[:, :],
                    AF.Copy), waits=[("t", tb)])
                tcr = E("a", lambda h, n=n, psc=psc: h.activation(
                    crepall[:, n * WSEG + 1:n * WSEG + 1 + T], psc[:, :],
                    AF.Copy), waits=[("t", tc)])
                if p == 0:
                    tok_b0_free, tok_b1_free = ("a", tbr), ("a", tcr)
                else:
                    tok_m_free, tok_i_free = ("a", tbr), ("a", tcr)
                tok_bc = tcr

            pend_idmm = None
            pend_epi = None
            pend_sq = None
            ltt1 = ltt2 = 0

            def emit_idmm(pend, vtok):
                nonlocal tok_f_free_a
                (kk, gg, sbase) = pend
                psf = ps_f0 if kk % 2 == 0 else ps_f1
                tokm = 0
                for j in range(4):
                    w = []
                    if j == 0:
                        w = [("v", vtok)]
                        if gg == 0:
                            w.append(("a", tok_f_free_a[kk % 2]))
                    tokm = E("t", lambda h, j=j, psf=psf, sbase=sbase: h.matmul(
                        psf[:, :], identb[:, :],
                        sb2[:, sbase + j * WSEG + 1:sbase + j * WSEG + 1 + T],
                        start=(gg == 0 and j == 0), stop=(gg == 3 and j == 3)),
                        waits=w)
                last_idmm_par[(kk * 4 + gg) % 4] = tokm
                if gg == 3:
                    idmm_k_tok[kk] = tokm

            def emit_epiV(kk):
                nonlocal tok_f_free_a
                psf = ps_f0 if kk % 2 == 0 else ps_f1
                ys = yac2[:, (kk % 2) * T:(kk % 2 + 1) * T]
                ev = E("a", lambda h, psf=psf, ys=ys: h.activation(
                    ys, psf[:, :], AF.Copy), waits=[("t", idmm_k_tok[kk])])
                tok_f_free_a[kk % 2] = ev
                E("v", lambda h, kk=kk, ys=ys: h.scalar_tensor_tensor(
                    ys, hb[:, kk * T:(kk + 1) * T], dvec_sb[:, kk:kk + 1],
                    ys, OP.mult, OP.add), waits=[("a", ev)])
                wga = [("v", tok_ga)] if (m == 0 and kk == 0) else []
                tga = E("v", lambda h, kk=kk, ys=ys: h.scalar_tensor_tensor(
                    xf[:, kk * T:(kk + 1) * T], ys, ga_cl[:, kk:kk + 1],
                    xf[:, kk * T:(kk + 1) * T], OP.mult, OP.add), waits=wga)
                return tga

            def emit_epiT1(kk, x2pub):
                nonlocal ltt1
                w1 = [("v", x2pub)]
                if kk == 0:
                    w1 += [("v", tok_s1_free_v), ("a", tok_s1_free_a)]
                ltt1 = E("t", lambda h, kk=kk: h.matmul(
                    ps_s1[0:1, :], ones128f[:, :], xf[:, kk * T:(kk + 1) * T],
                    start=(kk == 0), stop=(kk == KD - 1)), waits=w1)

            def emit_epiSq(kk, x2pub):
                nonlocal ltt2
                tsq = E("a", lambda h, kk=kk: h.activation(
                    hb[:, kk * T:(kk + 1) * T], xf[:, kk * T:(kk + 1) * T],
                    AF.Square), waits=[("v", x2pub)])
                w2 = [("a", tsq)]
                if kk == 0:
                    w2.append(("v", tok_s2_free_v))
                ltt2 = E("t", lambda h, kk=kk: h.matmul(
                    ps_s2[0:1, :], ones128b[:, :], hb[:, kk * T:(kk + 1) * T],
                    start=(kk == 0), stop=(kk == KD - 1)), waits=w2)

            for G in range(64):
                k, g = G // 4, G % 4
                slot = G % 2
                slot4 = G % 4
                base = slot4 * WGRP
                dbase = slot * WGRP
                n0 = 4 * g
                for j in range(4):
                    n = n0 + j
                    w = [("v", last_scan_par[slot])] if j == 0 else []
                    E("a", lambda h, k=k, n=n, dbase=dbase, j=j: h.activation(
                        da2[:, dbase + j * WSEG + 1:dbase + j * WSEG + 1 + T],
                        dtb[:, k * T:(k + 1) * T], AF.Exp,
                        scale=a_sb[:, k * NN + n:k * NN + n + 1]),
                        waits=w, defer=True)
                exp_tok = cnt["a"]
                sb2v = sb2[:, base:base + WGRP].rearrange(
                    "p (s w) -> p s w", s=4)
                w = [("t", last_idmm_par[slot4])]
                if G == 0:
                    w.append(("a", tok_bc))
                E("v", lambda h, sb2v=sb2v, n0=n0, k=k: h.tensor_copy(
                    sb2v[:, :, 0:1], carry_v[:, n0:n0 + 4, k:k + 1]), waits=w)
                for j in range(4):
                    n = n0 + j
                    E("v", lambda h, k=k, n=n, base=base, j=j: h.tensor_mul(
                        sb2[:, base + j * WSEG + 1:base + j * WSEG + 1 + T],
                        vb[:, k * T:(k + 1) * T],
                        brepall[:, n * WSEG + 1:n * WSEG + 1 + T]))
                tok_scan = E("v", lambda h, base=base, dbase=dbase:
                             h.tensor_tensor_scan(
                                 sb2[:, base:base + WGRP],
                                 da2[:, dbase:dbase + WGRP],
                                 sb2[:, base:base + WGRP], 0.0,
                                 OP.mult, OP.add),
                             waits=[("a", exp_tok)], war=True)
                last_scan_par[slot] = tok_scan
                E("v", lambda h, sb2v=sb2v, n0=n0, k=k: h.tensor_copy(
                    carry_v[:, n0:n0 + 4, k:k + 1], sb2v[:, :, T:T + 1]))
                E("v", lambda h, base=base, n0=n0: h.tensor_mul(
                    sb2[:, base:base + WGRP], sb2[:, base:base + WGRP],
                    crepall[:, n0 * WSEG:(n0 + 4) * WSEG]))
                if pend_idmm is not None:
                    emit_idmm(pend_idmm, tok_scan)
                    pend_idmm = None
                if pend_sq is not None:
                    emit_epiSq(*pend_sq)
                    pend_sq = None
                if pend_epi is not None:
                    emit_epiT1(pend_epi[0], tok_scan)
                    pend_sq = (pend_epi[0], tok_scan)
                    pend_epi = None
                pend_idmm = (k, g, base)
                if G % 4 == 2 and G >= 6:
                    kk = G // 4 - 1
                    tga = emit_epiV(kk)
                    pend_epi = (kk, tga)

            t_tail = FLUSHV()
            emit_idmm(pend_idmm, t_tail)
            pend_idmm = None
            if pend_sq is not None:
                emit_epiSq(*pend_sq)
                pend_sq = None
            if pend_epi is not None:
                emit_epiT1(pend_epi[0], t_tail)
                emit_epiSq(pend_epi[0], t_tail)
                pend_epi = None
            tga15 = emit_epiV(KD - 1)
            emit_epiT1(KD - 1, tga15)
            emit_epiSq(KD - 1, tga15)

            tok_h2 = emit_ln_finish(onepcb, 48, ltt1, ltt2)
            tok_hb_free_t = txp

            for gj in range(8):
                pg = gj % 2
                te1_last = 0
                for jj in range(2):
                    j = 2 * gj + jj
                    p = j % 2
                    psf = ps_f0 if p == 0 else ps_f1
                    col0 = p * KD * 128
                    w = [("t", tok_wqb_rd[p])]
                    if m == 0:
                        w += [("qw0", cnt["qw0"]), ("qw1", cnt["qw1"])]
                    E("g", lambda h, j=j, col0=col0: h.dma_start(
                        out=wqb[:, col0:col0 + KD * 128],
                        in_=bass.AP(wq_h, j * 128 * KD * 128,
                                    [[KD * 128, 128], [1, KD * 128]])),
                        waits=w, bump="wq%d" % p)
                    tok_wq_dma = cnt["wq%d" % p]
                    tmmf = 0
                    for k in range(KD):
                        w = []
                        if k == 0:
                            w = [("wq%d" % p, tok_wq_dma), ("v", tok_h2),
                                 ("a", tok_f_free_a[p])]
                        tmmf = E("t", lambda h, k=k, psf=psf, col0=col0: h.matmul(
                            psf[:, :], wqb[:, col0 + k * 128:col0 + (k + 1) * 128],
                            hb[:, k * T:(k + 1) * T],
                            start=(k == 0), stop=(k == KD - 1)), waits=w)
                    tok_wqb_rd[p] = tmmf
                    we = [("t", tmmf)]
                    if jj == 0 and add_hist[(gj - 2) % 8] and gj >= 2:
                        we.append(("v", add_hist[gj - 2]))
                    if m == 0 and j == 0:
                        we.append(("v", tok_gbb))
                    dst = e1b[:, pg * 1024 + jj * T:pg * 1024 + (jj + 1) * T]
                    te1_last = E("a", lambda h, j=j, psf=psf, dst=dst: h.activation(
                        dst, psf[:, :], AF.Identity,
                        bias=gbb[:, j:j + 1], scale=gq[:, j:j + 1]), waits=we)
                    tok_f_free_a[p] = te1_last
                tadd = E("v", lambda h, gj=gj, pg=pg: h.tensor_add(
                    outst[:, pg * 1024:(pg + 1) * 1024],
                    xf[:, gj * 1024:(gj + 1) * 1024],
                    e1b[:, pg * 1024:(pg + 1) * 1024]),
                    waits=[("a", te1_last), ("od%d" % pg, tok_out_dma[pg])])
                add_hist[gj] = tadd
                E("s", lambda h, gj=gj, pg=pg, c0=c0: h.dma_start(
                    out=yT[gj * 256:(gj + 1) * 256, c0:c0 + T].rearrange(
                        "(a p) c -> p a c", p=128),
                    in_=outst[:, pg * 1024:(pg + 1) * 1024].rearrange(
                        "p (a c) -> p a c", a=2)),
                    waits=[("v", tadd)], bump="od%d" % pg)
                tok_out_dma[pg] = cnt["od%d" % pg]
                if m < NM_RUN - 1:
                    c0n = (m + 1) * T
                    E("s", lambda h, gj=gj, c0n=c0n: h.dma_start(
                        out=xf[:, gj * 1024:(gj + 1) * 1024].rearrange(
                            "p (a c) -> p a c", a=2),
                        in_=xT[gj * 256:(gj + 1) * 256, c0n:c0n + T].rearrange(
                            "(a p) c -> p a c", p=128)),
                        waits=[("v", tadd)], bump="xd")
            tok_hb_free_t = max(tok_wqb_rd)

        E("s", lambda h: None,
          waits=[("od0", cnt["od0"]), ("od1", cnt["od1"])], bump=None)

        def runner(tasks, eng_name):
            compute = eng_name in ("vector", "scalar", "tensor")

            def f(h):
                pending = None
                for (waits, fn, bump, war, defer) in tasks:
                    if pending is not None and waits:
                        h.drain().then_inc(*pending)
                        pending = None
                    for (sk, val) in waits:
                        h.wait_ge(sems[SEM_OF_CNT[sk]], val)
                    if pending is not None and bump is not None and not defer:
                        h.drain().then_inc(*pending)
                        pending = None
                    instr = fn(h)
                    if instr is None:
                        continue
                    if pending is not None:
                        instr.then_inc(*pending)
                        pending = None
                    if bump is None:
                        continue
                    if defer:
                        pending = (sems[bump], STEP[bump])
                        continue
                    needs_drain = (compute and not war and
                                   cnt_tok[id(fn)] in waited[bump])
                    if needs_drain:
                        h.drain().then_inc(sems[bump], STEP[bump])
                    else:
                        instr.then_inc(sems[bump], STEP[bump])
                if pending is not None:
                    h.drain().then_inc(*pending)
            return f

        with nc.Block() as block:
            block.sync(runner(script["sync"], "sync"))
            block.gpsimd(runner(script["gpsimd"], "gpsimd"))
            block.tensor(runner(script["tensor"], "tensor"))
            block.vector(runner(script["vector"], "vector"))
            block.scalar(runner(script["scalar"], "scalar"))

    return nc


def kernel(L0_attn_x, L0_attn_W_x, L0_attn_W_dt, L0_attn_b_dt, L0_attn_A_log,
           L0_attn_D, L0_ffn_weight, L0_ffn_bias, L0_c, L0_wmod):
    nc = build_nc()

    wxT = np.ascontiguousarray(L0_attn_W_x.T)
    wdtT = np.ascontiguousarray(L0_attn_W_dt.T)
    bdt_a = np.ascontiguousarray(L0_attn_b_dt.reshape(KD, 128).T)
    alog_a = np.ascontiguousarray(
        L0_attn_A_log.reshape(KD, 128, NN).transpose(1, 0, 2).reshape(128, KD * NN))
    dvec_a = np.ascontiguousarray(L0_attn_D.reshape(KD, 128).T)
    fbias_a = np.ascontiguousarray(L0_ffn_bias.reshape(KD, 128).T)
    wT_a = np.ascontiguousarray(L0_ffn_weight.T)
    wmod_a = np.ascontiguousarray(
        L0_wmod.reshape(KD, 128, NFC, 128).transpose(2, 1, 0, 3)
        .reshape(NFC * 128, KD * 128))

    import ml_dtypes
    onehot = np.zeros((96, 32 * 128), dtype=ml_dtypes.bfloat16)
    ident = np.eye(128, dtype=ml_dtypes.bfloat16)
    for j in range(32):
        onehot[64 + j, j * 128:(j + 1) * 128] = 1.0

    in_maps = []
    for b in range(NB):
        in_maps.append({
            "xT": np.ascontiguousarray(L0_attn_x[b].T),
            "wxT": wxT, "wdtT": wdtT, "bdt": bdt_a, "alog": alog_a,
            "dvec": dvec_a, "fbias": fbias_a, "wT": wT_a,
            "cb_in": np.ascontiguousarray(L0_c[b].reshape(KD, 128).T),
            "wmod_r": wmod_a, "onehot_bc": onehot, "ident_bc": ident,
        })
    res = run_bass_kernel_spmd(nc, in_maps, core_ids=list(range(NB)))
    out = np.stack([np.ascontiguousarray(r["yT"].T) for r in res.results], axis=0)
    return out.astype(np.float32)


if __name__ == "__main__":
    build_nc()
    print("build ok")


# version 24
# speedup vs baseline: 1.4464x; 1.0074x over previous
import numpy as np
import contextlib
import concourse.bass as bass
import concourse.mybir as mybir
from concourse.bass_utils import run_bass_kernel_spmd

F32 = mybir.dt.float32
BF16 = mybir.dt.bfloat16
AF = mybir.ActivationFunctionType
OP = mybir.AluOpType
AX = mybir.AxisListType

D, L, NN, RD = 2048, 4096, 16, 64
NB = 8
KD = D // 128
T = 512
NM = L // T
NFC = 96
LN_EPS = 1e-5
Q_EPS = 1e-10
WSEG = T + 1
WGRP = 4 * WSEG

ENGS = ("sync", "gpsimd", "tensor", "vector", "scalar")
NM_RUN = NM
ENG_OF = dict(v="vector", a="scalar", t="tensor", g="gpsimd", s="sync")
DMA_SEMS = ("sw", "ws0", "ws1", "qw0", "qw1", "wm0", "wm1", "md",
            "xd", "xg", "wq0", "wq1", "od0", "od1")
STEP = dict(v=1, a=1, t=1, g=1, s=1, **{k: 16 for k in DMA_SEMS})
SEM_OF_CNT = dict(v="v", a="a", t="t", **{k: k for k in DMA_SEMS})


def build_nc():
    nc = bass.Bass("TRN2", target_bir_lowering=False)

    xT = nc.dram_tensor("xT", [D, L], F32, kind="ExternalInput").ap()
    wxT = nc.dram_tensor("wxT", [D, 96], F32, kind="ExternalInput").ap()
    wdtT = nc.dram_tensor("wdtT", [RD, D], F32, kind="ExternalInput").ap()
    bdt = nc.dram_tensor("bdt", [128, KD], F32, kind="ExternalInput").ap()
    alog = nc.dram_tensor("alog", [128, KD * NN], F32, kind="ExternalInput").ap()
    dvec = nc.dram_tensor("dvec", [128, KD], F32, kind="ExternalInput").ap()
    fbias = nc.dram_tensor("fbias", [128, KD], F32, kind="ExternalInput").ap()
    wT = nc.dram_tensor("wT", [D, D], F32, kind="ExternalInput").ap()
    cb_in = nc.dram_tensor("cb_in", [128, KD], F32, kind="ExternalInput").ap()
    wmod = nc.dram_tensor("wmod_r", [NFC * 128, KD * 128], F32,
                          kind="ExternalInput").ap()
    onehot_d = nc.dram_tensor("onehot_bc", [96, 32 * 128], BF16,
                              kind="ExternalInput").ap()
    ident_d = nc.dram_tensor("ident_bc", [128, 128], BF16,
                             kind="ExternalInput").ap()
    yT = nc.dram_tensor("yT", [D, L], F32, kind="ExternalOutput").ap()
    wq_h = nc.dram_tensor("wq_scratch", [D, D], BF16)
    wq_d = wq_h.ap()
    modscr_h = nc.dram_tensor("mod_scratch", [6 * D], F32)
    modscr = modscr_h.ap()

    script = {e: [] for e in ENGS}
    cnt = dict(v=0, a=0, t=0, **{k: 0 for k in DMA_SEMS})
    DEFAULT_BUMP = dict(v="v", a="a", t="t", g=None, s=None)

    waited = {k: set() for k in cnt}
    cnt_tok = {}

    def E(eng, fn, waits=(), bump="default", war=False, defer=False):
        if bump == "default":
            bump = DEFAULT_BUMP[eng]
        w = tuple((k, val) for (k, val) in waits if val and val > 0)
        for (k, val) in w:
            waited[k].add(val)
        tok = None
        if bump is not None:
            cnt[bump] += STEP[bump]
            tok = cnt[bump]
        cnt_tok[id(fn)] = tok
        script[ENG_OF[eng]].append((w, fn, bump, war, defer))
        return tok

    def FLUSHV():
        return E("v", lambda h: h.tensor_copy(flsh[:, 128:256], flsh[:, 0:128]))

    def DRAIN(eng):
        E(eng, lambda h: h.drain(), bump=None)

    def FLUSHA():
        return E("a", lambda h: h.activation(flsh[:, 128:256], flsh[:, 0:128],
                                             AF.Copy))

    stack = contextlib.ExitStack()
    with stack:
        sems = {}
        for k in ("v", "a", "t") + DMA_SEMS:
            sems[k] = stack.enter_context(nc.semaphore("sem_" + k))

        def ST(name, shape, dt):
            return stack.enter_context(nc.sbuf_tensor(name, shape, dt))

        def PT(name, shape, dt):
            return stack.enter_context(nc.psum_tensor(name, shape, dt))

        xf = ST("xf", [128, KD * T], F32)
        hb = ST("hb", [128, KD * T], BF16)
        dtb = ST("dtb", [128, KD * T], BF16)
        vb = ST("vb", [128, KD * T], BF16)
        da2 = ST("da2", [128, 2 * WGRP], BF16)
        sb2 = ST("sb2", [128, 4 * WGRP], BF16)
        yac2 = ST("yac2", [128, 2 * T], BF16)
        brepall = ST("brepall", [128, NN * WSEG], BF16)
        crepall = ST("crepall", [128, NN * WSEG], BF16)
        identb = ST("identb", [128, 128], BF16)
        carry = ST("carry", [128, NN * KD], BF16)
        xprj = ST("xprj", [96, T], BF16)
        e1b = ST("e1b", [128, 2 * 1024], BF16)
        outst = ST("outst", [128, 2 * 1024], F32)
        wqb = ST("wqb", [128, 2 * KD * 128], BF16)
        t1f = ST("t1f", [128, T], F32)
        mrow = ST("mrow", [1, T], F32)
        e2row = ST("e2row", [1, T], F32)
        vrow = ST("vrow", [1, T], F32)
        isrow = ST("isrow", [1, T], F32)
        m_bf = ST("m_bf", [1, T], BF16)
        is_bf = ST("is_bf", [1, T], BF16)
        mrep_sb = ST("mrep_sb", [128, T], BF16)
        isrep_sb = ST("isrep_sb", [128, T], BF16)
        wx_sb = ST("wx_sb", [128, KD * 96], BF16)
        wdt_sb = ST("wdt_sb", [RD, D], BF16)
        a_sb = ST("a_sb", [128, KD * NN], F32)
        bdt_sb = ST("bdt_sb", [128, KD], F32)
        dvec_sb = ST("dvec_sb", [128, KD], F32)
        fbias_sb = ST("fbias_sb", [128, KD], F32)
        cbq = ST("cbq", [128, KD], BF16)
        modmat = ST("modmat", [128, 96], F32)
        onepca = ST("onepca", [128, KD], F32)
        ga_cl = ST("ga_cl", [128, KD], F32)
        onepcb = ST("onepcb", [128, KD], F32)
        gb_cl = ST("gb_cl", [128, KD], F32)
        gq = ST("gq", [128, KD], F32)
        gbb = ST("gbb", [128, KD], F32)
        invcol = ST("invcol", [128, 1], F32)
        scol = ST("scol", [128, 1], F32)
        sc_sc = ST("sc_sc", [1, 2], F32)
        invr = ST("invr", [1, 1], F32)
        abscol = ST("abscol", [128, KD], F32)
        abssum = ST("abssum", [128, 1], F32)
        modrow = ST("modrow", [1, 12 * 128], F32)
        ones128f = ST("ones128f", [128, 1], F32)
        ones128b = ST("ones128b", [128, 1], BF16)
        ones1f = ST("ones1f", [1, 128], F32)
        ones1b = ST("ones1b", [1, 128], BF16)
        onehotm = ST("onehotm", [96, 32 * 128], BF16)
        qtmp = ST("qtmp", [128, 4 * KD], F32)
        t2b = ST("t2b", [128, T], BF16)
        flsh = ST("flsh", [128, 256], F32)

        ps_s1 = PT("ps_s1", [128, T], F32)
        ps_s2 = PT("ps_s2", [128, T], F32)
        ps_m = PT("ps_m", [128, T], F32)
        ps_i = PT("ps_i", [128, T], F32)
        ps_b0 = PT("ps_b0", [128, T], F32)
        ps_b1 = PT("ps_b1", [128, T], F32)
        ps_f0 = PT("ps_f0", [128, T], F32)
        ps_f1 = PT("ps_f1", [128, T], F32)

        E("v", lambda h: h.memset(ones128f[:, :], 1.0))
        E("v", lambda h: h.memset(ones128b[:, :], 1.0))
        E("v", lambda h: h.memset(ones1f[:, :], 1.0))
        E("v", lambda h: h.memset(ones1b[:, :], 1.0))

        E("v", lambda h: h.memset(carry[:, :], 0.0))
        E("v", lambda h: h.memset(da2[:, :], 0.0))
        E("v", lambda h: h.memset(crepall[:, :], 0.0))

        E("g", lambda h: h.dma_start(
            out=xf[:, 0:KD * 96].rearrange("p (k c) -> p k c", k=KD),
            in_=wxT.rearrange("(k p) c -> p k c", p=128)), bump="sw")
        E("g", lambda h: h.dma_start(
            out=xf[0:RD, 2048:2048 + D], in_=wdtT[:, :]), bump="sw")
        E("g", lambda h: h.dma_start(
            out=xf[:, 4096:4096 + KD * NN], in_=alog[:, :]), bump="sw")
        E("g", lambda h: h.dma_start(out=bdt_sb[:, :], in_=bdt[:, :]), bump="sw")
        E("g", lambda h: h.dma_start(out=dvec_sb[:, :], in_=dvec[:, :]), bump="sw")
        E("g", lambda h: h.dma_start(out=fbias_sb[:, :], in_=fbias[:, :]),
          bump="sw")
        E("g", lambda h: h.dma_start(
            out=xf[:, 4624:4624 + KD], in_=cb_in[:, :]), bump="sw")
        E("g", lambda h: h.dma_start(out=onehotm[:, :], in_=onehot_d[:, :]),
          bump="sw")
        E("g", lambda h: h.dma_start(out=identb[:, :], in_=ident_d[:, :]),
          bump="sw")
        tok_sw = cnt["sw"]

        tok_wx = E("v", lambda h: h.tensor_copy(wx_sb[:, :], xf[:, 0:KD * 96]),
                   waits=[("sw", tok_sw)])
        tok_wdt = E("v", lambda h: h.tensor_copy(wdt_sb[:, :],
                                                 xf[0:RD, 2048:2048 + D]),
                    waits=[("sw", tok_sw)])
        tok_ae = E("a", lambda h: h.activation(xf[:, 4352:4352 + KD * NN],
                                               xf[:, 4096:4096 + KD * NN], AF.Exp),
                   waits=[("sw", tok_sw)])
        tok_a = E("v", lambda h: h.tensor_scalar(
            a_sb[:, :], xf[:, 4352:4352 + KD * NN], -1.0, None, OP.mult),
            waits=[("a", tok_ae)])
        tok_af = FLUSHV()
        tok_cs = E("a", lambda h: h.activation(qtmp[:, 0:KD],
                                               xf[:, 4624:4624 + KD], AF.Silu),
                   waits=[("sw", tok_sw)])
        tok_cb = E("v", lambda h: h.tensor_copy(cbq[:, :], qtmp[:, 0:KD]),
                   waits=[("a", tok_cs)])

        wst = [xf[:, 0:D], xf[:, D:2 * D]]
        tok_v_wst = [tok_wx, tok_wdt]
        tok_wst_d = [0, 0]
        for k in range(KD):
            p = k % 2
            tok_wst_d[p] = E("g" if p == 0 else "s", lambda h, k=k, p=p: h.dma_start(
                out=wst[p], in_=wT[k * 128:(k + 1) * 128, :]),
                waits=[("v", tok_v_wst[p])], bump="ws%d" % p)
            tok_v_wst[p] = E("v", lambda h, k=k, p=p: h.reduce_sum(
                abscol[:, k:k + 1], wst[p], axis=AX.X, apply_absolute_value=True),
                waits=[("ws%d" % p, tok_wst_d[p])])
        FLUSHV()
        E("v", lambda h: h.reduce_sum(abssum[:, :], abscol[:, :], axis=AX.X))
        tok_absc = FLUSHV()
        tok_sred = E("t", lambda h: h.matmul(ps_s1[0:1, 0:1], ones128f[:, :],
                                             abssum[:, :], start=True, stop=True),
                     waits=[("v", tok_absc)])
        E("v", lambda h: h.tensor_scalar(sc_sc[0:1, 0:1], ps_s1[0:1, 0:1],
                                         1.0 / (D * D), None, OP.mult),
          waits=[("t", tok_sred)])
        FLUSHV()
        E("v", lambda h: h.tensor_scalar(sc_sc[0:1, 1:2], sc_sc[0:1, 0:1],
                                         Q_EPS, None, OP.add))
        FLUSHV()
        E("v", lambda h: h.reciprocal(invr[0:1, :], sc_sc[0:1, 1:2]))
        tok_inv = FLUSHV()
        tok_bc1 = E("t", lambda h: h.matmul(ps_m[:, 0:1], ones1f[:, :],
                                            invr[0:1, :], start=True, stop=True),
                    waits=[("v", tok_inv)])
        tok_bc2 = E("t", lambda h: h.matmul(ps_i[:, 0:1], ones1f[:, :],
                                            sc_sc[0:1, 0:1], start=True, stop=True),
                    waits=[("v", tok_inv)])
        E("v", lambda h: h.tensor_copy(invcol[:, :], ps_m[:, 0:1]),
          waits=[("t", tok_bc1)])
        E("v", lambda h: h.tensor_copy(scol[:, :], ps_i[:, 0:1]),
          waits=[("t", tok_bc2)])
        tok_invc = FLUSHV()
        tok_scol = tok_invc

        qabs = [xf[:, 4096 + 0:4096 + D], xf[:, 6144:6144 + D]]
        qsgn = [hb[:, 0:D], hb[:, D:2 * D]]
        qmsk = [hb[:, 2 * D:3 * D], hb[:, 3 * D:4 * D]]
        wqt = [dtb[:, 0:D], dtb[:, D:2 * D]]
        tok_wq_wr = [0, 0]
        tok_v_wq = [0, 0]
        tok_a_abs = [0, 0]
        tok_msk_v = [0, 0]
        for k in range(KD):
            p = k % 2
            tok_wst_d[p] = E("g" if p == 0 else "s", lambda h, k=k, p=p: h.dma_start(
                out=wst[p], in_=wT[k * 128:(k + 1) * 128, :]),
                waits=[("v", tok_v_wst[p]), ("a", tok_a_abs[p])],
                bump="ws%d" % p)
            E("a", lambda h, p=p: h.activation(qsgn[p], wst[p], AF.Sign),
              waits=[("ws%d" % p, tok_wst_d[p]), ("v", tok_v_wq[p])])
            tok_a_abs[p] = E("a", lambda h, p=p: h.activation(
                qabs[p], wst[p], AF.Abs, scale=invcol[:, 0:1]),
                waits=[("v", tok_invc), ("v", tok_msk_v[p])])
            tok_msk_v[p] = E("v", lambda h, p=p: h.tensor_single_scalar(
                qmsk[p], qabs[p], 0.5, OP.is_ge),
                waits=[("a", tok_a_abs[p])], war=True)
            tok_v_wst[p] = tok_msk_v[p]
            tok_v_wq[p] = E("v", lambda h, p=p: h.tensor_mul(
                wqt[p], qsgn[p], qmsk[p]),
                waits=[("qw%d" % p, tok_wq_wr[p])])
            tok_wq_wr[p] = E("g", lambda h, k=k, p=p: h.dma_start(
                out=bass.AP(wq_h, k * 128,
                            [[KD * 128, 128], [128 * KD * 128, KD], [1, 128]]),
                in_=wqt[p].rearrange("p (j c) -> p j c", j=KD)),
                waits=[("v", tok_v_wq[p])], bump="qw%d" % p)
        tok_quant_v = cnt["v"]

        wmf = [xf[:, 0:D], xf[:, D:2 * D]]
        tok_wm_d = [0, 0]
        tok_wm_t = [0, 0]
        tok_mod_a = [0, 0]
        md_last = 0
        for fc in range(NFC):
            p = fc % 2
            psmod = ps_f0 if p == 0 else ps_f1
            s12 = fc % 12
            w = [("t", tok_wm_t[p])]
            if fc < 2:
                w.append(("v", tok_quant_v))
            tok_wm_d[p] = E("g" if p == 0 else "s", lambda h, fc=fc, p=p: h.dma_start(
                out=wmf[p], in_=wmod[fc * 128:(fc + 1) * 128, :]),
                waits=w, bump="wm%d" % p)
            tmod = 0
            for k in range(KD):
                w = []
                if k == 0:
                    w = [("wm%d" % p, tok_wm_d[p]), ("a", tok_cs),
                         ("a", tok_mod_a[p])]
                tmod = E("t", lambda h, p=p, k=k, psmod=psmod: h.matmul(
                    psmod[0:1, 0:128], qtmp[:, k:k + 1],
                    wmf[p][:, k * 128:(k + 1) * 128],
                    start=(k == 0), stop=(k == KD - 1)), waits=w)
            tok_wm_t[p] = tmod
            tok_mod_a[p] = E("a", lambda h, s12=s12, psmod=psmod: h.activation(
                modrow[0:1, s12 * 128:(s12 + 1) * 128], psmod[0:1, 0:128],
                AF.Copy), waits=[("t", tmod), ("md", md_last)])
            if s12 == 11:
                b = fc // 12
                E("s", lambda h, b=b: h.dma_start(
                    out=modscr[b * 1536:(b + 1) * 1536].rearrange(
                        "(a b) -> a b", a=1),
                    in_=modrow[0:1, :]),
                    waits=[("a", tok_mod_a[1])], bump="md")
                md_last = cnt["md"]
        tok_wm_full = [cnt["wm0"], cnt["wm1"]]

        tok_mm_d = E("s", lambda h: h.dma_start(
            out=modmat[:, :].rearrange("p (v k) -> p v k", v=6),
            in_=bass.AP(modscr_h, 0, [[1, 128], [D, 6], [128, KD]]),
            allow_slow_non_contiguous=True),
            waits=[("md", cnt["md"])], bump="md")

        E("v", lambda h: h.tensor_scalar(onepca[:, :], modmat[:, 16:32], 1.0, None,
                                         OP.add), waits=[("md", tok_mm_d)])
        E("v", lambda h: h.tensor_scalar(onepcb[:, :], modmat[:, 64:80], 1.0, None,
                                         OP.add))

        def emit_clamp(src_off, dst, prev_tok):
            E("v", lambda h: h.tensor_single_scalar(
                qtmp[:, 0:KD], modmat[:, src_off:src_off + KD], 0.0, OP.is_ge),
              waits=[("md", tok_mm_d)])
            FLUSHV()
            E("v", lambda h: h.tensor_scalar(qtmp[:, 2 * KD:3 * KD], qtmp[:, 0:KD],
                                             2.0, -1.0, OP.mult, OP.add))
            ta = E("a", lambda h: h.activation(
                qtmp[:, KD:2 * KD], modmat[:, src_off:src_off + KD], AF.Abs),
                waits=[("md", tok_mm_d), ("v", prev_tok)])
            FLUSHA()
            ta = FLUSHA()
            E("v", lambda h: h.tensor_scalar(qtmp[:, 3 * KD:4 * KD],
                                             qtmp[:, KD:2 * KD],
                                             0.25, None, OP.max),
              waits=[("a", ta)])
            FLUSHV()
            return E("v", lambda h, dst=dst: h.tensor_mul(
                dst[:, :], qtmp[:, 2 * KD:3 * KD], qtmp[:, 3 * KD:4 * KD]))

        tok_ga = emit_clamp(32, ga_cl, 0)
        FLUSHV()
        tok_gb = emit_clamp(80, gb_cl, tok_ga)
        FLUSHV()
        E("v", lambda h: h.tensor_scalar(gq[:, :], gb_cl[:, :], scol[:, 0:1], None,
                                         OP.mult), waits=[("v", tok_scol)])
        FLUSHV()
        E("v", lambda h: h.tensor_mul(gbb[:, :], gb_cl[:, :], fbias_sb[:, :]))
        tok_gbb = FLUSHV()
        tok_pro_v = cnt["v"]
        tok_pro_a = cnt["a"]
        tok_pro_t = cnt["t"]

        tok_xf_free_v = tok_pro_v
        tok_hb_free_t = tok_pro_t
        tok_s1_free_a = tok_pro_a
        tok_s1_free_v = tok_pro_v
        tok_s2_free_v = tok_pro_v
        tok_m_free = ("v", tok_pro_v)
        tok_i_free = ("v", tok_pro_v)
        tok_b0_free = ("v", tok_pro_v)
        tok_b1_free = ("v", tok_pro_v)
        tok_f_free_a = [tok_pro_a, tok_pro_a]
        last_scan_par = {0: tok_af, 1: tok_af}
        last_idmm_par = {i: tok_pro_t for i in range(4)}
        idmm_k_tok = {}
        tok_wqb_rd = [0, 0]
        tok_out_dma = [0, 0]
        add_hist = [0] * 8
        xd_grp = [0] * 8
        tok_x = 0

        carry_v = carry[:, :].rearrange("p (n k) -> p n k", n=NN)

        for m in range(NM_RUN):
            c0 = m * T
            if m == 0:
                for k in range(KD):
                    w = []
                    if k == 0:
                        w = [("v", tok_xf_free_v),
                             ("t", tok_wm_t[0]), ("t", tok_wm_t[1]),
                             ("wm0", tok_wm_full[0]), ("wm1", tok_wm_full[1])]
                    E("s", lambda h, k=k, c0=c0: h.dma_start(
                        out=xf[:, k * T:(k + 1) * T],
                        in_=xT[k * 128:(k + 1) * 128, c0:c0 + T]),
                        waits=w, bump="xd")
                tok_x = cnt["xd"]
                ktok = lambda k: [("xd", tok_x)]
            else:
                ktok = lambda k, xd=cnt["xd"]: (
                    [("xd", xd)] if k == 0 else [])

            def emit_ln_finish(onepc, svec_off, tt1, tt2):
                nonlocal tok_s1_free_a, tok_s1_free_v, tok_s2_free_v
                nonlocal tok_m_free, tok_i_free, tok_hb_free_t
                tok_s1_free_v = E("v", lambda h: h.tensor_scalar(
                    mrow[0:1, :], ps_s1[0:1, :], 1.0 / D, None, OP.mult),
                    waits=[("t", tt1)], war=True)
                tok_s2_free_v = E("v", lambda h: h.tensor_scalar(
                    e2row[0:1, :], ps_s2[0:1, :], 1.0 / D, None, OP.mult),
                    waits=[("t", tt2)], war=True)
                E("v", lambda h: h.tensor_mul(vrow[0:1, :], mrow[0:1, :],
                                              mrow[0:1, :]))
                E("v", lambda h: h.tensor_sub(e2row[0:1, :], e2row[0:1, :],
                                              vrow[0:1, :]))
                tv = E("v", lambda h: h.tensor_scalar(vrow[0:1, :], e2row[0:1, :],
                                                      LN_EPS, None, OP.add))
                tln = E("a", lambda h: h.activation(isrow[0:1, :], vrow[0:1, :],
                                                    AF.Ln), waits=[("v", tv)])
                tlh = E("v", lambda h: h.tensor_scalar(
                    vrow[0:1, :], isrow[0:1, :], -0.5, None, OP.mult),
                    waits=[("a", tln)])
                ti = E("a", lambda h: h.activation(isrow[0:1, :], vrow[0:1, :],
                                                   AF.Exp),
                       waits=[("v", tlh)])
                tmb = E("v", lambda h: h.tensor_copy(m_bf[0:1, :], mrow[0:1, :]))
                tib = E("v", lambda h: h.tensor_copy(is_bf[0:1, :], isrow[0:1, :]),
                        waits=[("a", ti)])
                trm = E("t", lambda h: h.matmul(ps_m[:, :], ones1b[:, :],
                                                m_bf[0:1, :], start=True, stop=True),
                        waits=[("v", tmb), tok_m_free])
                tri = E("t", lambda h: h.matmul(ps_i[:, :], ones1b[:, :],
                                                is_bf[0:1, :], start=True, stop=True),
                        waits=[("v", tib), tok_i_free])
                tmr = E("v", lambda h: h.tensor_copy(mrep_sb[:, :], ps_m[:, :]),
                        waits=[("t", trm)], war=True)
                tir = E("v", lambda h: h.tensor_copy(isrep_sb[:, :], ps_i[:, :]),
                        waits=[("t", tri)], war=True)
                tok_m_free = ("v", tmr)
                tok_i_free = ("v", tir)
                tout = 0
                for k in range(KD):
                    E("v", lambda h, k=k: h.tensor_sub(
                        t1f[:, :], xf[:, k * T:(k + 1) * T], mrep_sb[:, :]))
                    E("v", lambda h, k=k, onepc=onepc: h.scalar_tensor_tensor(
                        t2b[:, :], t1f[:, :], onepc[:, k:k + 1],
                        isrep_sb[:, :], OP.mult, OP.mult))
                    tout = E("v", lambda h, k=k, svec_off=svec_off: h.tensor_scalar(
                        hb[:, k * T:(k + 1) * T], t2b[:, :],
                        modmat[:, svec_off + k:svec_off + k + 1], None, OP.add))
                return tout

            ta_last = 0
            for k in range(KD):
                w = ktok(k)
                if k == 0:
                    w = w + [("t", tok_hb_free_t)]
                    if m == 0:
                        w.append(("v", tok_xf_free_v))
                ta_last = E("a", lambda h, k=k: h.activation(
                    hb[:, k * T:(k + 1) * T], xf[:, k * T:(k + 1) * T],
                    AF.Square), waits=w)
            tt1 = 0
            for k in range(KD):
                w = ktok(k)
                if k == 0:
                    w = w + [("v", tok_s1_free_v), ("a", tok_s1_free_a)]
                tt1 = E("t", lambda h, k=k: h.matmul(
                    ps_s1[0:1, :], ones128f[:, :], xf[:, k * T:(k + 1) * T],
                    start=(k == 0), stop=(k == KD - 1)), waits=w)
            tt2 = 0
            for k in range(KD):
                w = [("a", ta_last), ("v", tok_s2_free_v)] if k == 0 else []
                tt2 = E("t", lambda h, k=k: h.matmul(
                    ps_s2[0:1, :], ones128b[:, :], hb[:, k * T:(k + 1) * T],
                    start=(k == 0), stop=(k == KD - 1)), waits=w)
            tok_h = emit_ln_finish(onepca, 0, tt1, tt2)

            txp = 0
            for k in range(KD):
                w = []
                if k == 0:
                    w = [("v", tok_h), ("a", tok_f_free_a[0])]
                    if m == 0:
                        w.append(("v", tok_wx))
                txp = E("t", lambda h, k=k: h.matmul(
                    ps_f0[0:96, :], wx_sb[:, k * 96:(k + 1) * 96],
                    hb[:, k * T:(k + 1) * T],
                    start=(k == 0), stop=(k == KD - 1)), waits=w)
            ta_xp = E("a", lambda h: h.activation(xprj[:, :], ps_f0[0:96, :],
                                                  AF.Copy), waits=[("t", txp)])
            tok_f_free_a[0] = ta_xp
            tok_sp = []
            prev_sp = 0
            for k in range(KD):
                w = [("a", prev_sp)]
                if k == 0:
                    w = [("a", ta_xp), ("v", tok_s1_free_v)]
                    if m == 0:
                        w.append(("v", tok_wdt))
                tdt = E("t", lambda h, k=k: h.matmul(
                    ps_s1[:, :], wdt_sb[:, k * 128:(k + 1) * 128], xprj[0:RD, :],
                    start=True, stop=True), waits=w)
                tex = E("a", lambda h, k=k: h.activation(
                    t1f[:, :], ps_s1[:, :], AF.Exp,
                    bias=bdt_sb[:, k:k + 1]), waits=[("t", tdt)])
                DRAIN("a")
                prev_sp = E("a", lambda h, k=k: h.activation(
                    dtb[:, k * T:(k + 1) * T], t1f[:, :], AF.Ln,
                    bias=ones128f[:, 0:1]))
                tok_sp.append(prev_sp)
            tok_s1_free_a = tok_sp[-1]
            for k in range(KD):
                E("v", lambda h, k=k: h.tensor_mul(
                    vb[:, k * T:(k + 1) * T], dtb[:, k * T:(k + 1) * T],
                    hb[:, k * T:(k + 1) * T]), waits=[("a", tok_sp[k])])

            tok_bc = 0
            for n in range(NN):
                p = n % 2
                psb = ps_b0 if p == 0 else ps_m
                psc = ps_b1 if p == 0 else ps_i
                wb = [("a", ta_xp), (tok_b0_free if p == 0 else tok_m_free)]
                tb = E("t", lambda h, n=n, psb=psb: h.matmul(
                    psb[:, :], onehotm[:, n * 128:(n + 1) * 128], xprj[:, :],
                    start=True, stop=True), waits=wb)
                wc = [(tok_b1_free if p == 0 else tok_i_free)]
                tc = E("t", lambda h, n=n, psc=psc: h.matmul(
                    psc[:, :], onehotm[:, (16 + n) * 128:(17 + n) * 128],
                    xprj[:, :], start=True, stop=True), waits=wc)
                tbr = E("a", lambda h, n=n, psb=psb: h.activation(
                    brepall[:, n * WSEG + 1:n * WSEG + 1 + T], psb[:, :],
                    AF.Copy), waits=[("t", tb)])
                tcr = E("a", lambda h, n=n, psc=psc: h.activation(
                    crepall[:, n * WSEG + 1:n * WSEG + 1 + T], psc[:, :],
                    AF.Copy), waits=[("t", tc)])
                if p == 0:
                    tok_b0_free, tok_b1_free = ("a", tbr), ("a", tcr)
                else:
                    tok_m_free, tok_i_free = ("a", tbr), ("a", tcr)
                tok_bc = tcr

            pend_idmm = None
            pend_epi = None
            pend_sq = None
            ltt1 = ltt2 = 0

            def emit_idmm(pend, vtok):
                nonlocal tok_f_free_a
                (kk, gg, sbase) = pend
                psf = ps_f0 if kk % 2 == 0 else ps_f1
                tokm = 0
                for j in range(4):
                    w = []
                    if j == 0:
                        w = [("v", vtok)]
                        if gg == 0:
                            w.append(("a", tok_f_free_a[kk % 2]))
                    tokm = E("t", lambda h, j=j, psf=psf, sbase=sbase: h.matmul(
                        psf[:, :], identb[:, :],
                        sb2[:, sbase + j * WSEG + 1:sbase + j * WSEG + 1 + T],
                        start=(gg == 0 and j == 0), stop=(gg == 3 and j == 3)),
                        waits=w)
                last_idmm_par[(kk * 4 + gg) % 4] = tokm
                if gg == 3:
                    idmm_k_tok[kk] = tokm

            def emit_epiV(kk):
                nonlocal tok_f_free_a
                psf = ps_f0 if kk % 2 == 0 else ps_f1
                ys = yac2[:, (kk % 2) * T:(kk % 2 + 1) * T]
                ev = E("a", lambda h, psf=psf, ys=ys: h.activation(
                    ys, psf[:, :], AF.Copy), waits=[("t", idmm_k_tok[kk])])
                tok_f_free_a[kk % 2] = ev
                E("v", lambda h, kk=kk, ys=ys: h.scalar_tensor_tensor(
                    ys, hb[:, kk * T:(kk + 1) * T], dvec_sb[:, kk:kk + 1],
                    ys, OP.mult, OP.add), waits=[("a", ev)])
                wga = [("v", tok_ga)] if (m == 0 and kk == 0) else []
                tga = E("v", lambda h, kk=kk, ys=ys: h.scalar_tensor_tensor(
                    xf[:, kk * T:(kk + 1) * T], ys, ga_cl[:, kk:kk + 1],
                    xf[:, kk * T:(kk + 1) * T], OP.mult, OP.add), waits=wga)
                return tga

            def emit_epiT1(kk, x2pub):
                nonlocal ltt1
                w1 = [("v", x2pub)]
                if kk == 0:
                    w1 += [("v", tok_s1_free_v), ("a", tok_s1_free_a)]
                ltt1 = E("t", lambda h, kk=kk: h.matmul(
                    ps_s1[0:1, :], ones128f[:, :], xf[:, kk * T:(kk + 1) * T],
                    start=(kk == 0), stop=(kk == KD - 1)), waits=w1)

            def emit_epiSq(kk, x2pub):
                nonlocal ltt2
                tsq = E("a", lambda h, kk=kk: h.activation(
                    hb[:, kk * T:(kk + 1) * T], xf[:, kk * T:(kk + 1) * T],
                    AF.Square), waits=[("v", x2pub)])
                w2 = [("a", tsq)]
                if kk == 0:
                    w2.append(("v", tok_s2_free_v))
                ltt2 = E("t", lambda h, kk=kk: h.matmul(
                    ps_s2[0:1, :], ones128b[:, :], hb[:, kk * T:(kk + 1) * T],
                    start=(kk == 0), stop=(kk == KD - 1)), waits=w2)

            for G in range(64):
                k, g = G // 4, G % 4
                slot = G % 2
                slot4 = G % 4
                base = slot4 * WGRP
                dbase = slot * WGRP
                n0 = 4 * g
                for j in range(4):
                    n = n0 + j
                    w = [("v", last_scan_par[slot])] if j == 0 else []
                    E("a", lambda h, k=k, n=n, dbase=dbase, j=j: h.activation(
                        da2[:, dbase + j * WSEG + 1:dbase + j * WSEG + 1 + T],
                        dtb[:, k * T:(k + 1) * T], AF.Exp,
                        scale=a_sb[:, k * NN + n:k * NN + n + 1]),
                        waits=w, defer=True)
                exp_tok = cnt["a"]
                sb2v = sb2[:, base:base + WGRP].rearrange(
                    "p (s w) -> p s w", s=4)
                w = [("t", last_idmm_par[slot4])]
                if G == 0:
                    w.append(("a", tok_bc))
                E("v", lambda h, sb2v=sb2v, n0=n0, k=k: h.tensor_copy(
                    sb2v[:, :, 0:1], carry_v[:, n0:n0 + 4, k:k + 1]), waits=w)
                for j in range(4):
                    n = n0 + j
                    E("v", lambda h, k=k, n=n, base=base, j=j: h.tensor_mul(
                        sb2[:, base + j * WSEG + 1:base + j * WSEG + 1 + T],
                        vb[:, k * T:(k + 1) * T],
                        brepall[:, n * WSEG + 1:n * WSEG + 1 + T]))
                tok_scan = E("v", lambda h, base=base, dbase=dbase:
                             h.tensor_tensor_scan(
                                 sb2[:, base:base + WGRP],
                                 da2[:, dbase:dbase + WGRP],
                                 sb2[:, base:base + WGRP], 0.0,
                                 OP.mult, OP.add),
                             waits=[("a", exp_tok)], war=True)
                last_scan_par[slot] = tok_scan
                E("v", lambda h, sb2v=sb2v, n0=n0, k=k: h.tensor_copy(
                    carry_v[:, n0:n0 + 4, k:k + 1], sb2v[:, :, T:T + 1]))
                E("v", lambda h, base=base, n0=n0: h.tensor_mul(
                    sb2[:, base:base + WGRP], sb2[:, base:base + WGRP],
                    crepall[:, n0 * WSEG:(n0 + 4) * WSEG]))
                if pend_idmm is not None:
                    emit_idmm(pend_idmm, tok_scan)
                    pend_idmm = None
                if pend_sq is not None:
                    emit_epiSq(*pend_sq)
                    pend_sq = None
                if pend_epi is not None:
                    emit_epiT1(pend_epi[0], tok_scan)
                    pend_sq = (pend_epi[0], tok_scan)
                    pend_epi = None
                pend_idmm = (k, g, base)
                if G % 4 == 2 and G >= 6:
                    kk = G // 4 - 1
                    tga = emit_epiV(kk)
                    pend_epi = (kk, tga)

            t_tail = FLUSHV()
            emit_idmm(pend_idmm, t_tail)
            pend_idmm = None
            if pend_sq is not None:
                emit_epiSq(*pend_sq)
                pend_sq = None
            if pend_epi is not None:
                emit_epiT1(pend_epi[0], t_tail)
                emit_epiSq(pend_epi[0], t_tail)
                pend_epi = None
            tga15 = emit_epiV(KD - 1)
            emit_epiT1(KD - 1, tga15)
            emit_epiSq(KD - 1, tga15)

            tok_h2 = emit_ln_finish(onepcb, 48, ltt1, ltt2)
            tok_hb_free_t = txp

            for gj in range(8):
                pg = gj % 2
                te1_last = 0
                for jj in range(2):
                    j = 2 * gj + jj
                    p = j % 2
                    psf = ps_f0 if p == 0 else ps_f1
                    col0 = p * KD * 128
                    w = [("t", tok_wqb_rd[p])]
                    if m == 0:
                        w += [("qw0", cnt["qw0"]), ("qw1", cnt["qw1"])]
                    E("g", lambda h, j=j, col0=col0: h.dma_start(
                        out=wqb[:, col0:col0 + KD * 128],
                        in_=bass.AP(wq_h, j * 128 * KD * 128,
                                    [[KD * 128, 128], [1, KD * 128]])),
                        waits=w, bump="wq%d" % p)
                    tok_wq_dma = cnt["wq%d" % p]
                    tmmf = 0
                    for k in range(KD):
                        w = []
                        if k == 0:
                            w = [("wq%d" % p, tok_wq_dma), ("v", tok_h2),
                                 ("a", tok_f_free_a[p])]
                        tmmf = E("t", lambda h, k=k, psf=psf, col0=col0: h.matmul(
                            psf[:, :], wqb[:, col0 + k * 128:col0 + (k + 1) * 128],
                            hb[:, k * T:(k + 1) * T],
                            start=(k == 0), stop=(k == KD - 1)), waits=w)
                    tok_wqb_rd[p] = tmmf
                    we = [("t", tmmf)]
                    if jj == 0 and add_hist[(gj - 2) % 8] and gj >= 2:
                        we.append(("v", add_hist[gj - 2]))
                    if m == 0 and j == 0:
                        we.append(("v", tok_gbb))
                    dst = e1b[:, pg * 1024 + jj * T:pg * 1024 + (jj + 1) * T]
                    te1_last = E("a", lambda h, j=j, psf=psf, dst=dst: h.activation(
                        dst, psf[:, :], AF.Identity,
                        bias=gbb[:, j:j + 1], scale=gq[:, j:j + 1]), waits=we)
                    tok_f_free_a[p] = te1_last
                tadd = E("v", lambda h, gj=gj, pg=pg: h.tensor_add(
                    outst[:, pg * 1024:(pg + 1) * 1024],
                    xf[:, gj * 1024:(gj + 1) * 1024],
                    e1b[:, pg * 1024:(pg + 1) * 1024]),
                    waits=[("a", te1_last), ("od%d" % pg, tok_out_dma[pg])])
                add_hist[gj] = tadd
                E("s", lambda h, gj=gj, pg=pg, c0=c0: h.dma_start(
                    out=yT[gj * 256:(gj + 1) * 256, c0:c0 + T].rearrange(
                        "(a p) c -> p a c", p=128),
                    in_=outst[:, pg * 1024:(pg + 1) * 1024].rearrange(
                        "p (a c) -> p a c", a=2)),
                    waits=[("v", tadd)], bump="od%d" % pg)
                tok_out_dma[pg] = cnt["od%d" % pg]
                if m < NM_RUN - 1:
                    c0n = (m + 1) * T
                    E("s", lambda h, gj=gj, c0n=c0n: h.dma_start(
                        out=xf[:, gj * 1024:(gj + 1) * 1024].rearrange(
                            "p (a c) -> p a c", a=2),
                        in_=xT[gj * 256:(gj + 1) * 256, c0n:c0n + T].rearrange(
                            "(a p) c -> p a c", p=128)),
                        waits=[("v", tadd)], bump="xd")
            tok_hb_free_t = max(tok_wqb_rd)

        E("s", lambda h: None,
          waits=[("od0", cnt["od0"]), ("od1", cnt["od1"])], bump=None)

        def runner(tasks, eng_name):
            compute = eng_name in ("vector", "scalar", "tensor")

            def f(h):
                pending = None
                for (waits, fn, bump, war, defer) in tasks:
                    if pending is not None and waits:
                        h.drain().then_inc(*pending)
                        pending = None
                    for (sk, val) in waits:
                        h.wait_ge(sems[SEM_OF_CNT[sk]], val)
                    if pending is not None and bump is not None and not defer:
                        h.drain().then_inc(*pending)
                        pending = None
                    instr = fn(h)
                    if instr is None:
                        continue
                    if pending is not None:
                        instr.then_inc(*pending)
                        pending = None
                    if bump is None:
                        continue
                    if defer:
                        pending = (sems[bump], STEP[bump])
                        continue
                    needs_drain = (compute and not war and
                                   cnt_tok[id(fn)] in waited[bump])
                    if needs_drain:
                        h.drain().then_inc(sems[bump], STEP[bump])
                    else:
                        instr.then_inc(sems[bump], STEP[bump])
                if pending is not None:
                    h.drain().then_inc(*pending)
            return f

        with nc.Block() as block:
            block.sync(runner(script["sync"], "sync"))
            block.gpsimd(runner(script["gpsimd"], "gpsimd"))
            block.tensor(runner(script["tensor"], "tensor"))
            block.vector(runner(script["vector"], "vector"))
            block.scalar(runner(script["scalar"], "scalar"))

    return nc


def kernel(L0_attn_x, L0_attn_W_x, L0_attn_W_dt, L0_attn_b_dt, L0_attn_A_log,
           L0_attn_D, L0_ffn_weight, L0_ffn_bias, L0_c, L0_wmod):
    nc = build_nc()

    wxT = np.ascontiguousarray(L0_attn_W_x.T)
    wdtT = np.ascontiguousarray(L0_attn_W_dt.T)
    bdt_a = np.ascontiguousarray(L0_attn_b_dt.reshape(KD, 128).T)
    alog_a = np.ascontiguousarray(
        L0_attn_A_log.reshape(KD, 128, NN).transpose(1, 0, 2).reshape(128, KD * NN))
    dvec_a = np.ascontiguousarray(L0_attn_D.reshape(KD, 128).T)
    fbias_a = np.ascontiguousarray(L0_ffn_bias.reshape(KD, 128).T)
    wT_a = np.ascontiguousarray(L0_ffn_weight.T)
    wmod_a = np.ascontiguousarray(
        L0_wmod.reshape(KD, 128, NFC, 128).transpose(2, 1, 0, 3)
        .reshape(NFC * 128, KD * 128))

    import ml_dtypes
    onehot = np.zeros((96, 32 * 128), dtype=ml_dtypes.bfloat16)
    ident = np.eye(128, dtype=ml_dtypes.bfloat16)
    for j in range(32):
        onehot[64 + j, j * 128:(j + 1) * 128] = 1.0

    in_maps = []
    for b in range(NB):
        in_maps.append({
            "xT": np.ascontiguousarray(L0_attn_x[b].T),
            "wxT": wxT, "wdtT": wdtT, "bdt": bdt_a, "alog": alog_a,
            "dvec": dvec_a, "fbias": fbias_a, "wT": wT_a,
            "cb_in": np.ascontiguousarray(L0_c[b].reshape(KD, 128).T),
            "wmod_r": wmod_a, "onehot_bc": onehot, "ident_bc": ident,
        })
    res = run_bass_kernel_spmd(nc, in_maps, core_ids=list(range(NB)))
    out = np.stack([np.ascontiguousarray(r["yT"].T) for r in res.results], axis=0)
    return out.astype(np.float32)


if __name__ == "__main__":
    build_nc()
    print("build ok")


# version 25
# speedup vs baseline: 1.4582x; 1.0081x over previous
import numpy as np
import contextlib
import concourse.bass as bass
import concourse.mybir as mybir
from concourse.bass_utils import run_bass_kernel_spmd

F32 = mybir.dt.float32
BF16 = mybir.dt.bfloat16
AF = mybir.ActivationFunctionType
OP = mybir.AluOpType
AX = mybir.AxisListType

D, L, NN, RD = 2048, 4096, 16, 64
NB = 8
KD = D // 128
T = 512
NM = L // T
NFC = 96
LN_EPS = 1e-5
Q_EPS = 1e-10
WSEG = T + 1
WGRP = 4 * WSEG

ENGS = ("sync", "gpsimd", "tensor", "vector", "scalar")
NM_RUN = NM
ENG_OF = dict(v="vector", a="scalar", t="tensor", g="gpsimd", s="sync")
DMA_SEMS = ("sw", "ws0", "ws1", "qw0", "qw1", "wm0", "wm1", "wm2", "wm3",
            "md", "xd", "xg", "wq0", "wq1", "od0", "od1")
STEP = dict(v=1, a=1, t=1, g=1, s=1, **{k: 16 for k in DMA_SEMS})
SEM_OF_CNT = dict(v="v", a="a", t="t", **{k: k for k in DMA_SEMS})


def build_nc():
    nc = bass.Bass("TRN2", target_bir_lowering=False)

    xT = nc.dram_tensor("xT", [D, L], F32, kind="ExternalInput").ap()
    wxT = nc.dram_tensor("wxT", [D, 96], F32, kind="ExternalInput").ap()
    wdtT = nc.dram_tensor("wdtT", [RD, D], F32, kind="ExternalInput").ap()
    bdt = nc.dram_tensor("bdt", [128, KD], F32, kind="ExternalInput").ap()
    alog = nc.dram_tensor("alog", [128, KD * NN], F32, kind="ExternalInput").ap()
    dvec = nc.dram_tensor("dvec", [128, KD], F32, kind="ExternalInput").ap()
    fbias = nc.dram_tensor("fbias", [128, KD], F32, kind="ExternalInput").ap()
    wT = nc.dram_tensor("wT", [D, D], F32, kind="ExternalInput").ap()
    cb_in = nc.dram_tensor("cb_in", [128, KD], F32, kind="ExternalInput").ap()
    wmod = nc.dram_tensor("wmod_r", [NFC * 128, KD * 128], F32,
                          kind="ExternalInput").ap()
    onehot_d = nc.dram_tensor("onehot_bc", [96, 32 * 128], BF16,
                              kind="ExternalInput").ap()
    ident_d = nc.dram_tensor("ident_bc", [128, 128], BF16,
                             kind="ExternalInput").ap()
    yT = nc.dram_tensor("yT", [D, L], F32, kind="ExternalOutput").ap()
    wq_h = nc.dram_tensor("wq_scratch", [D, D], BF16)
    wq_d = wq_h.ap()
    modscr_h = nc.dram_tensor("mod_scratch", [6 * D], F32)
    modscr = modscr_h.ap()

    script = {e: [] for e in ENGS}
    cnt = dict(v=0, a=0, t=0, **{k: 0 for k in DMA_SEMS})
    DEFAULT_BUMP = dict(v="v", a="a", t="t", g=None, s=None)

    waited = {k: set() for k in cnt}
    cnt_tok = {}

    def E(eng, fn, waits=(), bump="default", war=False, defer=False):
        if bump == "default":
            bump = DEFAULT_BUMP[eng]
        w = tuple((k, val) for (k, val) in waits if val and val > 0)
        for (k, val) in w:
            waited[k].add(val)
        tok = None
        if bump is not None:
            cnt[bump] += STEP[bump]
            tok = cnt[bump]
        cnt_tok[id(fn)] = tok
        script[ENG_OF[eng]].append((w, fn, bump, war, defer))
        return tok

    def FLUSHV():
        return E("v", lambda h: h.tensor_copy(flsh[:, 128:256], flsh[:, 0:128]))

    def DRAIN(eng):
        E(eng, lambda h: h.drain(), bump=None)

    def FLUSHA():
        return E("a", lambda h: h.activation(flsh[:, 128:256], flsh[:, 0:128],
                                             AF.Copy))

    stack = contextlib.ExitStack()
    with stack:
        sems = {}
        for k in ("v", "a", "t") + DMA_SEMS:
            sems[k] = stack.enter_context(nc.semaphore("sem_" + k))

        def ST(name, shape, dt):
            return stack.enter_context(nc.sbuf_tensor(name, shape, dt))

        def PT(name, shape, dt):
            return stack.enter_context(nc.psum_tensor(name, shape, dt))

        xf = ST("xf", [128, KD * T], F32)
        hb = ST("hb", [128, KD * T], BF16)
        dtb = ST("dtb", [128, KD * T], BF16)
        vb = ST("vb", [128, KD * T], BF16)
        da2 = ST("da2", [128, 2 * WGRP], BF16)
        sb2 = ST("sb2", [128, 4 * WGRP], BF16)
        yac2 = ST("yac2", [128, 2 * T], BF16)
        brepall = ST("brepall", [128, NN * WSEG], BF16)
        crepall = ST("crepall", [128, NN * WSEG], BF16)
        identb = ST("identb", [128, 128], BF16)
        carry = ST("carry", [128, NN * KD], BF16)
        xprj = ST("xprj", [96, T], BF16)
        e1b = ST("e1b", [128, 2 * 1024], BF16)
        outst = ST("outst", [128, 2 * 1024], F32)
        wqb = ST("wqb", [128, 2 * KD * 128], BF16)
        t1f = ST("t1f", [128, T], F32)
        mrow = ST("mrow", [1, T], F32)
        e2row = ST("e2row", [1, T], F32)
        vrow = ST("vrow", [1, T], F32)
        isrow = ST("isrow", [1, T], F32)
        m_bf = ST("m_bf", [1, T], BF16)
        is_bf = ST("is_bf", [1, T], BF16)
        mrep_sb = ST("mrep_sb", [128, T], BF16)
        isrep_sb = ST("isrep_sb", [128, T], BF16)
        wx_sb = ST("wx_sb", [128, KD * 96], BF16)
        wdt_sb = ST("wdt_sb", [RD, D], BF16)
        a_sb = ST("a_sb", [128, KD * NN], F32)
        bdt_sb = ST("bdt_sb", [128, KD], F32)
        dvec_sb = ST("dvec_sb", [128, KD], F32)
        fbias_sb = ST("fbias_sb", [128, KD], F32)
        cbq = ST("cbq", [128, KD], BF16)
        modmat = ST("modmat", [128, 96], F32)
        onepca = ST("onepca", [128, KD], F32)
        ga_cl = ST("ga_cl", [128, KD], F32)
        onepcb = ST("onepcb", [128, KD], F32)
        gb_cl = ST("gb_cl", [128, KD], F32)
        gq = ST("gq", [128, KD], F32)
        gbb = ST("gbb", [128, KD], F32)
        invcol = ST("invcol", [128, 1], F32)
        scol = ST("scol", [128, 1], F32)
        sc_sc = ST("sc_sc", [1, 2], F32)
        invr = ST("invr", [1, 1], F32)
        abscol = ST("abscol", [128, KD], F32)
        abssum = ST("abssum", [128, 1], F32)
        modrow = ST("modrow", [1, 12 * 128], F32)
        ones128f = ST("ones128f", [128, 1], F32)
        ones128b = ST("ones128b", [128, 1], BF16)
        ones1f = ST("ones1f", [1, 128], F32)
        ones1b = ST("ones1b", [1, 128], BF16)
        onehotm = ST("onehotm", [96, 32 * 128], BF16)
        qtmp = ST("qtmp", [128, 4 * KD], F32)
        t2b = ST("t2b", [128, T], BF16)
        flsh = ST("flsh", [128, 256], F32)

        ps_s1 = PT("ps_s1", [128, T], F32)
        ps_s2 = PT("ps_s2", [128, T], F32)
        ps_m = PT("ps_m", [128, T], F32)
        ps_i = PT("ps_i", [128, T], F32)
        ps_b0 = PT("ps_b0", [128, T], F32)
        ps_b1 = PT("ps_b1", [128, T], F32)
        ps_f0 = PT("ps_f0", [128, T], F32)
        ps_f1 = PT("ps_f1", [128, T], F32)

        E("v", lambda h: h.memset(ones128f[:, :], 1.0))
        E("v", lambda h: h.memset(ones128b[:, :], 1.0))
        E("v", lambda h: h.memset(ones1f[:, :], 1.0))
        E("v", lambda h: h.memset(ones1b[:, :], 1.0))

        E("v", lambda h: h.memset(carry[:, :], 0.0))
        E("v", lambda h: h.memset(da2[:, :], 0.0))
        E("v", lambda h: h.memset(crepall[:, :], 0.0))

        E("g", lambda h: h.dma_start(
            out=xf[:, 0:KD * 96].rearrange("p (k c) -> p k c", k=KD),
            in_=wxT.rearrange("(k p) c -> p k c", p=128)), bump="sw")
        E("g", lambda h: h.dma_start(
            out=xf[0:RD, 2048:2048 + D], in_=wdtT[:, :]), bump="sw")
        E("g", lambda h: h.dma_start(
            out=xf[:, 4096:4096 + KD * NN], in_=alog[:, :]), bump="sw")
        E("g", lambda h: h.dma_start(out=bdt_sb[:, :], in_=bdt[:, :]), bump="sw")
        E("g", lambda h: h.dma_start(out=dvec_sb[:, :], in_=dvec[:, :]), bump="sw")
        E("g", lambda h: h.dma_start(out=fbias_sb[:, :], in_=fbias[:, :]),
          bump="sw")
        E("g", lambda h: h.dma_start(
            out=xf[:, 4624:4624 + KD], in_=cb_in[:, :]), bump="sw")
        E("g", lambda h: h.dma_start(out=onehotm[:, :], in_=onehot_d[:, :]),
          bump="sw")
        E("g", lambda h: h.dma_start(out=identb[:, :], in_=ident_d[:, :]),
          bump="sw")
        tok_sw = cnt["sw"]

        tok_wx = E("v", lambda h: h.tensor_copy(wx_sb[:, :], xf[:, 0:KD * 96]),
                   waits=[("sw", tok_sw)])
        tok_wdt = E("v", lambda h: h.tensor_copy(wdt_sb[:, :],
                                                 xf[0:RD, 2048:2048 + D]),
                    waits=[("sw", tok_sw)])
        tok_ae = E("a", lambda h: h.activation(xf[:, 4352:4352 + KD * NN],
                                               xf[:, 4096:4096 + KD * NN], AF.Exp),
                   waits=[("sw", tok_sw)])
        tok_a = E("v", lambda h: h.tensor_scalar(
            a_sb[:, :], xf[:, 4352:4352 + KD * NN], -1.0, None, OP.mult),
            waits=[("a", tok_ae)])
        tok_af = FLUSHV()
        tok_cs = E("a", lambda h: h.activation(qtmp[:, 0:KD],
                                               xf[:, 4624:4624 + KD], AF.Silu),
                   waits=[("sw", tok_sw)])
        tok_cb = E("v", lambda h: h.tensor_copy(cbq[:, :], qtmp[:, 0:KD]),
                   waits=[("a", tok_cs)])

        wst = [xf[:, 0:D], xf[:, D:2 * D]]
        tok_v_wst = [tok_wx, tok_wdt]
        tok_wst_d = [0, 0]
        for k in range(KD):
            p = k % 2
            tok_wst_d[p] = E("g" if p == 0 else "s", lambda h, k=k, p=p: h.dma_start(
                out=wst[p], in_=wT[k * 128:(k + 1) * 128, :]),
                waits=[("v", tok_v_wst[p])], bump="ws%d" % p)
            tok_v_wst[p] = E("v", lambda h, k=k, p=p: h.reduce_sum(
                abscol[:, k:k + 1], wst[p], axis=AX.X, apply_absolute_value=True),
                waits=[("ws%d" % p, tok_wst_d[p])])
        FLUSHV()
        E("v", lambda h: h.reduce_sum(abssum[:, :], abscol[:, :], axis=AX.X))
        tok_absc = FLUSHV()
        tok_sred = E("t", lambda h: h.matmul(ps_s1[0:1, 0:1], ones128f[:, :],
                                             abssum[:, :], start=True, stop=True),
                     waits=[("v", tok_absc)])
        E("v", lambda h: h.tensor_scalar(sc_sc[0:1, 0:1], ps_s1[0:1, 0:1],
                                         1.0 / (D * D), None, OP.mult),
          waits=[("t", tok_sred)])
        FLUSHV()
        E("v", lambda h: h.tensor_scalar(sc_sc[0:1, 1:2], sc_sc[0:1, 0:1],
                                         Q_EPS, None, OP.add))
        FLUSHV()
        E("v", lambda h: h.reciprocal(invr[0:1, :], sc_sc[0:1, 1:2]))
        tok_inv = FLUSHV()
        tok_bc1 = E("t", lambda h: h.matmul(ps_m[:, 0:1], ones1f[:, :],
                                            invr[0:1, :], start=True, stop=True),
                    waits=[("v", tok_inv)])
        tok_bc2 = E("t", lambda h: h.matmul(ps_i[:, 0:1], ones1f[:, :],
                                            sc_sc[0:1, 0:1], start=True, stop=True),
                    waits=[("v", tok_inv)])
        E("v", lambda h: h.tensor_copy(invcol[:, :], ps_m[:, 0:1]),
          waits=[("t", tok_bc1)])
        E("v", lambda h: h.tensor_copy(scol[:, :], ps_i[:, 0:1]),
          waits=[("t", tok_bc2)])
        tok_invc = FLUSHV()
        tok_scol = tok_invc

        qabs = [xf[:, 4096 + 0:4096 + D], xf[:, 6144:6144 + D]]
        qsgn = [hb[:, 0:D], hb[:, D:2 * D]]
        qmsk = [hb[:, 2 * D:3 * D], hb[:, 3 * D:4 * D]]
        wqt = [dtb[:, 0:D], dtb[:, D:2 * D]]
        tok_wq_wr = [0, 0]
        tok_v_wq = [0, 0]
        tok_a_abs = [0, 0]
        tok_msk_v = [0, 0]
        for k in range(KD):
            p = k % 2
            tok_wst_d[p] = E("g" if p == 0 else "s", lambda h, k=k, p=p: h.dma_start(
                out=wst[p], in_=wT[k * 128:(k + 1) * 128, :]),
                waits=[("v", tok_v_wst[p]), ("a", tok_a_abs[p])],
                bump="ws%d" % p)
            E("a", lambda h, p=p: h.activation(qsgn[p], wst[p], AF.Sign),
              waits=[("ws%d" % p, tok_wst_d[p]), ("v", tok_v_wq[p])])
            tok_a_abs[p] = E("a", lambda h, p=p: h.activation(
                qabs[p], wst[p], AF.Abs, scale=invcol[:, 0:1]),
                waits=[("v", tok_invc), ("v", tok_msk_v[p])])
            tok_msk_v[p] = E("v", lambda h, p=p: h.tensor_single_scalar(
                qmsk[p], qabs[p], 0.5, OP.is_ge),
                waits=[("a", tok_a_abs[p])], war=True)
            tok_v_wst[p] = tok_msk_v[p]
            tok_v_wq[p] = E("v", lambda h, p=p: h.tensor_mul(
                wqt[p], qsgn[p], qmsk[p]),
                waits=[("qw%d" % p, tok_wq_wr[p])])
            tok_wq_wr[p] = E("g", lambda h, k=k, p=p: h.dma_start(
                out=bass.AP(wq_h, k * 128,
                            [[KD * 128, 128], [128 * KD * 128, KD], [1, 128]]),
                in_=wqt[p].rearrange("p (j c) -> p j c", j=KD)),
                waits=[("v", tok_v_wq[p])], bump="qw%d" % p)
        tok_quant_v = cnt["v"]

        wmf = [xf[:, 0:D], xf[:, D:2 * D],
               xf[:, 2 * D:3 * D], xf[:, 3 * D:4 * D]]
        tok_wm_d = [0, 0, 0, 0]
        tok_wm_t = [0, 0, 0, 0]
        tok_mod_a = [0, 0]
        md_last = 0
        for fc in range(NFC):
            p = fc % 4
            pb = p % 2
            psmod = ps_f0 if pb == 0 else ps_f1
            s12 = fc % 12
            w = [("t", tok_wm_t[p])]
            if fc < 4:
                w.append(("v", tok_quant_v))
            tok_wm_d[p] = E("g" if pb == 0 else "s", lambda h, fc=fc, p=p: h.dma_start(
                out=wmf[p], in_=wmod[fc * 128:(fc + 1) * 128, :]),
                waits=w, bump="wm%d" % p)
            tmod = 0
            for k in range(KD):
                w = []
                if k == 0:
                    w = [("wm%d" % p, tok_wm_d[p]), ("a", tok_cs),
                         ("a", tok_mod_a[pb])]
                tmod = E("t", lambda h, p=p, k=k, psmod=psmod: h.matmul(
                    psmod[0:1, 0:128], qtmp[:, k:k + 1],
                    wmf[p][:, k * 128:(k + 1) * 128],
                    start=(k == 0), stop=(k == KD - 1)), waits=w)
            tok_wm_t[p] = tmod
            tok_mod_a[pb] = E("a", lambda h, s12=s12, psmod=psmod: h.activation(
                modrow[0:1, s12 * 128:(s12 + 1) * 128], psmod[0:1, 0:128],
                AF.Copy), waits=[("t", tmod), ("md", md_last)])
            if s12 == 11:
                b = fc // 12
                E("s", lambda h, b=b: h.dma_start(
                    out=modscr[b * 1536:(b + 1) * 1536].rearrange(
                        "(a b) -> a b", a=1),
                    in_=modrow[0:1, :]),
                    waits=[("a", tok_mod_a[1])], bump="md")
                md_last = cnt["md"]
        tok_wm_full = [cnt["wm%d" % i] for i in range(4)]

        tok_mm_d = E("s", lambda h: h.dma_start(
            out=modmat[:, :].rearrange("p (v k) -> p v k", v=6),
            in_=bass.AP(modscr_h, 0, [[1, 128], [D, 6], [128, KD]]),
            allow_slow_non_contiguous=True),
            waits=[("md", cnt["md"])], bump="md")

        E("v", lambda h: h.tensor_scalar(onepca[:, :], modmat[:, 16:32], 1.0, None,
                                         OP.add), waits=[("md", tok_mm_d)])
        E("v", lambda h: h.tensor_scalar(onepcb[:, :], modmat[:, 64:80], 1.0, None,
                                         OP.add))

        def emit_clamp(src_off, dst, prev_tok):
            E("v", lambda h: h.tensor_single_scalar(
                qtmp[:, 0:KD], modmat[:, src_off:src_off + KD], 0.0, OP.is_ge),
              waits=[("md", tok_mm_d)])
            FLUSHV()
            E("v", lambda h: h.tensor_scalar(qtmp[:, 2 * KD:3 * KD], qtmp[:, 0:KD],
                                             2.0, -1.0, OP.mult, OP.add))
            ta = E("a", lambda h: h.activation(
                qtmp[:, KD:2 * KD], modmat[:, src_off:src_off + KD], AF.Abs),
                waits=[("md", tok_mm_d), ("v", prev_tok)])
            FLUSHA()
            ta = FLUSHA()
            E("v", lambda h: h.tensor_scalar(qtmp[:, 3 * KD:4 * KD],
                                             qtmp[:, KD:2 * KD],
                                             0.25, None, OP.max),
              waits=[("a", ta)])
            FLUSHV()
            return E("v", lambda h, dst=dst: h.tensor_mul(
                dst[:, :], qtmp[:, 2 * KD:3 * KD], qtmp[:, 3 * KD:4 * KD]))

        tok_ga = emit_clamp(32, ga_cl, 0)
        FLUSHV()
        tok_gb = emit_clamp(80, gb_cl, tok_ga)
        FLUSHV()
        E("v", lambda h: h.tensor_scalar(gq[:, :], gb_cl[:, :], scol[:, 0:1], None,
                                         OP.mult), waits=[("v", tok_scol)])
        FLUSHV()
        E("v", lambda h: h.tensor_mul(gbb[:, :], gb_cl[:, :], fbias_sb[:, :]))
        tok_gbb = FLUSHV()
        tok_pro_v = cnt["v"]
        tok_pro_a = cnt["a"]
        tok_pro_t = cnt["t"]

        tok_xf_free_v = tok_pro_v
        tok_hb_free_t = tok_pro_t
        tok_s1_free_a = tok_pro_a
        tok_s1_free_v = tok_pro_v
        tok_s2_free_v = tok_pro_v
        tok_m_free = ("v", tok_pro_v)
        tok_i_free = ("v", tok_pro_v)
        tok_b0_free = ("v", tok_pro_v)
        tok_b1_free = ("v", tok_pro_v)
        tok_f_free_a = [tok_pro_a, tok_pro_a]
        last_scan_par = {0: tok_af, 1: tok_af}
        last_idmm_par = {i: tok_pro_t for i in range(4)}
        idmm_k_tok = {}
        tok_wqb_rd = [0, 0]
        tok_out_dma = [0, 0]
        add_hist = [0] * 8
        xd_grp = [0] * 8
        tok_x = 0

        carry_v = carry[:, :].rearrange("p (n k) -> p n k", n=NN)

        for m in range(NM_RUN):
            c0 = m * T
            if m == 0:
                for k in range(KD):
                    w = []
                    if k == 0:
                        w = [("v", tok_xf_free_v),
                             ("t", tok_wm_t[0]), ("t", tok_wm_t[1]),
                             ("t", tok_wm_t[2]), ("t", tok_wm_t[3]),
                             ("wm0", tok_wm_full[0]), ("wm1", tok_wm_full[1]),
                             ("wm2", tok_wm_full[2]), ("wm3", tok_wm_full[3])]
                    E("s", lambda h, k=k, c0=c0: h.dma_start(
                        out=xf[:, k * T:(k + 1) * T],
                        in_=xT[k * 128:(k + 1) * 128, c0:c0 + T]),
                        waits=w, bump="xd")
                tok_x = cnt["xd"]
                ktok = lambda k: [("xd", tok_x)]
            else:
                ktok = lambda k, xd=cnt["xd"]: (
                    [("xd", xd)] if k == 0 else [])

            def emit_ln_finish(onepc, svec_off, tt1, tt2):
                nonlocal tok_s1_free_a, tok_s1_free_v, tok_s2_free_v
                nonlocal tok_m_free, tok_i_free, tok_hb_free_t
                tok_s1_free_v = E("v", lambda h: h.tensor_scalar(
                    mrow[0:1, :], ps_s1[0:1, :], 1.0 / D, None, OP.mult),
                    waits=[("t", tt1)], war=True)
                tok_s2_free_v = E("v", lambda h: h.tensor_scalar(
                    e2row[0:1, :], ps_s2[0:1, :], 1.0 / D, None, OP.mult),
                    waits=[("t", tt2)], war=True)
                E("v", lambda h: h.tensor_mul(vrow[0:1, :], mrow[0:1, :],
                                              mrow[0:1, :]))
                E("v", lambda h: h.tensor_sub(e2row[0:1, :], e2row[0:1, :],
                                              vrow[0:1, :]))
                tv = E("v", lambda h: h.tensor_scalar(vrow[0:1, :], e2row[0:1, :],
                                                      LN_EPS, None, OP.add))
                tln = E("a", lambda h: h.activation(isrow[0:1, :], vrow[0:1, :],
                                                    AF.Ln), waits=[("v", tv)])
                tlh = E("v", lambda h: h.tensor_scalar(
                    vrow[0:1, :], isrow[0:1, :], -0.5, None, OP.mult),
                    waits=[("a", tln)])
                ti = E("a", lambda h: h.activation(isrow[0:1, :], vrow[0:1, :],
                                                   AF.Exp),
                       waits=[("v", tlh)])
                tmb = E("v", lambda h: h.tensor_copy(m_bf[0:1, :], mrow[0:1, :]))
                tib = E("v", lambda h: h.tensor_copy(is_bf[0:1, :], isrow[0:1, :]),
                        waits=[("a", ti)])
                trm = E("t", lambda h: h.matmul(ps_m[:, :], ones1b[:, :],
                                                m_bf[0:1, :], start=True, stop=True),
                        waits=[("v", tmb), tok_m_free])
                tri = E("t", lambda h: h.matmul(ps_i[:, :], ones1b[:, :],
                                                is_bf[0:1, :], start=True, stop=True),
                        waits=[("v", tib), tok_i_free])
                tmr = E("v", lambda h: h.tensor_copy(mrep_sb[:, :], ps_m[:, :]),
                        waits=[("t", trm)], war=True)
                tir = E("v", lambda h: h.tensor_copy(isrep_sb[:, :], ps_i[:, :]),
                        waits=[("t", tri)], war=True)
                tok_m_free = ("v", tmr)
                tok_i_free = ("v", tir)
                tout = 0
                for k in range(KD):
                    E("v", lambda h, k=k: h.tensor_sub(
                        t1f[:, :], xf[:, k * T:(k + 1) * T], mrep_sb[:, :]))
                    E("v", lambda h, k=k, onepc=onepc: h.scalar_tensor_tensor(
                        t2b[:, :], t1f[:, :], onepc[:, k:k + 1],
                        isrep_sb[:, :], OP.mult, OP.mult))
                    tout = E("v", lambda h, k=k, svec_off=svec_off: h.tensor_scalar(
                        hb[:, k * T:(k + 1) * T], t2b[:, :],
                        modmat[:, svec_off + k:svec_off + k + 1], None, OP.add))
                return tout

            ta_last = 0
            for k in range(KD):
                w = ktok(k)
                if k == 0:
                    w = w + [("t", tok_hb_free_t)]
                    if m == 0:
                        w.append(("v", tok_xf_free_v))
                ta_last = E("a", lambda h, k=k: h.activation(
                    hb[:, k * T:(k + 1) * T], xf[:, k * T:(k + 1) * T],
                    AF.Square), waits=w)
            tt1 = 0
            for k in range(KD):
                w = ktok(k)
                if k == 0:
                    w = w + [("v", tok_s1_free_v), ("a", tok_s1_free_a)]
                tt1 = E("t", lambda h, k=k: h.matmul(
                    ps_s1[0:1, :], ones128f[:, :], xf[:, k * T:(k + 1) * T],
                    start=(k == 0), stop=(k == KD - 1)), waits=w)
            tt2 = 0
            for k in range(KD):
                w = [("a", ta_last), ("v", tok_s2_free_v)] if k == 0 else []
                tt2 = E("t", lambda h, k=k: h.matmul(
                    ps_s2[0:1, :], ones128b[:, :], hb[:, k * T:(k + 1) * T],
                    start=(k == 0), stop=(k == KD - 1)), waits=w)
            tok_h = emit_ln_finish(onepca, 0, tt1, tt2)

            txp = 0
            for k in range(KD):
                w = []
                if k == 0:
                    w = [("v", tok_h), ("a", tok_f_free_a[0])]
                    if m == 0:
                        w.append(("v", tok_wx))
                txp = E("t", lambda h, k=k: h.matmul(
                    ps_f0[0:96, :], wx_sb[:, k * 96:(k + 1) * 96],
                    hb[:, k * T:(k + 1) * T],
                    start=(k == 0), stop=(k == KD - 1)), waits=w)
            ta_xp = E("a", lambda h: h.activation(xprj[:, :], ps_f0[0:96, :],
                                                  AF.Copy), waits=[("t", txp)])
            tok_f_free_a[0] = ta_xp
            tok_sp = []
            prev_sp = 0
            for k in range(KD):
                w = [("a", prev_sp)]
                if k == 0:
                    w = [("a", ta_xp), ("v", tok_s1_free_v)]
                    if m == 0:
                        w.append(("v", tok_wdt))
                tdt = E("t", lambda h, k=k: h.matmul(
                    ps_s1[:, :], wdt_sb[:, k * 128:(k + 1) * 128], xprj[0:RD, :],
                    start=True, stop=True), waits=w)
                tex = E("a", lambda h, k=k: h.activation(
                    t1f[:, :], ps_s1[:, :], AF.Exp,
                    bias=bdt_sb[:, k:k + 1]), waits=[("t", tdt)])
                DRAIN("a")
                prev_sp = E("a", lambda h, k=k: h.activation(
                    dtb[:, k * T:(k + 1) * T], t1f[:, :], AF.Ln,
                    bias=ones128f[:, 0:1]))
                tok_sp.append(prev_sp)
            tok_s1_free_a = tok_sp[-1]
            for k in range(KD):
                E("v", lambda h, k=k: h.tensor_mul(
                    vb[:, k * T:(k + 1) * T], dtb[:, k * T:(k + 1) * T],
                    hb[:, k * T:(k + 1) * T]), waits=[("a", tok_sp[k])])

            tok_bc = 0
            for n in range(NN):
                p = n % 2
                psb = ps_b0 if p == 0 else ps_m
                psc = ps_b1 if p == 0 else ps_i
                wb = [("a", ta_xp), (tok_b0_free if p == 0 else tok_m_free)]
                tb = E("t", lambda h, n=n, psb=psb: h.matmul(
                    psb[:, :], onehotm[:, n * 128:(n + 1) * 128], xprj[:, :],
                    start=True, stop=True), waits=wb)
                wc = [(tok_b1_free if p == 0 else tok_i_free)]
                tc = E("t", lambda h, n=n, psc=psc: h.matmul(
                    psc[:, :], onehotm[:, (16 + n) * 128:(17 + n) * 128],
                    xprj[:, :], start=True, stop=True), waits=wc)
                tbr = E("a", lambda h, n=n, psb=psb: h.activation(
                    brepall[:, n * WSEG + 1:n * WSEG + 1 + T], psb[:, :],
                    AF.Copy), waits=[("t", tb)])
                tcr = E("a", lambda h, n=n, psc=psc: h.activation(
                    crepall[:, n * WSEG + 1:n * WSEG + 1 + T], psc[:, :],
                    AF.Copy), waits=[("t", tc)])
                if p == 0:
                    tok_b0_free, tok_b1_free = ("a", tbr), ("a", tcr)
                else:
                    tok_m_free, tok_i_free = ("a", tbr), ("a", tcr)
                tok_bc = tcr

            pend_idmm = None
            pend_epi = None
            pend_sq = None
            ltt1 = ltt2 = 0

            def emit_idmm(pend, vtok):
                nonlocal tok_f_free_a
                (kk, gg, sbase) = pend
                psf = ps_f0 if kk % 2 == 0 else ps_f1
                tokm = 0
                for j in range(4):
                    w = []
                    if j == 0:
                        w = [("v", vtok)]
                        if gg == 0:
                            w.append(("a", tok_f_free_a[kk % 2]))
                    tokm = E("t", lambda h, j=j, psf=psf, sbase=sbase: h.matmul(
                        psf[:, :], identb[:, :],
                        sb2[:, sbase + j * WSEG + 1:sbase + j * WSEG + 1 + T],
                        start=(gg == 0 and j == 0), stop=(gg == 3 and j == 3)),
                        waits=w)
                last_idmm_par[(kk * 4 + gg) % 4] = tokm
                if gg == 3:
                    idmm_k_tok[kk] = tokm

            def emit_epiV(kk):
                nonlocal tok_f_free_a
                psf = ps_f0 if kk % 2 == 0 else ps_f1
                ys = yac2[:, (kk % 2) * T:(kk % 2 + 1) * T]
                ev = E("a", lambda h, psf=psf, ys=ys: h.activation(
                    ys, psf[:, :], AF.Copy), waits=[("t", idmm_k_tok[kk])])
                tok_f_free_a[kk % 2] = ev
                E("v", lambda h, kk=kk, ys=ys: h.scalar_tensor_tensor(
                    ys, hb[:, kk * T:(kk + 1) * T], dvec_sb[:, kk:kk + 1],
                    ys, OP.mult, OP.add), waits=[("a", ev)])
                wga = [("v", tok_ga)] if (m == 0 and kk == 0) else []
                tga = E("v", lambda h, kk=kk, ys=ys: h.scalar_tensor_tensor(
                    xf[:, kk * T:(kk + 1) * T], ys, ga_cl[:, kk:kk + 1],
                    xf[:, kk * T:(kk + 1) * T], OP.mult, OP.add), waits=wga)
                return tga

            def emit_epiT1(kk, x2pub):
                nonlocal ltt1
                w1 = [("v", x2pub)]
                if kk == 0:
                    w1 += [("v", tok_s1_free_v), ("a", tok_s1_free_a)]
                ltt1 = E("t", lambda h, kk=kk: h.matmul(
                    ps_s1[0:1, :], ones128f[:, :], xf[:, kk * T:(kk + 1) * T],
                    start=(kk == 0), stop=(kk == KD - 1)), waits=w1)

            def emit_epiSq(kk, x2pub):
                nonlocal ltt2
                tsq = E("a", lambda h, kk=kk: h.activation(
                    hb[:, kk * T:(kk + 1) * T], xf[:, kk * T:(kk + 1) * T],
                    AF.Square), waits=[("v", x2pub)])
                w2 = [("a", tsq)]
                if kk == 0:
                    w2.append(("v", tok_s2_free_v))
                ltt2 = E("t", lambda h, kk=kk: h.matmul(
                    ps_s2[0:1, :], ones128b[:, :], hb[:, kk * T:(kk + 1) * T],
                    start=(kk == 0), stop=(kk == KD - 1)), waits=w2)

            for G in range(64):
                k, g = G // 4, G % 4
                slot = G % 2
                slot4 = G % 4
                base = slot4 * WGRP
                dbase = slot * WGRP
                n0 = 4 * g
                for j in range(4):
                    n = n0 + j
                    w = [("v", last_scan_par[slot])] if j == 0 else []
                    E("a", lambda h, k=k, n=n, dbase=dbase, j=j: h.activation(
                        da2[:, dbase + j * WSEG + 1:dbase + j * WSEG + 1 + T],
                        dtb[:, k * T:(k + 1) * T], AF.Exp,
                        scale=a_sb[:, k * NN + n:k * NN + n + 1]),
                        waits=w, defer=True)
                exp_tok = cnt["a"]
                sb2v = sb2[:, base:base + WGRP].rearrange(
                    "p (s w) -> p s w", s=4)
                w = [("t", last_idmm_par[slot4])]
                if G == 0:
                    w.append(("a", tok_bc))
                E("v", lambda h, sb2v=sb2v, n0=n0, k=k: h.tensor_copy(
                    sb2v[:, :, 0:1], carry_v[:, n0:n0 + 4, k:k + 1]), waits=w)
                for j in range(4):
                    n = n0 + j
                    E("v", lambda h, k=k, n=n, base=base, j=j: h.tensor_mul(
                        sb2[:, base + j * WSEG + 1:base + j * WSEG + 1 + T],
                        vb[:, k * T:(k + 1) * T],
                        brepall[:, n * WSEG + 1:n * WSEG + 1 + T]))
                tok_scan = E("v", lambda h, base=base, dbase=dbase:
                             h.tensor_tensor_scan(
                                 sb2[:, base:base + WGRP],
                                 da2[:, dbase:dbase + WGRP],
                                 sb2[:, base:base + WGRP], 0.0,
                                 OP.mult, OP.add),
                             waits=[("a", exp_tok)], war=True)
                last_scan_par[slot] = tok_scan
                E("v", lambda h, sb2v=sb2v, n0=n0, k=k: h.tensor_copy(
                    carry_v[:, n0:n0 + 4, k:k + 1], sb2v[:, :, T:T + 1]))
                E("v", lambda h, base=base, n0=n0: h.tensor_mul(
                    sb2[:, base:base + WGRP], sb2[:, base:base + WGRP],
                    crepall[:, n0 * WSEG:(n0 + 4) * WSEG]))
                if pend_idmm is not None:
                    emit_idmm(pend_idmm, tok_scan)
                    pend_idmm = None
                if pend_sq is not None:
                    emit_epiSq(*pend_sq)
                    pend_sq = None
                if pend_epi is not None:
                    emit_epiT1(pend_epi[0], tok_scan)
                    pend_sq = (pend_epi[0], tok_scan)
                    pend_epi = None
                pend_idmm = (k, g, base)
                if G % 4 == 2 and G >= 6:
                    kk = G // 4 - 1
                    tga = emit_epiV(kk)
                    pend_epi = (kk, tga)

            t_tail = FLUSHV()
            emit_idmm(pend_idmm, t_tail)
            pend_idmm = None
            if pend_sq is not None:
                emit_epiSq(*pend_sq)
                pend_sq = None
            if pend_epi is not None:
                emit_epiT1(pend_epi[0], t_tail)
                emit_epiSq(pend_epi[0], t_tail)
                pend_epi = None
            tga15 = emit_epiV(KD - 1)
            emit_epiT1(KD - 1, tga15)
            emit_epiSq(KD - 1, tga15)

            tok_h2 = emit_ln_finish(onepcb, 48, ltt1, ltt2)
            tok_hb_free_t = txp

            for gj in range(8):
                pg = gj % 2
                te1_last = 0
                for jj in range(2):
                    j = 2 * gj + jj
                    p = j % 2
                    psf = ps_f0 if p == 0 else ps_f1
                    col0 = p * KD * 128
                    w = [("t", tok_wqb_rd[p])]
                    if m == 0:
                        w += [("qw0", cnt["qw0"]), ("qw1", cnt["qw1"])]
                    E("g", lambda h, j=j, col0=col0: h.dma_start(
                        out=wqb[:, col0:col0 + KD * 128],
                        in_=bass.AP(wq_h, j * 128 * KD * 128,
                                    [[KD * 128, 128], [1, KD * 128]])),
                        waits=w, bump="wq%d" % p)
                    tok_wq_dma = cnt["wq%d" % p]
                    tmmf = 0
                    for k in range(KD):
                        w = []
                        if k == 0:
                            w = [("wq%d" % p, tok_wq_dma), ("v", tok_h2),
                                 ("a", tok_f_free_a[p])]
                        tmmf = E("t", lambda h, k=k, psf=psf, col0=col0: h.matmul(
                            psf[:, :], wqb[:, col0 + k * 128:col0 + (k + 1) * 128],
                            hb[:, k * T:(k + 1) * T],
                            start=(k == 0), stop=(k == KD - 1)), waits=w)
                    tok_wqb_rd[p] = tmmf
                    we = [("t", tmmf)]
                    if jj == 0 and add_hist[(gj - 2) % 8] and gj >= 2:
                        we.append(("v", add_hist[gj - 2]))
                    if m == 0 and j == 0:
                        we.append(("v", tok_gbb))
                    dst = e1b[:, pg * 1024 + jj * T:pg * 1024 + (jj + 1) * T]
                    te1_last = E("a", lambda h, j=j, psf=psf, dst=dst: h.activation(
                        dst, psf[:, :], AF.Identity,
                        bias=gbb[:, j:j + 1], scale=gq[:, j:j + 1]), waits=we)
                    tok_f_free_a[p] = te1_last
                tadd = E("v", lambda h, gj=gj, pg=pg: h.tensor_add(
                    outst[:, pg * 1024:(pg + 1) * 1024],
                    xf[:, gj * 1024:(gj + 1) * 1024],
                    e1b[:, pg * 1024:(pg + 1) * 1024]),
                    waits=[("a", te1_last), ("od%d" % pg, tok_out_dma[pg])])
                add_hist[gj] = tadd
                E("s", lambda h, gj=gj, pg=pg, c0=c0: h.dma_start(
                    out=yT[gj * 256:(gj + 1) * 256, c0:c0 + T].rearrange(
                        "(a p) c -> p a c", p=128),
                    in_=outst[:, pg * 1024:(pg + 1) * 1024].rearrange(
                        "p (a c) -> p a c", a=2)),
                    waits=[("v", tadd)], bump="od%d" % pg)
                tok_out_dma[pg] = cnt["od%d" % pg]
                if m < NM_RUN - 1:
                    c0n = (m + 1) * T
                    E("s", lambda h, gj=gj, c0n=c0n: h.dma_start(
                        out=xf[:, gj * 1024:(gj + 1) * 1024].rearrange(
                            "p (a c) -> p a c", a=2),
                        in_=xT[gj * 256:(gj + 1) * 256, c0n:c0n + T].rearrange(
                            "(a p) c -> p a c", p=128)),
                        waits=[("v", tadd)], bump="xd")
            tok_hb_free_t = max(tok_wqb_rd)

        E("s", lambda h: None,
          waits=[("od0", cnt["od0"]), ("od1", cnt["od1"])], bump=None)

        def runner(tasks, eng_name):
            compute = eng_name in ("vector", "scalar", "tensor")

            def f(h):
                pending = None
                for (waits, fn, bump, war, defer) in tasks:
                    if pending is not None and waits:
                        h.drain().then_inc(*pending)
                        pending = None
                    for (sk, val) in waits:
                        h.wait_ge(sems[SEM_OF_CNT[sk]], val)
                    if pending is not None and bump is not None and not defer:
                        h.drain().then_inc(*pending)
                        pending = None
                    instr = fn(h)
                    if instr is None:
                        continue
                    if pending is not None:
                        instr.then_inc(*pending)
                        pending = None
                    if bump is None:
                        continue
                    if defer:
                        pending = (sems[bump], STEP[bump])
                        continue
                    needs_drain = (compute and not war and
                                   cnt_tok[id(fn)] in waited[bump])
                    if needs_drain:
                        h.drain().then_inc(sems[bump], STEP[bump])
                    else:
                        instr.then_inc(sems[bump], STEP[bump])
                if pending is not None:
                    h.drain().then_inc(*pending)
            return f

        with nc.Block() as block:
            block.sync(runner(script["sync"], "sync"))
            block.gpsimd(runner(script["gpsimd"], "gpsimd"))
            block.tensor(runner(script["tensor"], "tensor"))
            block.vector(runner(script["vector"], "vector"))
            block.scalar(runner(script["scalar"], "scalar"))

    return nc


def kernel(L0_attn_x, L0_attn_W_x, L0_attn_W_dt, L0_attn_b_dt, L0_attn_A_log,
           L0_attn_D, L0_ffn_weight, L0_ffn_bias, L0_c, L0_wmod):
    nc = build_nc()

    wxT = np.ascontiguousarray(L0_attn_W_x.T)
    wdtT = np.ascontiguousarray(L0_attn_W_dt.T)
    bdt_a = np.ascontiguousarray(L0_attn_b_dt.reshape(KD, 128).T)
    alog_a = np.ascontiguousarray(
        L0_attn_A_log.reshape(KD, 128, NN).transpose(1, 0, 2).reshape(128, KD * NN))
    dvec_a = np.ascontiguousarray(L0_attn_D.reshape(KD, 128).T)
    fbias_a = np.ascontiguousarray(L0_ffn_bias.reshape(KD, 128).T)
    wT_a = np.ascontiguousarray(L0_ffn_weight.T)
    wmod_a = np.ascontiguousarray(
        L0_wmod.reshape(KD, 128, NFC, 128).transpose(2, 1, 0, 3)
        .reshape(NFC * 128, KD * 128))

    import ml_dtypes
    onehot = np.zeros((96, 32 * 128), dtype=ml_dtypes.bfloat16)
    ident = np.eye(128, dtype=ml_dtypes.bfloat16)
    for j in range(32):
        onehot[64 + j, j * 128:(j + 1) * 128] = 1.0

    in_maps = []
    for b in range(NB):
        in_maps.append({
            "xT": np.ascontiguousarray(L0_attn_x[b].T),
            "wxT": wxT, "wdtT": wdtT, "bdt": bdt_a, "alog": alog_a,
            "dvec": dvec_a, "fbias": fbias_a, "wT": wT_a,
            "cb_in": np.ascontiguousarray(L0_c[b].reshape(KD, 128).T),
            "wmod_r": wmod_a, "onehot_bc": onehot, "ident_bc": ident,
        })
    res = run_bass_kernel_spmd(nc, in_maps, core_ids=list(range(NB)))
    out = np.stack([np.ascontiguousarray(r["yT"].T) for r in res.results], axis=0)
    return out.astype(np.float32)


if __name__ == "__main__":
    build_nc()
    print("build ok")


# version 26 (final)
# speedup vs baseline: 1.4725x; 1.0098x over previous
import numpy as np
import contextlib
import concourse.bass as bass
import concourse.mybir as mybir
from concourse.bass_utils import run_bass_kernel_spmd

F32 = mybir.dt.float32
BF16 = mybir.dt.bfloat16
AF = mybir.ActivationFunctionType
OP = mybir.AluOpType
AX = mybir.AxisListType

D, L, NN, RD = 2048, 4096, 16, 64
NB = 8
KD = D // 128
T = 512
NM = L // T
NFC = 96
LN_EPS = 1e-5
Q_EPS = 1e-10
WSEG = T + 1
WGRP = 4 * WSEG

ENGS = ("sync", "gpsimd", "tensor", "vector", "scalar")
NM_RUN = NM
ENG_OF = dict(v="vector", a="scalar", t="tensor", g="gpsimd", s="sync")
DMA_SEMS = ("sw", "ws0", "ws1", "qw0", "qw1", "wm0", "wm1", "wm2", "wm3",
            "md", "xd", "xg", "wq0", "wq1", "wq2", "wq3", "od0", "od1")
STEP = dict(v=1, a=1, t=1, g=1, s=1, **{k: 16 for k in DMA_SEMS})
SEM_OF_CNT = dict(v="v", a="a", t="t", **{k: k for k in DMA_SEMS})


def build_nc():
    nc = bass.Bass("TRN2", target_bir_lowering=False)

    xT = nc.dram_tensor("xT", [D, L], F32, kind="ExternalInput").ap()
    wxT = nc.dram_tensor("wxT", [D, 96], F32, kind="ExternalInput").ap()
    wdtT = nc.dram_tensor("wdtT", [RD, D], F32, kind="ExternalInput").ap()
    bdt = nc.dram_tensor("bdt", [128, KD], F32, kind="ExternalInput").ap()
    alog = nc.dram_tensor("alog", [128, KD * NN], F32, kind="ExternalInput").ap()
    dvec = nc.dram_tensor("dvec", [128, KD], F32, kind="ExternalInput").ap()
    fbias = nc.dram_tensor("fbias", [128, KD], F32, kind="ExternalInput").ap()
    wT = nc.dram_tensor("wT", [D, D], F32, kind="ExternalInput").ap()
    cb_in = nc.dram_tensor("cb_in", [128, KD], F32, kind="ExternalInput").ap()
    wmod = nc.dram_tensor("wmod_r", [NFC * 128, KD * 128], F32,
                          kind="ExternalInput").ap()
    onehot_d = nc.dram_tensor("onehot_bc", [96, 32 * 128], BF16,
                              kind="ExternalInput").ap()
    ident_d = nc.dram_tensor("ident_bc", [128, 128], BF16,
                             kind="ExternalInput").ap()
    yT = nc.dram_tensor("yT", [D, L], F32, kind="ExternalOutput").ap()
    wq_h = nc.dram_tensor("wq_scratch", [D, D], BF16)
    wq_d = wq_h.ap()
    modscr_h = nc.dram_tensor("mod_scratch", [6 * D], F32)
    modscr = modscr_h.ap()

    script = {e: [] for e in ENGS}
    cnt = dict(v=0, a=0, t=0, **{k: 0 for k in DMA_SEMS})
    DEFAULT_BUMP = dict(v="v", a="a", t="t", g=None, s=None)

    waited = {k: set() for k in cnt}
    cnt_tok = {}

    def E(eng, fn, waits=(), bump="default", war=False, defer=False):
        if bump == "default":
            bump = DEFAULT_BUMP[eng]
        w = tuple((k, val) for (k, val) in waits if val and val > 0)
        for (k, val) in w:
            waited[k].add(val)
        tok = None
        if bump is not None:
            cnt[bump] += STEP[bump]
            tok = cnt[bump]
        cnt_tok[id(fn)] = tok
        script[ENG_OF[eng]].append((w, fn, bump, war, defer))
        return tok

    def FLUSHV():
        return E("v", lambda h: h.tensor_copy(flsh[:, 32:64], flsh[:, 0:32]))

    def DRAIN(eng):
        E(eng, lambda h: h.drain(), bump=None)

    def FLUSHA():
        return E("a", lambda h: h.activation(flsh[:, 32:64], flsh[:, 0:32],
                                             AF.Copy))

    stack = contextlib.ExitStack()
    with stack:
        sems = {}
        for k in ("v", "a", "t") + DMA_SEMS:
            sems[k] = stack.enter_context(nc.semaphore("sem_" + k))

        def ST(name, shape, dt):
            return stack.enter_context(nc.sbuf_tensor(name, shape, dt))

        def PT(name, shape, dt):
            return stack.enter_context(nc.psum_tensor(name, shape, dt))

        xf = ST("xf", [128, KD * T], F32)
        hb = ST("hb", [128, KD * T], BF16)
        dtb = ST("dtb", [128, KD * T], BF16)
        vb = ST("vb", [128, KD * T], BF16)
        da2 = ST("da2", [128, 2 * WGRP], BF16)
        sb2 = ST("sb2", [128, 4 * WGRP], BF16)
        yac2 = ST("yac2", [128, 2 * T], BF16)
        brepall = ST("brepall", [128, NN * WSEG], BF16)
        crepall = ST("crepall", [128, NN * WSEG], BF16)
        identb = ST("identb", [128, 128], BF16)
        carry = ST("carry", [128, NN * KD], BF16)
        xprj = ST("xprj", [96, T], BF16)
        e1b = ST("e1b", [128, 2 * 1024], BF16)
        outst = ST("outst", [128, 2 * 1024], F32)
        wqb = ST("wqb", [128, 4 * KD * 128], BF16)
        t1f = ST("t1f", [128, T], F32)
        mrow = ST("mrow", [1, T], F32)
        e2row = ST("e2row", [1, T], F32)
        vrow = ST("vrow", [1, T], F32)
        isrow = ST("isrow", [1, T], F32)
        m_bf = ST("m_bf", [1, T], BF16)
        is_bf = ST("is_bf", [1, T], BF16)
        mrep_sb = ST("mrep_sb", [128, T], BF16)
        isrep_sb = ST("isrep_sb", [128, T], BF16)
        wx_sb = ST("wx_sb", [128, KD * 96], BF16)
        wdt_sb = ST("wdt_sb", [RD, D], BF16)
        a_sb = ST("a_sb", [128, KD * NN], F32)
        bdt_sb = ST("bdt_sb", [128, KD], F32)
        dvec_sb = ST("dvec_sb", [128, KD], F32)
        fbias_sb = ST("fbias_sb", [128, KD], F32)
        cbq = ST("cbq", [128, KD], BF16)
        modmat = ST("modmat", [128, 96], F32)
        onepca = ST("onepca", [128, KD], F32)
        ga_cl = ST("ga_cl", [128, KD], F32)
        onepcb = ST("onepcb", [128, KD], F32)
        gb_cl = ST("gb_cl", [128, KD], F32)
        gq = ST("gq", [128, KD], F32)
        gbb = ST("gbb", [128, KD], F32)
        invcol = ST("invcol", [128, 1], F32)
        scol = ST("scol", [128, 1], F32)
        sc_sc = ST("sc_sc", [1, 2], F32)
        invr = ST("invr", [1, 1], F32)
        abscol = ST("abscol", [128, KD], F32)
        abssum = ST("abssum", [128, 1], F32)
        modrow = ST("modrow", [1, 12 * 128], F32)
        ones128f = ST("ones128f", [128, 1], F32)
        ones128b = ST("ones128b", [128, 1], BF16)
        ones1f = ST("ones1f", [1, 128], F32)
        ones1b = ST("ones1b", [1, 128], BF16)
        onehotm = ST("onehotm", [96, 32 * 128], BF16)
        qtmp = ST("qtmp", [128, 4 * KD], F32)
        t2b = ST("t2b", [128, T], BF16)
        flsh = ST("flsh", [128, 64], F32)

        ps_s1 = PT("ps_s1", [128, T], F32)
        ps_s2 = PT("ps_s2", [128, T], F32)
        ps_m = PT("ps_m", [128, T], F32)
        ps_i = PT("ps_i", [128, T], F32)
        ps_b0 = PT("ps_b0", [128, T], F32)
        ps_b1 = PT("ps_b1", [128, T], F32)
        ps_f0 = PT("ps_f0", [128, T], F32)
        ps_f1 = PT("ps_f1", [128, T], F32)

        E("v", lambda h: h.memset(ones128f[:, :], 1.0))
        E("v", lambda h: h.memset(ones128b[:, :], 1.0))
        E("v", lambda h: h.memset(ones1f[:, :], 1.0))
        E("v", lambda h: h.memset(ones1b[:, :], 1.0))

        E("v", lambda h: h.memset(carry[:, :], 0.0))
        E("v", lambda h: h.memset(da2[:, :], 0.0))
        E("v", lambda h: h.memset(crepall[:, :], 0.0))

        E("g", lambda h: h.dma_start(
            out=xf[:, 0:KD * 96].rearrange("p (k c) -> p k c", k=KD),
            in_=wxT.rearrange("(k p) c -> p k c", p=128)), bump="sw")
        E("g", lambda h: h.dma_start(
            out=xf[0:RD, 2048:2048 + D], in_=wdtT[:, :]), bump="sw")
        E("g", lambda h: h.dma_start(
            out=xf[:, 4096:4096 + KD * NN], in_=alog[:, :]), bump="sw")
        E("g", lambda h: h.dma_start(out=bdt_sb[:, :], in_=bdt[:, :]), bump="sw")
        E("g", lambda h: h.dma_start(out=dvec_sb[:, :], in_=dvec[:, :]), bump="sw")
        E("g", lambda h: h.dma_start(out=fbias_sb[:, :], in_=fbias[:, :]),
          bump="sw")
        E("g", lambda h: h.dma_start(
            out=xf[:, 4624:4624 + KD], in_=cb_in[:, :]), bump="sw")
        E("g", lambda h: h.dma_start(out=onehotm[:, :], in_=onehot_d[:, :]),
          bump="sw")
        E("g", lambda h: h.dma_start(out=identb[:, :], in_=ident_d[:, :]),
          bump="sw")
        tok_sw = cnt["sw"]

        tok_wx = E("v", lambda h: h.tensor_copy(wx_sb[:, :], xf[:, 0:KD * 96]),
                   waits=[("sw", tok_sw)])
        tok_wdt = E("v", lambda h: h.tensor_copy(wdt_sb[:, :],
                                                 xf[0:RD, 2048:2048 + D]),
                    waits=[("sw", tok_sw)])
        tok_ae = E("a", lambda h: h.activation(xf[:, 4352:4352 + KD * NN],
                                               xf[:, 4096:4096 + KD * NN], AF.Exp),
                   waits=[("sw", tok_sw)])
        tok_a = E("v", lambda h: h.tensor_scalar(
            a_sb[:, :], xf[:, 4352:4352 + KD * NN], -1.0, None, OP.mult),
            waits=[("a", tok_ae)])
        tok_af = FLUSHV()
        tok_cs = E("a", lambda h: h.activation(qtmp[:, 0:KD],
                                               xf[:, 4624:4624 + KD], AF.Silu),
                   waits=[("sw", tok_sw)])
        tok_cb = E("v", lambda h: h.tensor_copy(cbq[:, :], qtmp[:, 0:KD]),
                   waits=[("a", tok_cs)])

        wst = [xf[:, 0:D], xf[:, D:2 * D]]
        tok_v_wst = [tok_wx, tok_wdt]
        tok_wst_d = [0, 0]
        for k in range(KD):
            p = k % 2
            tok_wst_d[p] = E("g" if p == 0 else "s", lambda h, k=k, p=p: h.dma_start(
                out=wst[p], in_=wT[k * 128:(k + 1) * 128, :]),
                waits=[("v", tok_v_wst[p])], bump="ws%d" % p)
            tok_v_wst[p] = E("v", lambda h, k=k, p=p: h.reduce_sum(
                abscol[:, k:k + 1], wst[p], axis=AX.X, apply_absolute_value=True),
                waits=[("ws%d" % p, tok_wst_d[p])])
        FLUSHV()
        E("v", lambda h: h.reduce_sum(abssum[:, :], abscol[:, :], axis=AX.X))
        tok_absc = FLUSHV()
        tok_sred = E("t", lambda h: h.matmul(ps_s1[0:1, 0:1], ones128f[:, :],
                                             abssum[:, :], start=True, stop=True),
                     waits=[("v", tok_absc)])
        E("v", lambda h: h.tensor_scalar(sc_sc[0:1, 0:1], ps_s1[0:1, 0:1],
                                         1.0 / (D * D), None, OP.mult),
          waits=[("t", tok_sred)])
        FLUSHV()
        E("v", lambda h: h.tensor_scalar(sc_sc[0:1, 1:2], sc_sc[0:1, 0:1],
                                         Q_EPS, None, OP.add))
        FLUSHV()
        E("v", lambda h: h.reciprocal(invr[0:1, :], sc_sc[0:1, 1:2]))
        tok_inv = FLUSHV()
        tok_bc1 = E("t", lambda h: h.matmul(ps_m[:, 0:1], ones1f[:, :],
                                            invr[0:1, :], start=True, stop=True),
                    waits=[("v", tok_inv)])
        tok_bc2 = E("t", lambda h: h.matmul(ps_i[:, 0:1], ones1f[:, :],
                                            sc_sc[0:1, 0:1], start=True, stop=True),
                    waits=[("v", tok_inv)])
        E("v", lambda h: h.tensor_copy(invcol[:, :], ps_m[:, 0:1]),
          waits=[("t", tok_bc1)])
        E("v", lambda h: h.tensor_copy(scol[:, :], ps_i[:, 0:1]),
          waits=[("t", tok_bc2)])
        tok_invc = FLUSHV()
        tok_scol = tok_invc

        qabs = [xf[:, 4096 + 0:4096 + D], xf[:, 6144:6144 + D]]
        qsgn = [hb[:, 0:D], hb[:, D:2 * D]]
        qmsk = [hb[:, 2 * D:3 * D], hb[:, 3 * D:4 * D]]
        wqt = [dtb[:, 0:D], dtb[:, D:2 * D]]
        tok_wq_wr = [0, 0]
        tok_v_wq = [0, 0]
        tok_a_abs = [0, 0]
        tok_msk_v = [0, 0]
        for k in range(KD):
            p = k % 2
            tok_wst_d[p] = E("g" if p == 0 else "s", lambda h, k=k, p=p: h.dma_start(
                out=wst[p], in_=wT[k * 128:(k + 1) * 128, :]),
                waits=[("v", tok_v_wst[p]), ("a", tok_a_abs[p])],
                bump="ws%d" % p)
            E("a", lambda h, p=p: h.activation(qsgn[p], wst[p], AF.Sign),
              waits=[("ws%d" % p, tok_wst_d[p]), ("v", tok_v_wq[p])])
            tok_a_abs[p] = E("a", lambda h, p=p: h.activation(
                qabs[p], wst[p], AF.Abs, scale=invcol[:, 0:1]),
                waits=[("v", tok_invc), ("v", tok_msk_v[p])])
            tok_msk_v[p] = E("v", lambda h, p=p: h.tensor_single_scalar(
                qmsk[p], qabs[p], 0.5, OP.is_ge),
                waits=[("a", tok_a_abs[p])], war=True)
            tok_v_wst[p] = tok_msk_v[p]
            tok_v_wq[p] = E("v", lambda h, p=p: h.tensor_mul(
                wqt[p], qsgn[p], qmsk[p]),
                waits=[("qw%d" % p, tok_wq_wr[p])])
            tok_wq_wr[p] = E("g", lambda h, k=k, p=p: h.dma_start(
                out=bass.AP(wq_h, k * 128,
                            [[KD * 128, 128], [128 * KD * 128, KD], [1, 128]]),
                in_=wqt[p].rearrange("p (j c) -> p j c", j=KD)),
                waits=[("v", tok_v_wq[p])], bump="qw%d" % p)
        tok_quant_v = cnt["v"]

        wmf = [xf[:, 0:D], xf[:, D:2 * D],
               xf[:, 2 * D:3 * D], xf[:, 3 * D:4 * D]]
        tok_wm_d = [0, 0, 0, 0]
        tok_wm_t = [0, 0, 0, 0]
        tok_mod_a = [0, 0]
        md_last = 0
        for fc in range(NFC):
            p = fc % 4
            pb = p % 2
            psmod = ps_f0 if pb == 0 else ps_f1
            s12 = fc % 12
            w = [("t", tok_wm_t[p])]
            if fc < 4:
                w.append(("v", tok_quant_v))
            tok_wm_d[p] = E("g" if pb == 0 else "s", lambda h, fc=fc, p=p: h.dma_start(
                out=wmf[p], in_=wmod[fc * 128:(fc + 1) * 128, :]),
                waits=w, bump="wm%d" % p)
            tmod = 0
            for k in range(KD):
                w = []
                if k == 0:
                    w = [("wm%d" % p, tok_wm_d[p]), ("a", tok_cs),
                         ("a", tok_mod_a[pb])]
                tmod = E("t", lambda h, p=p, k=k, psmod=psmod: h.matmul(
                    psmod[0:1, 0:128], qtmp[:, k:k + 1],
                    wmf[p][:, k * 128:(k + 1) * 128],
                    start=(k == 0), stop=(k == KD - 1)), waits=w)
            tok_wm_t[p] = tmod
            tok_mod_a[pb] = E("a", lambda h, s12=s12, psmod=psmod: h.activation(
                modrow[0:1, s12 * 128:(s12 + 1) * 128], psmod[0:1, 0:128],
                AF.Copy), waits=[("t", tmod), ("md", md_last)])
            if s12 == 11:
                b = fc // 12
                E("s", lambda h, b=b: h.dma_start(
                    out=modscr[b * 1536:(b + 1) * 1536].rearrange(
                        "(a b) -> a b", a=1),
                    in_=modrow[0:1, :]),
                    waits=[("a", tok_mod_a[1])], bump="md")
                md_last = cnt["md"]
        tok_wm_full = [cnt["wm%d" % i] for i in range(4)]

        tok_mm_d = E("s", lambda h: h.dma_start(
            out=modmat[:, :].rearrange("p (v k) -> p v k", v=6),
            in_=bass.AP(modscr_h, 0, [[1, 128], [D, 6], [128, KD]]),
            allow_slow_non_contiguous=True),
            waits=[("md", cnt["md"])], bump="md")

        E("v", lambda h: h.tensor_scalar(onepca[:, :], modmat[:, 16:32], 1.0, None,
                                         OP.add), waits=[("md", tok_mm_d)])
        E("v", lambda h: h.tensor_scalar(onepcb[:, :], modmat[:, 64:80], 1.0, None,
                                         OP.add))

        def emit_clamp(src_off, dst, prev_tok):
            E("v", lambda h: h.tensor_single_scalar(
                qtmp[:, 0:KD], modmat[:, src_off:src_off + KD], 0.0, OP.is_ge),
              waits=[("md", tok_mm_d)])
            FLUSHV()
            E("v", lambda h: h.tensor_scalar(qtmp[:, 2 * KD:3 * KD], qtmp[:, 0:KD],
                                             2.0, -1.0, OP.mult, OP.add))
            ta = E("a", lambda h: h.activation(
                qtmp[:, KD:2 * KD], modmat[:, src_off:src_off + KD], AF.Abs),
                waits=[("md", tok_mm_d), ("v", prev_tok)])
            FLUSHA()
            ta = FLUSHA()
            E("v", lambda h: h.tensor_scalar(qtmp[:, 3 * KD:4 * KD],
                                             qtmp[:, KD:2 * KD],
                                             0.25, None, OP.max),
              waits=[("a", ta)])
            FLUSHV()
            return E("v", lambda h, dst=dst: h.tensor_mul(
                dst[:, :], qtmp[:, 2 * KD:3 * KD], qtmp[:, 3 * KD:4 * KD]))

        tok_ga = emit_clamp(32, ga_cl, 0)
        FLUSHV()
        tok_gb = emit_clamp(80, gb_cl, tok_ga)
        FLUSHV()
        E("v", lambda h: h.tensor_scalar(gq[:, :], gb_cl[:, :], scol[:, 0:1], None,
                                         OP.mult), waits=[("v", tok_scol)])
        FLUSHV()
        E("v", lambda h: h.tensor_mul(gbb[:, :], gb_cl[:, :], fbias_sb[:, :]))
        tok_gbb = FLUSHV()
        tok_pro_v = cnt["v"]
        tok_pro_a = cnt["a"]
        tok_pro_t = cnt["t"]

        tok_xf_free_v = tok_pro_v
        tok_hb_free_t = tok_pro_t
        tok_s1_free_a = tok_pro_a
        tok_s1_free_v = tok_pro_v
        tok_s2_free_v = tok_pro_v
        tok_m_free = ("v", tok_pro_v)
        tok_i_free = ("v", tok_pro_v)
        tok_b0_free = ("v", tok_pro_v)
        tok_b1_free = ("v", tok_pro_v)
        tok_f_free_a = [tok_pro_a, tok_pro_a]
        last_scan_par = {0: tok_af, 1: tok_af}
        last_idmm_par = {i: tok_pro_t for i in range(4)}
        idmm_k_tok = {}
        tok_wqb_rd = [0, 0, 0, 0]
        tok_out_dma = [0, 0]
        add_hist = [0] * 8
        xd_grp = [0] * 8
        tok_x = 0

        carry_v = carry[:, :].rearrange("p (n k) -> p n k", n=NN)

        for m in range(NM_RUN):
            c0 = m * T
            if m == 0:
                for k in range(KD):
                    w = []
                    if k == 0:
                        w = [("v", tok_xf_free_v),
                             ("t", tok_wm_t[0]), ("t", tok_wm_t[1]),
                             ("t", tok_wm_t[2]), ("t", tok_wm_t[3]),
                             ("wm0", tok_wm_full[0]), ("wm1", tok_wm_full[1]),
                             ("wm2", tok_wm_full[2]), ("wm3", tok_wm_full[3])]
                    E("s", lambda h, k=k, c0=c0: h.dma_start(
                        out=xf[:, k * T:(k + 1) * T],
                        in_=xT[k * 128:(k + 1) * 128, c0:c0 + T]),
                        waits=w, bump="xd")
                tok_x = cnt["xd"]
                ktok = lambda k: [("xd", tok_x)]
            else:
                ktok = lambda k, xd=cnt["xd"]: (
                    [("xd", xd)] if k == 0 else [])

            def emit_ln_finish(onepc, svec_off, tt1, tt2):
                nonlocal tok_s1_free_a, tok_s1_free_v, tok_s2_free_v
                nonlocal tok_m_free, tok_i_free, tok_hb_free_t
                tok_s1_free_v = E("v", lambda h: h.tensor_scalar(
                    mrow[0:1, :], ps_s1[0:1, :], 1.0 / D, None, OP.mult),
                    waits=[("t", tt1)], war=True)
                tok_s2_free_v = E("v", lambda h: h.tensor_scalar(
                    e2row[0:1, :], ps_s2[0:1, :], 1.0 / D, None, OP.mult),
                    waits=[("t", tt2)], war=True)
                E("v", lambda h: h.tensor_mul(vrow[0:1, :], mrow[0:1, :],
                                              mrow[0:1, :]))
                E("v", lambda h: h.tensor_sub(e2row[0:1, :], e2row[0:1, :],
                                              vrow[0:1, :]))
                tv = E("v", lambda h: h.tensor_scalar(vrow[0:1, :], e2row[0:1, :],
                                                      LN_EPS, None, OP.add))
                tln = E("a", lambda h: h.activation(isrow[0:1, :], vrow[0:1, :],
                                                    AF.Ln), waits=[("v", tv)])
                tlh = E("v", lambda h: h.tensor_scalar(
                    vrow[0:1, :], isrow[0:1, :], -0.5, None, OP.mult),
                    waits=[("a", tln)])
                ti = E("a", lambda h: h.activation(isrow[0:1, :], vrow[0:1, :],
                                                   AF.Exp),
                       waits=[("v", tlh)])
                tmb = E("v", lambda h: h.tensor_copy(m_bf[0:1, :], mrow[0:1, :]))
                tib = E("v", lambda h: h.tensor_copy(is_bf[0:1, :], isrow[0:1, :]),
                        waits=[("a", ti)])
                trm = E("t", lambda h: h.matmul(ps_m[:, :], ones1b[:, :],
                                                m_bf[0:1, :], start=True, stop=True),
                        waits=[("v", tmb), tok_m_free])
                tri = E("t", lambda h: h.matmul(ps_i[:, :], ones1b[:, :],
                                                is_bf[0:1, :], start=True, stop=True),
                        waits=[("v", tib), tok_i_free])
                tmr = E("v", lambda h: h.tensor_copy(mrep_sb[:, :], ps_m[:, :]),
                        waits=[("t", trm)], war=True)
                tir = E("v", lambda h: h.tensor_copy(isrep_sb[:, :], ps_i[:, :]),
                        waits=[("t", tri)], war=True)
                tok_m_free = ("v", tmr)
                tok_i_free = ("v", tir)
                tout = 0
                for k in range(KD):
                    E("v", lambda h, k=k: h.tensor_sub(
                        t1f[:, :], xf[:, k * T:(k + 1) * T], mrep_sb[:, :]))
                    E("v", lambda h, k=k, onepc=onepc: h.scalar_tensor_tensor(
                        t2b[:, :], t1f[:, :], onepc[:, k:k + 1],
                        isrep_sb[:, :], OP.mult, OP.mult))
                    tout = E("v", lambda h, k=k, svec_off=svec_off: h.tensor_scalar(
                        hb[:, k * T:(k + 1) * T], t2b[:, :],
                        modmat[:, svec_off + k:svec_off + k + 1], None, OP.add))
                return tout

            ta_last = 0
            for k in range(KD):
                w = ktok(k)
                if k == 0:
                    w = w + [("t", tok_hb_free_t)]
                    if m == 0:
                        w.append(("v", tok_xf_free_v))
                ta_last = E("a", lambda h, k=k: h.activation(
                    hb[:, k * T:(k + 1) * T], xf[:, k * T:(k + 1) * T],
                    AF.Square), waits=w)
            tt1 = 0
            for k in range(KD):
                w = ktok(k)
                if k == 0:
                    w = w + [("v", tok_s1_free_v), ("a", tok_s1_free_a)]
                tt1 = E("t", lambda h, k=k: h.matmul(
                    ps_s1[0:1, :], ones128f[:, :], xf[:, k * T:(k + 1) * T],
                    start=(k == 0), stop=(k == KD - 1)), waits=w)
            tt2 = 0
            for k in range(KD):
                w = [("a", ta_last), ("v", tok_s2_free_v)] if k == 0 else []
                tt2 = E("t", lambda h, k=k: h.matmul(
                    ps_s2[0:1, :], ones128b[:, :], hb[:, k * T:(k + 1) * T],
                    start=(k == 0), stop=(k == KD - 1)), waits=w)
            tok_h = emit_ln_finish(onepca, 0, tt1, tt2)

            txp = 0
            for k in range(KD):
                w = []
                if k == 0:
                    w = [("v", tok_h), ("a", tok_f_free_a[0])]
                    if m == 0:
                        w.append(("v", tok_wx))
                txp = E("t", lambda h, k=k: h.matmul(
                    ps_f0[0:96, :], wx_sb[:, k * 96:(k + 1) * 96],
                    hb[:, k * T:(k + 1) * T],
                    start=(k == 0), stop=(k == KD - 1)), waits=w)
            ta_xp = E("a", lambda h: h.activation(xprj[:, :], ps_f0[0:96, :],
                                                  AF.Copy), waits=[("t", txp)])
            tok_f_free_a[0] = ta_xp
            tok_sp = []
            prev_sp = 0
            for k in range(KD):
                w = [("a", prev_sp)]
                if k == 0:
                    w = [("a", ta_xp), ("v", tok_s1_free_v)]
                    if m == 0:
                        w.append(("v", tok_wdt))
                tdt = E("t", lambda h, k=k: h.matmul(
                    ps_s1[:, :], wdt_sb[:, k * 128:(k + 1) * 128], xprj[0:RD, :],
                    start=True, stop=True), waits=w)
                tex = E("a", lambda h, k=k: h.activation(
                    t1f[:, :], ps_s1[:, :], AF.Exp,
                    bias=bdt_sb[:, k:k + 1]), waits=[("t", tdt)])
                DRAIN("a")
                prev_sp = E("a", lambda h, k=k: h.activation(
                    dtb[:, k * T:(k + 1) * T], t1f[:, :], AF.Ln,
                    bias=ones128f[:, 0:1]))
                tok_sp.append(prev_sp)
            tok_s1_free_a = tok_sp[-1]
            for k in range(KD):
                E("v", lambda h, k=k: h.tensor_mul(
                    vb[:, k * T:(k + 1) * T], dtb[:, k * T:(k + 1) * T],
                    hb[:, k * T:(k + 1) * T]), waits=[("a", tok_sp[k])])

            tok_bc = 0
            for n in range(NN):
                p = n % 2
                psb = ps_b0 if p == 0 else ps_m
                psc = ps_b1 if p == 0 else ps_i
                wb = [("a", ta_xp), (tok_b0_free if p == 0 else tok_m_free)]
                tb = E("t", lambda h, n=n, psb=psb: h.matmul(
                    psb[:, :], onehotm[:, n * 128:(n + 1) * 128], xprj[:, :],
                    start=True, stop=True), waits=wb)
                wc = [(tok_b1_free if p == 0 else tok_i_free)]
                tc = E("t", lambda h, n=n, psc=psc: h.matmul(
                    psc[:, :], onehotm[:, (16 + n) * 128:(17 + n) * 128],
                    xprj[:, :], start=True, stop=True), waits=wc)
                tbr = E("a", lambda h, n=n, psb=psb: h.activation(
                    brepall[:, n * WSEG + 1:n * WSEG + 1 + T], psb[:, :],
                    AF.Copy), waits=[("t", tb)])
                tcr = E("a", lambda h, n=n, psc=psc: h.activation(
                    crepall[:, n * WSEG + 1:n * WSEG + 1 + T], psc[:, :],
                    AF.Copy), waits=[("t", tc)])
                if p == 0:
                    tok_b0_free, tok_b1_free = ("a", tbr), ("a", tcr)
                else:
                    tok_m_free, tok_i_free = ("a", tbr), ("a", tcr)
                tok_bc = tcr

            pend_idmm = None
            pend_epi = None
            pend_sq = None
            ltt1 = ltt2 = 0

            def emit_idmm(pend, vtok):
                nonlocal tok_f_free_a
                (kk, gg, sbase) = pend
                psf = ps_f0 if kk % 2 == 0 else ps_f1
                tokm = 0
                for j in range(4):
                    w = []
                    if j == 0:
                        w = [("v", vtok)]
                        if gg == 0:
                            w.append(("a", tok_f_free_a[kk % 2]))
                    tokm = E("t", lambda h, j=j, psf=psf, sbase=sbase: h.matmul(
                        psf[:, :], identb[:, :],
                        sb2[:, sbase + j * WSEG + 1:sbase + j * WSEG + 1 + T],
                        start=(gg == 0 and j == 0), stop=(gg == 3 and j == 3)),
                        waits=w)
                last_idmm_par[(kk * 4 + gg) % 4] = tokm
                if gg == 3:
                    idmm_k_tok[kk] = tokm

            def emit_epiV(kk):
                nonlocal tok_f_free_a
                psf = ps_f0 if kk % 2 == 0 else ps_f1
                ys = yac2[:, (kk % 2) * T:(kk % 2 + 1) * T]
                ev = E("a", lambda h, psf=psf, ys=ys: h.activation(
                    ys, psf[:, :], AF.Copy), waits=[("t", idmm_k_tok[kk])])
                tok_f_free_a[kk % 2] = ev
                E("v", lambda h, kk=kk, ys=ys: h.scalar_tensor_tensor(
                    ys, hb[:, kk * T:(kk + 1) * T], dvec_sb[:, kk:kk + 1],
                    ys, OP.mult, OP.add), waits=[("a", ev)])
                wga = [("v", tok_ga)] if (m == 0 and kk == 0) else []
                tga = E("v", lambda h, kk=kk, ys=ys: h.scalar_tensor_tensor(
                    xf[:, kk * T:(kk + 1) * T], ys, ga_cl[:, kk:kk + 1],
                    xf[:, kk * T:(kk + 1) * T], OP.mult, OP.add), waits=wga)
                return tga

            def emit_epiT1(kk, x2pub):
                nonlocal ltt1
                w1 = [("v", x2pub)]
                if kk == 0:
                    w1 += [("v", tok_s1_free_v), ("a", tok_s1_free_a)]
                ltt1 = E("t", lambda h, kk=kk: h.matmul(
                    ps_s1[0:1, :], ones128f[:, :], xf[:, kk * T:(kk + 1) * T],
                    start=(kk == 0), stop=(kk == KD - 1)), waits=w1)

            def emit_epiSq(kk, x2pub):
                nonlocal ltt2
                tsq = E("a", lambda h, kk=kk: h.activation(
                    hb[:, kk * T:(kk + 1) * T], xf[:, kk * T:(kk + 1) * T],
                    AF.Square), waits=[("v", x2pub)])
                w2 = [("a", tsq)]
                if kk == 0:
                    w2.append(("v", tok_s2_free_v))
                ltt2 = E("t", lambda h, kk=kk: h.matmul(
                    ps_s2[0:1, :], ones128b[:, :], hb[:, kk * T:(kk + 1) * T],
                    start=(kk == 0), stop=(kk == KD - 1)), waits=w2)

            for G in range(64):
                k, g = G // 4, G % 4
                slot = G % 2
                slot4 = G % 4
                base = slot4 * WGRP
                dbase = slot * WGRP
                n0 = 4 * g
                for j in range(4):
                    n = n0 + j
                    w = [("v", last_scan_par[slot])] if j == 0 else []
                    E("a", lambda h, k=k, n=n, dbase=dbase, j=j: h.activation(
                        da2[:, dbase + j * WSEG + 1:dbase + j * WSEG + 1 + T],
                        dtb[:, k * T:(k + 1) * T], AF.Exp,
                        scale=a_sb[:, k * NN + n:k * NN + n + 1]),
                        waits=w, defer=True)
                exp_tok = cnt["a"]
                sb2v = sb2[:, base:base + WGRP].rearrange(
                    "p (s w) -> p s w", s=4)
                w = [("t", last_idmm_par[slot4])]
                if G == 0:
                    w.append(("a", tok_bc))
                E("v", lambda h, sb2v=sb2v, n0=n0, k=k: h.tensor_copy(
                    sb2v[:, :, 0:1], carry_v[:, n0:n0 + 4, k:k + 1]), waits=w)
                for j in range(4):
                    n = n0 + j
                    E("v", lambda h, k=k, n=n, base=base, j=j: h.tensor_mul(
                        sb2[:, base + j * WSEG + 1:base + j * WSEG + 1 + T],
                        vb[:, k * T:(k + 1) * T],
                        brepall[:, n * WSEG + 1:n * WSEG + 1 + T]))
                tok_scan = E("v", lambda h, base=base, dbase=dbase:
                             h.tensor_tensor_scan(
                                 sb2[:, base:base + WGRP],
                                 da2[:, dbase:dbase + WGRP],
                                 sb2[:, base:base + WGRP], 0.0,
                                 OP.mult, OP.add),
                             waits=[("a", exp_tok)], war=True)
                last_scan_par[slot] = tok_scan
                E("v", lambda h, sb2v=sb2v, n0=n0, k=k: h.tensor_copy(
                    carry_v[:, n0:n0 + 4, k:k + 1], sb2v[:, :, T:T + 1]))
                E("v", lambda h, base=base, n0=n0: h.tensor_mul(
                    sb2[:, base:base + WGRP], sb2[:, base:base + WGRP],
                    crepall[:, n0 * WSEG:(n0 + 4) * WSEG]))
                if pend_idmm is not None:
                    emit_idmm(pend_idmm, tok_scan)
                    pend_idmm = None
                if pend_sq is not None:
                    emit_epiSq(*pend_sq)
                    pend_sq = None
                if pend_epi is not None:
                    emit_epiT1(pend_epi[0], tok_scan)
                    pend_sq = (pend_epi[0], tok_scan)
                    pend_epi = None
                pend_idmm = (k, g, base)
                if G % 4 == 2 and G >= 6:
                    kk = G // 4 - 1
                    tga = emit_epiV(kk)
                    pend_epi = (kk, tga)

            t_tail = FLUSHV()
            emit_idmm(pend_idmm, t_tail)
            pend_idmm = None
            if pend_sq is not None:
                emit_epiSq(*pend_sq)
                pend_sq = None
            if pend_epi is not None:
                emit_epiT1(pend_epi[0], t_tail)
                emit_epiSq(pend_epi[0], t_tail)
                pend_epi = None
            tga15 = emit_epiV(KD - 1)
            emit_epiT1(KD - 1, tga15)
            emit_epiSq(KD - 1, tga15)

            tok_h2 = emit_ln_finish(onepcb, 48, ltt1, ltt2)
            tok_hb_free_t = txp

            for gj in range(8):
                pg = gj % 2
                te1_last = 0
                for jj in range(2):
                    j = 2 * gj + jj
                    p = j % 2
                    p4 = j % 4
                    psf = ps_f0 if p == 0 else ps_f1
                    col0 = p4 * KD * 128
                    w = [("t", tok_wqb_rd[p4])]
                    if m == 0:
                        w += [("qw0", cnt["qw0"]), ("qw1", cnt["qw1"])]
                    E("g", lambda h, j=j, col0=col0: h.dma_start(
                        out=wqb[:, col0:col0 + KD * 128],
                        in_=bass.AP(wq_h, j * 128 * KD * 128,
                                    [[KD * 128, 128], [1, KD * 128]])),
                        waits=w, bump="wq%d" % p4)
                    tok_wq_dma = cnt["wq%d" % p4]
                    tmmf = 0
                    for k in range(KD):
                        w = []
                        if k == 0:
                            w = [("wq%d" % p4, tok_wq_dma), ("v", tok_h2),
                                 ("a", tok_f_free_a[p])]
                        tmmf = E("t", lambda h, k=k, psf=psf, col0=col0: h.matmul(
                            psf[:, :], wqb[:, col0 + k * 128:col0 + (k + 1) * 128],
                            hb[:, k * T:(k + 1) * T],
                            start=(k == 0), stop=(k == KD - 1)), waits=w)
                    tok_wqb_rd[p4] = tmmf
                    we = [("t", tmmf)]
                    if jj == 0 and add_hist[(gj - 2) % 8] and gj >= 2:
                        we.append(("v", add_hist[gj - 2]))
                    if m == 0 and j == 0:
                        we.append(("v", tok_gbb))
                    dst = e1b[:, pg * 1024 + jj * T:pg * 1024 + (jj + 1) * T]
                    te1_last = E("a", lambda h, j=j, psf=psf, dst=dst: h.activation(
                        dst, psf[:, :], AF.Identity,
                        bias=gbb[:, j:j + 1], scale=gq[:, j:j + 1]), waits=we)
                    tok_f_free_a[p] = te1_last
                tadd = E("v", lambda h, gj=gj, pg=pg: h.tensor_add(
                    outst[:, pg * 1024:(pg + 1) * 1024],
                    xf[:, gj * 1024:(gj + 1) * 1024],
                    e1b[:, pg * 1024:(pg + 1) * 1024]),
                    waits=[("a", te1_last), ("od%d" % pg, tok_out_dma[pg])])
                add_hist[gj] = tadd
                E("s", lambda h, gj=gj, pg=pg, c0=c0: h.dma_start(
                    out=yT[gj * 256:(gj + 1) * 256, c0:c0 + T].rearrange(
                        "(a p) c -> p a c", p=128),
                    in_=outst[:, pg * 1024:(pg + 1) * 1024].rearrange(
                        "p (a c) -> p a c", a=2)),
                    waits=[("v", tadd)], bump="od%d" % pg)
                tok_out_dma[pg] = cnt["od%d" % pg]
                if m < NM_RUN - 1:
                    c0n = (m + 1) * T
                    E("s", lambda h, gj=gj, c0n=c0n: h.dma_start(
                        out=xf[:, gj * 1024:(gj + 1) * 1024].rearrange(
                            "p (a c) -> p a c", a=2),
                        in_=xT[gj * 256:(gj + 1) * 256, c0n:c0n + T].rearrange(
                            "(a p) c -> p a c", p=128)),
                        waits=[("v", tadd)], bump="xd")
            tok_hb_free_t = max(tok_wqb_rd)

        E("s", lambda h: None,
          waits=[("od0", cnt["od0"]), ("od1", cnt["od1"])], bump=None)

        def runner(tasks, eng_name):
            compute = eng_name in ("vector", "scalar", "tensor")

            def f(h):
                pending = None
                for (waits, fn, bump, war, defer) in tasks:
                    if pending is not None and waits:
                        h.drain().then_inc(*pending)
                        pending = None
                    for (sk, val) in waits:
                        h.wait_ge(sems[SEM_OF_CNT[sk]], val)
                    if pending is not None and bump is not None and not defer:
                        h.drain().then_inc(*pending)
                        pending = None
                    instr = fn(h)
                    if instr is None:
                        continue
                    if pending is not None:
                        instr.then_inc(*pending)
                        pending = None
                    if bump is None:
                        continue
                    if defer:
                        pending = (sems[bump], STEP[bump])
                        continue
                    needs_drain = (compute and not war and
                                   cnt_tok[id(fn)] in waited[bump])
                    if needs_drain:
                        h.drain().then_inc(sems[bump], STEP[bump])
                    else:
                        instr.then_inc(sems[bump], STEP[bump])
                if pending is not None:
                    h.drain().then_inc(*pending)
            return f

        with nc.Block() as block:
            block.sync(runner(script["sync"], "sync"))
            block.gpsimd(runner(script["gpsimd"], "gpsimd"))
            block.tensor(runner(script["tensor"], "tensor"))
            block.vector(runner(script["vector"], "vector"))
            block.scalar(runner(script["scalar"], "scalar"))

    return nc


def kernel(L0_attn_x, L0_attn_W_x, L0_attn_W_dt, L0_attn_b_dt, L0_attn_A_log,
           L0_attn_D, L0_ffn_weight, L0_ffn_bias, L0_c, L0_wmod):
    nc = build_nc()

    wxT = np.ascontiguousarray(L0_attn_W_x.T)
    wdtT = np.ascontiguousarray(L0_attn_W_dt.T)
    bdt_a = np.ascontiguousarray(L0_attn_b_dt.reshape(KD, 128).T)
    alog_a = np.ascontiguousarray(
        L0_attn_A_log.reshape(KD, 128, NN).transpose(1, 0, 2).reshape(128, KD * NN))
    dvec_a = np.ascontiguousarray(L0_attn_D.reshape(KD, 128).T)
    fbias_a = np.ascontiguousarray(L0_ffn_bias.reshape(KD, 128).T)
    wT_a = np.ascontiguousarray(L0_ffn_weight.T)
    wmod_a = np.ascontiguousarray(
        L0_wmod.reshape(KD, 128, NFC, 128).transpose(2, 1, 0, 3)
        .reshape(NFC * 128, KD * 128))

    import ml_dtypes
    onehot = np.zeros((96, 32 * 128), dtype=ml_dtypes.bfloat16)
    ident = np.eye(128, dtype=ml_dtypes.bfloat16)
    for j in range(32):
        onehot[64 + j, j * 128:(j + 1) * 128] = 1.0

    in_maps = []
    for b in range(NB):
        in_maps.append({
            "xT": np.ascontiguousarray(L0_attn_x[b].T),
            "wxT": wxT, "wdtT": wdtT, "bdt": bdt_a, "alog": alog_a,
            "dvec": dvec_a, "fbias": fbias_a, "wT": wT_a,
            "cb_in": np.ascontiguousarray(L0_c[b].reshape(KD, 128).T),
            "wmod_r": wmod_a, "onehot_bc": onehot, "ident_bc": ident,
        })
    res = run_bass_kernel_spmd(nc, in_maps, core_ids=list(range(NB)))
    out = np.stack([np.ascontiguousarray(r["yT"].T) for r in res.results], axis=0)
    return out.astype(np.float32)


if __name__ == "__main__":
    build_nc()
    print("build ok")
